# version 03
# baseline (speedup 1.0000x reference)
import numpy as np
import ml_dtypes
import concourse.bass as bass
import concourse.mybir as mybir
from concourse.bass_utils import run_bass_kernel_spmd

F32 = mybir.dt.float32
BF16 = mybir.dt.bfloat16
I32 = mybir.dt.int32
U8 = mybir.dt.uint8
AF = mybir.ActivationFunctionType
ALU = mybir.AluOpType
AXX = mybir.AxisListType.X
bf = ml_dtypes.bfloat16

B_CORE, L, D, H, DH, SEG, R, NCH, FF = 2, 2048, 1024, 16, 64, 64, 32, 16, 3072
EPS_LN = 1e-5
EPS_W = 1e-5
INV127 = 1.0 / 127.0
SCALE_S = 0.125

DEBUG_TAPS = ()
PHASE_LIMIT = 10 ** 9
SUB_LIMIT = 10 ** 9
NPCT = 16


class _RecOp:

    __slots__ = ("name", "args", "kw", "incs")

    def __init__(self, name, args, kw):
        self.name = name
        self.args = args
        self.kw = kw
        self.incs = []


_NO_SELF_SYNC = {"wait_ge", "dma_start", "dma_start_transpose"}


class _Recorder:
    def __init__(self, eng):
        self._eng = eng

    def __getattr__(self, name):
        eng = self._eng

        def call(*a, **kw):
            if eng.self_sync and name not in _NO_SELF_SYNC:
                if eng.n > 0 and eng.waited.get(eng.sem.num, -1) < eng.n:
                    eng.q.append(_RecOp("wait_ge", (eng.sem, eng.n), {}))
                    eng.waited[eng.sem.num] = eng.n
                op = _RecOp(name, a, kw)
                eng.q.append(op)
                op.incs.append((eng.sem, eng.step))
                eng.n += eng.step
            else:
                op = _RecOp(name, a, kw)
                eng.q.append(op)
            return op

        return call


class Eng:

    def __init__(self, nc, name, dma_step=1, self_sync=False):
        self.sem = nc.alloc_semaphore("sem_" + name)
        self.n = 0
        self.step = dma_step
        self.waited = {}
        self.q = []
        self.self_sync = self_sync
        self.raw = _Recorder(self)

    def wait(self, other, val=None):
        if other is self:
            return
        val = other.n if val is None else val
        if val <= 0 or self.waited.get(other.sem.num, -1) >= val:
            return
        self.raw.wait_ge(other.sem, val)
        self.waited[other.sem.num] = val

    def inc(self, op):
        if (self.sem, self.step) in op.incs:
            return self.n
        op.incs.append((self.sem, self.step))
        self.n += self.step
        return self.n

    def replay(self, raw):
        for op in self.q:
            inst = getattr(raw, op.name)(*op.args, **op.kw)
            for sem, v in op.incs:
                inst.then_inc(sem, v)


class K:
    def __init__(self, taps=()):
        self.taps = set(taps)
        self.nc = bass.Bass("TRN2", target_bir_lowering=False)
        self.tap_tensors = {}
        self._declare_io()
        self._alloc_sbuf()
        self._alloc_psum()

    def _declare_io(self):
        nc = self.nc
        t = lambda n, s, d, k: nc.dram_tensor(n, s, d, kind=k).ap()
        self.x = t("x", [B_CORE, L, D], F32, "ExternalInput")
        self.Wq = t("Wq", [D, D], F32, "ExternalInput")
        self.Wk = t("Wk", [D, D], F32, "ExternalInput")
        self.Wv = t("Wv", [D, D], F32, "ExternalInput")
        self.Wc = t("Wc", [D, D], F32, "ExternalInput")
        self.wg = t("wg", [FF, D], I32, "ExternalInput")
        self.wu = t("wu", [FF, D], I32, "ExternalInput")
        self.wd = t("wd", [D, FF], I32, "ExternalInput")
        self.tm = t("tm", [SEG], F32, "ExternalInput")
        self.idec = t("idec", [H, SEG], F32, "ExternalInput")
        self.mdec = t("mdec", [H], F32, "ExternalInput")
        self.c_id = t("c_id", [128, 128], BF16, "ExternalInput")
        self.c_idf = t("c_idf", [16, 16], F32, "ExternalInput")
        self.c_tri = t("c_tri", [128, 128], BF16, "ExternalInput")
        self.c_terr = t("c_terr", [128, SEG], BF16, "ExternalInput")
        self.c_terr0 = t("c_terr0", [128, SEG], BF16, "ExternalInput")
        self.c_ones = t("c_ones", [1, 128], BF16, "ExternalInput")
        self.c_half = t("c_half", [128, 1], F32, "ExternalInput")
        self.out = t("out", [B_CORE, L, D], F32, "ExternalOutput")
        self.scr_wq = t("scr_wq", [D, D], BF16, "Internal")
        self.scr_wk = t("scr_wk", [D, D], BF16, "Internal")
        self.scr_wv = t("scr_wv", [D, D], BF16, "Internal")
        self.scr_wc = t("scr_wc", [D, D], BF16, "Internal")
        self.scr_wg = t("scr_wg", [FF, D], BF16, "Internal")
        self.scr_wu = t("scr_wu", [FF, D], BF16, "Internal")
        self.scr_wd = t("scr_wd", [D, FF], BF16, "Internal")
        self.scr_row = t("scr_row", [2, 512], F32, "Internal")

    def _alloc_sbuf(self):
        nc = self.nc
        A = lambda n, s, d: nc.alloc_sbuf_tensor(n, s, d).ap()
        self.id_sb = A("id_sb", [128, 128], BF16)
        self.tri_sb = A("tri_sb", [128, 128], BF16)
        self.terr_sb = A("terr_sb", [128, SEG], BF16)
        self.terr0_sb = A("terr0_sb", [128, SEG], BF16)
        self.ones_sb = A("ones_sb", [1, 128], BF16)
        self.half_sb = A("half_sb", [128, 1], F32)
        self.idf_sb = A("idf_sb", [16, 16], F32)
        self.tw_col = A("tw_col", [128, 1], F32)
        self.f2_col = A("f2_col", [128, 1], F32)
        self.eide = A("eide", [128, H], F32)
        self.eido = A("eido", [128, H], F32)
        self.d0_bc = A("d0_bc", [128, 512], BF16)
        self.inv_bc = A("inv_bc", [128, 512], BF16)
        self.idec_sb = A("idec_sb", [H, SEG], F32)
        self.eid_hs = A("eid_hs", [H, SEG], F32)
        self.sh_col = A("sh_col", [H, 1], F32)
        self.mega_col = A("mega_col", [H, 1], F32)
        self.norm_hs = A("norm_hs", [H, 33], F32)
        self.d0_hs = A("d0_hs", [H, 32], F32)
        self.inv_hs = A("inv_hs", [H, 32], F32)
        self.tmp_hs = A("tmp_hs", [H, 33], F32)
        self.rowd0f = A("rowd0f", [1, 512], F32)
        self.rowinvf = A("rowinvf", [1, 512], F32)
        self.rowd0 = A("rowd0", [1, 512], BF16)
        self.rowinv = A("rowinv", [1, 512], BF16)
        self.C_sb = A("C_sb", [128, 32], BF16)
        self.acc1 = A("acc1", [128, NCH], F32)
        self.accs = A("accs", [128, NCH], F32)
        self.rs4 = A("rs4", [128, NCH], F32)
        self.rcp = A("rcp", [128, NCH], F32)
        self.rst = A("rst", [128, NCH], F32)
        self.Ffac = A("Ffac", [128, NCH], F32)
        self.ln_cols = A("ln_cols", [128, 8], F32)
        self.t1_sb = A("t1_sb", [128, 512], F32)
        self.xn = A("xn", [128, 8 * L], BF16)
        self.y_fm = A("y_fm", [128, 8 * L], BF16)
        self.g_tok = A("g_tok", [128, NCH * D], BF16)
        self.pool_q = A("pool_q", [128, 66048], U8)
        self.pool_a = A("pool_a", [128, 33856], U8)

        qb = self.pool_q.bitcast(BF16)
        qf = self.pool_q.bitcast(F32)
        ab = self.pool_a.bitcast(BF16)
        af = self.pool_a.bitcast(F32)
        self.stage = qb[:, 0:24576]
        self.q_fm = qb[:, 0:8192]
        self.k_fm = qb[:, 8192:16384]
        self.v_aug = qb[:, 16384:24704]
        self.v_aug2 = qb[:, 24704:33024]
        self.h_sb = qb[:, 0:24576]
        self.sgu = qb[:, 24576:26624]
        self.wdT_buf = qb[:, 26624:28672]
        self.wcT_buf = qb[:, 0:8192]
        self.x_out = qf[:, 4096:6144]
        self.x_in2 = qf[:, 6144:8192]
        self.x_in = af[:, 0:2048]
        self.ln_sq = af[:, 2048:3072]
        self.xt_ln = ab[:, 4096:5120]
        self.wT = [ab[:, 0:4096], ab[:, 4096:8192], ab[:, 8192:12288]]
        self.E4 = ab[:, 0:8192]
        self.Et = ab[:, 8192:10240]
        self.ktok = ab[:, 10240:11264]
        self.w_sb = ab[:, 11264:11792]
        self.vw = ab[:, 11792:12304]
        self.vw2 = ab[:, 12304:12816]
        self.pcs_sb = ab[:, 12816:13840]
        self.kv_sb = ab[:, 13840:15888]
        self.ot_sb = ab[:, 15888:16928]
        self.y_tok = ab[:, 0:1024]
        self.wgT_buf = ab[:, 0:6144]
        self.wuT_buf = ab[:, 6144:12288]

    def _alloc_psum(self):
        nc = self.nc
        P = lambda n, s, d: nc.alloc_psum_tensor(n, s, d).ap()
        self.ps_strip = P("ps_strip", [128, 2048], F32)
        self.ps_a = P("ps_a", [128, 512], F32)
        self.ps_b = P("ps_b", [128, 512], F32)
        self.ps_c = P("ps_c", [128, 512], F32)
        self.ps_t = P("ps_t", [128, 1024], BF16)

    def build(self):
        nc = self.nc
        self._kbase = -1
        self._last_mm = None
        self.pe = Eng(nc, "pe")
        self.dv = Eng(nc, "dv", self_sync=True)
        self.ac = Eng(nc, "ac", self_sync=True)
        self.gp = Eng(nc, "gp", self_sync=True)
        self.sy = Eng(nc, "sy")
        self.dq = Eng(nc, "dq", dma_step=16)
        self.dqs = Eng(nc, "dqs", dma_step=16)
        self.dqg = Eng(nc, "dqg", dma_step=16)
        self.emit()
        with nc.Block() as blk:
            @blk.tensor
            def _(t):
                self.pe.replay(t)

            @blk.vector
            def _(v):
                self.dv.replay(v)

            @blk.scalar
            def _(s):
                self.ac.replay(s)

            @blk.gpsimd
            def _(g):
                self.gp.replay(g)

            @blk.sync
            def _(sy):
                self.sy.replay(sy)
        return nc

    def pe_mm(self, kbase, out, lhsT, rhs, **kw):
        if kbase != self._kbase:
            if self._last_mm is not None:
                self.pe.inc(self._last_mm)
            if self.pe.n > 0:
                self.pe.raw.wait_ge(self.pe.sem, self.pe.n)
            self._kbase = kbase
        op = self.pe.raw.matmul(out, lhsT, rhs, **kw)
        self._last_mm = op
        return op

    def tr(self, out, in_, ident, first, last, tp=None, kbase=0):
        kw = dict(start=first, stop=last, is_transpose=True)
        if tp is not None:
            kw["tile_position"] = tp
        return self.pe_mm(kbase, out, in_, ident, **kw)

    def dma(self, q, out, in_, fence=False, **kw):
        eng, ctr = {"sy": (self.sy, self.dq), "ac": (self.ac, self.dqs),
                    "gp": (self.gp, self.dqg)}[q]
        if fence:
            eng.wait(ctr, ctr.n)
        ctr.inc(eng.raw.dma_start(out=out, in_=in_, **kw))
        return ctr.n

    def dmat(self, q, out, in_):
        eng, ctr = {"sy": (self.sy, self.dq), "ac": (self.ac, self.dqs)}[q]
        ctr.inc(eng.raw.dma_start_transpose(out, in_))
        return ctr.n

    def barrier(self):
        engs = (self.pe, self.dv, self.ac, self.gp)
        for e in engs:
            for o in engs:
                if o is not e:
                    e.wait(o)
            e.wait(self.dq)
            e.wait(self.dqs)
            e.wait(self.dqg)
        for e in engs:
            self.sy.wait(e)

    def tap(self, name, src_ap, dtype=None):
        if name not in self.taps:
            return
        dtype = dtype or src_ap.dtype
        dst = self.nc.dram_tensor("tap_" + name, list(src_ap.shape), dtype,
                                  kind="ExternalOutput").ap()
        self.tap_tensors[name] = dst
        for e in (self.pe, self.dv, self.ac, self.gp):
            self.sy.wait(e)
        n = self.dma("sy", dst, src_ap, fence=True)
        for e in (self.pe, self.dv, self.ac, self.gp):
            e.wait(self.dq, n)

    def emit(self):
        ph = [0]

        def go():
            ph[0] += 1
            return ph[0] <= PHASE_LIMIT

        self.setup()
        for b in range(B_CORE):
            if go():
                self.ln_batch(b)
                self.barrier()
            for half in range(2):
                if go():
                    self.proj_half(b, half)
                    self.barrier()
                for i in range(8):
                    if go():
                        self.attn_head(b, half, i)
                self.barrier()
            if b == 0:
                self.tap(f"yfm_{b}", self.y_fm[:])
            if go():
                self.ffn_batch(b)
                self.barrier()
            if go():
                self.wc_combine(b)
                self.barrier()
        self.sy.wait(self.dq)
        self.ac.wait(self.dqs)
        self.gp.wait(self.dqg)

    def setup(self):
        sy, dv, ac, pe, gp = self.sy, self.dv, self.ac, self.pe, self.gp
        self.dma("sy", self.id_sb[:], self.c_id)
        self.dma("sy", self.tri_sb[:], self.c_tri)
        self.dma("sy", self.terr_sb[:], self.c_terr)
        self.dma("sy", self.terr0_sb[:], self.c_terr0)
        self.dma("sy", self.ones_sb[:], self.c_ones)
        self.dma("sy", self.half_sb[:], self.c_half)
        self.dma("sy", self.idf_sb[:], self.c_idf)
        self.dma("sy", self.idec_sb[:], self.idec)
        self.dma("sy", self.mega_col[:], self.mdec.unsqueeze(1))
        n_consts = self.dma("sy", self.tw_col[0:SEG, :], self.tm.unsqueeze(1))

        for src, scr, rows in ((self.Wq, self.scr_wq, D), (self.Wk, self.scr_wk, D),
                               (self.Wv, self.scr_wv, D), (self.Wc, self.scr_wc, D),
                               (self.wg, self.scr_wg, FF), (self.wu, self.scr_wu, FF),
                               (self.wd, self.scr_wd, D)):
            cols = src.shape[1]
            nch = rows // 128
            stg = self.stage[:, 0 : nch * cols]
            gp.wait(self.dq)
            n = self.dma("gp", stg.rearrange("p (c m) -> p c m", c=nch),
                         src.rearrange("(c p) m -> p c m", p=128))
            sy.wait(self.dqg, n)
            self.dma("sy", scr.rearrange("(c p) m -> p c m", p=128),
                     stg.rearrange("p (c m) -> p c m", c=nch))
        self.wb_n = self.dq.n

        dv.wait(self.dq, n_consts)
        dv.inc(dv.raw.tensor_copy(out=self.tw_col[64:128, :], in_=self.tw_col[0:SEG, :]))
        ac.wait(dv)
        ac.inc(ac.raw.activation(out=self.tw_col[:], in_=self.tw_col[:], func=AF.Sigmoid))
        dv.wait(ac)
        dv.inc(dv.raw.tensor_scalar(out=self.f2_col[:], in0=self.tw_col[:],
                                    scalar1=-1.0, scalar2=-1.0,
                                    op0=ALU.mult, op1=ALU.subtract))
        ac.inc(ac.raw.activation(out=self.eid_hs[:], in_=self.idec_sb[:], func=AF.Exp,
                                 accum_out=self.sh_col[:]))
        ac.inc(ac.raw.activation(out=self.mega_col[:], in_=self.mega_col[:],
                                 func=AF.Exp, scale=float(SEG)))
        dv.wait(ac)
        dv.inc(dv.raw.tensor_scalar_min(out=self.mega_col[:], in0=self.mega_col[:],
                                        scalar1=2.0))
        dv.inc(dv.raw.tensor_tensor_scan(
            out=self.tmp_hs[:, 0:32],
            data0=self.mega_col[:].broadcast_to([H, 32]),
            data1=self.sh_col[:].broadcast_to([H, 32]),
            initial=0.0, op0=ALU.mult, op1=ALU.add))
        dv.inc(dv.raw.memset(self.norm_hs[:, 0:1], EPS_W))
        dv.inc(dv.raw.tensor_scalar_add(out=self.norm_hs[:, 1:33],
                                        in0=self.tmp_hs[:, 0:32], scalar1=EPS_W))
        dv.inc(dv.raw.reciprocal(out=self.inv_hs[:], in_=self.norm_hs[:, 1:33]))
        dv.inc(dv.raw.tensor_tensor(out=self.d0_hs[:], in0=self.norm_hs[:, 0:32],
                                    in1=self.inv_hs[:], op=ALU.mult))
        dv.inc(dv.raw.tensor_scalar_mul(out=self.d0_hs[:], in0=self.d0_hs[:],
                                        scalar1=self.mega_col[:]))
        sy.wait(dv)
        self.dma("sy", self.scr_row[0:1, :].rearrange("o (h r) -> (o h) r", h=H),
                 self.d0_hs[:])
        self.dma("sy", self.scr_row[1:2, :].rearrange("o (h r) -> (o h) r", h=H),
                 self.inv_hs[:])
        sy.wait(self.dq)
        self.dma("sy", self.rowd0f[:], self.scr_row[0:1, :])
        n = self.dma("sy", self.rowinvf[:], self.scr_row[1:2, :])
        dv.wait(self.dq, n)
        dv.inc(dv.raw.tensor_copy(out=self.rowd0[:], in_=self.rowd0f[:]))
        dv.inc(dv.raw.tensor_copy(out=self.rowinv[:], in_=self.rowinvf[:]))
        pe.wait(dv)
        pe.inc(self.pe_mm(0, self.ps_a[:], self.ones_sb[:], self.rowd0[:],
                          start=True, stop=True))
        pe.inc(self.pe_mm(0, self.ps_b[:], self.ones_sb[:], self.rowinv[:],
                          start=True, stop=True))
        dv.wait(pe)
        dv.inc(dv.raw.tensor_copy(out=self.d0_bc[:], in_=self.ps_a[:]))
        dv.inc(dv.raw.tensor_copy(out=self.inv_bc[:], in_=self.ps_b[:]))
        dv.inc(dv.raw.memset(self.eide[:], 0.0))
        dv.inc(dv.raw.memset(self.eido[:], 0.0))
        pe.wait(dv)
        psf = self.ps_c
        pe.inc(self.tr(psf[0:64, 0:16], self.eid_hs[:], self.idf_sb[:], True, True))
        dv.wait(pe)
        dv.inc(dv.raw.tensor_copy(out=self.eide[0:64, :], in_=psf[0:64, 0:16]))
        scr16 = self.scr_row.rearrange("a (c d) -> (a c) d", c=32, d=16)[0:64, :]
        sy.wait(dv)
        n = self.dma("sy", scr16, self.eide[0:64, :], fence=True)
        sy.wait(self.dq, n)
        n = self.dma("sy", self.eido[64:128, :], scr16, fence=True)
        dv.wait(self.dq, n)
        dv.inc(dv.raw.memset(self.w_sb[:], 0.0))
        va = self.v_aug.rearrange("p (c g e) -> p c g e", c=NCH, g=8)
        va2 = self.v_aug2.rearrange("p (c g e) -> p c g e", c=NCH, g=8)
        dv.inc(dv.raw.memset(va[:, :, :, 64:65], 1.0))
        dv.inc(dv.raw.memset(va2[:, :, :, 64:65], 1.0))
        self.barrier()
        self.tap("d0_bc", self.d0_bc[:])
        self.tap("inv_bc", self.inv_bc[:])
        self.tap("eide", self.eide[:])
        self.tap("norms", self.norm_hs[:])

    def ln_batch(self, b):
        sy, dv, ac, pe = self.sy, self.dv, self.ac, self.pe
        xin = self.x_in.rearrange("p (i m) -> p i m", i=2)
        lnc = self.ln_cols
        gp = self.gp
        for lc in range(NCH):
            buf = lc % 2
            q = "sy" if buf == 0 else "gp"
            qeng = sy if buf == 0 else gp
            ctr = self.dq if buf == 0 else self.dqg
            if lc >= 2:
                qeng.wait(dv)
                qeng.wait(ac)
            n = self.dma(q, xin[:, buf, :], self.x[b, lc * 128 : lc * 128 + 128, :],
                         fence=True)
            dv.wait(ctr, n)
            ac.wait(ctr, n)
            ac.wait(pe)
            ac.inc(ac.raw.activation(out=self.ln_sq[:], in_=xin[:, buf, :],
                                     func=AF.Square, accum_out=lnc[:, 1:2]))
            dv.inc(dv.raw.reduce_sum(out=lnc[:, 0:1], in_=xin[:, buf, :], axis=AXX))
            dv.wait(ac)
            dv.inc(dv.raw.tensor_scalar_mul(out=lnc[:, 2:3], in0=lnc[:, 0:1],
                                            scalar1=1.0 / D))
            dv.inc(dv.raw.tensor_scalar_mul(out=lnc[:, 3:4], in0=lnc[:, 1:2],
                                            scalar1=1.0 / D))
            dv.inc(dv.raw.tensor_tensor(out=lnc[:, 4:5], in0=lnc[:, 2:3],
                                        in1=lnc[:, 2:3], op=ALU.mult))
            dv.inc(dv.raw.tensor_tensor(out=lnc[:, 5:6], in0=lnc[:, 3:4],
                                        in1=lnc[:, 4:5], op=ALU.subtract))
            dv.inc(dv.raw.tensor_scalar_add(out=lnc[:, 5:6], in0=lnc[:, 5:6],
                                            scalar1=EPS_LN))
            ac.wait(dv)
            ac.inc(ac.raw.activation(out=lnc[:, 6:7], in_=lnc[:, 5:6],
                                     func=AF.Sqrt))
            dv.wait(ac)
            dv.inc(dv.raw.reciprocal(out=lnc[:, 7:8], in_=lnc[:, 6:7]))
            xt = self.xt_ln
            dv.inc(dv.raw.tensor_scalar(out=xt, in0=xin[:, buf, :],
                                        scalar1=lnc[:, 2:3], scalar2=lnc[:, 7:8],
                                        op0=ALU.subtract, op1=ALU.mult))
            pe.wait(dv)
            for g2 in range(2):
                if lc >= 1 or g2 >= 1:
                    pe.wait(dv)
                last = None
                for dc4 in range(4):
                    dc = g2 * 4 + dc4
                    last = self.tr(
                        self.ps_t[:, g2 * 512 + dc4 * 128 : g2 * 512 + dc4 * 128 + 128],
                        xt[:, dc * 128 : dc * 128 + 128], self.id_sb[:],
                        dc4 == 0, dc4 == 3)
                pe.inc(last)
                dv.wait(pe)
                dst = self.xn.rearrange("p (c m) -> p c m", c=8)[
                    :, g2 * 4 : g2 * 4 + 4, lc * 128 : lc * 128 + 128]
                dv.inc(dv.raw.tensor_copy(
                    out=dst,
                    in_=self.ps_t[:, g2 * 512 : g2 * 512 + 512]
                    .rearrange("p (c m) -> p c m", c=4)))
        self.tap(f"xn_{b}", self.xn[:])

    def proj_half(self, b, half):
        sy, dv, ac, pe = self.sy, self.dv, self.ac, self.pe
        self.ac.wait(self.dq, self.wb_n)
        for j, scr in enumerate((self.scr_wq, self.scr_wk, self.scr_wv)):
            for ds in range(8):
                self.dmat("ac", self.wT[j][:, ds * 512 : ds * 512 + 512],
                          scr[half * 512 : half * 512 + 512, ds * 128 : ds * 128 + 128])
        pe.wait(self.dqs)

        banks = [self.ps_strip[:, i * 512 : i * 512 + 512] for i in range(4)]
        for j, dst in ((0, self.q_fm), (1, self.k_fm)):
            for oc in range(4):
                pe.wait(dv)
                for dc in range(8):
                    lastmm = None
                    for lt in range(4):
                        lastmm = self.pe_mm(
                            0, banks[lt],
                            self.wT[j][:, dc * 512 + oc * 128 : dc * 512 + oc * 128 + 128],
                            self.xn[:, dc * 2048 + lt * 512 : dc * 2048 + lt * 512 + 512],
                            start=(dc == 0), stop=(dc == 7))
                    if dc == 7:
                        pe.inc(lastmm)
                dv.wait(pe)
                for lt in range(4):
                    dv.inc(dv.raw.tensor_copy(
                        out=dst[:, oc * 2048 + lt * 512 : oc * 2048 + lt * 512 + 512],
                        in_=banks[lt]))
        va = self.v_aug.rearrange("p (c g e) -> p c g e", c=NCH, g=8)
        va2 = self.v_aug2.rearrange("p (c g e) -> p c g e", c=NCH, g=8)
        dv.inc(dv.raw.memset(va[:, :, :, 64:65], 1.0))
        dv.inc(dv.raw.memset(va2[:, :, :, 64:65], 1.0))
        for lc in range(NCH):
            bank = self.ps_a if lc % 2 == 0 else self.ps_b
            pe.wait(dv)
            lastmm = None
            for dc in range(8):
                lastmm = self.pe_mm(
                    0, bank[:],
                    self.xn[:, dc * 2048 + lc * 128 : dc * 2048 + lc * 128 + 128],
                    self.wT[2][:, dc * 512 : dc * 512 + 512],
                    start=(dc == 0), stop=(dc == 7))
            pe.inc(lastmm)
            dv.wait(pe)
            src = bank.rearrange("p (g e) -> p g e", g=8)
            dv.inc(dv.raw.tensor_copy(out=va[:, lc, :, 0:64], in_=src))
            if lc >= 1:
                dv.inc(dv.raw.tensor_copy(out=va2[64:128, lc - 1, :, 0:64],
                                          in_=src[0:64, :, :]))
            if lc <= NCH - 2:
                dv.inc(dv.raw.tensor_copy(out=va2[0:64, lc, :, 0:64],
                                          in_=src[64:128, :, :]))
        self.tap(f"q_fm_{b}_{half}", self.q_fm[:])
        self.tap(f"k_fm_{b}_{half}", self.k_fm[:])
        self.tap(f"v_aug_{b}_{half}", self.v_aug[:])

    def attn_head(self, b, half, i):
        sy, dv, ac, pe, gp = self.sy, self.dv, self.ac, self.pe, self.gp
        h = half * 8 + i
        hp = i // 2
        rb = 64 * (i % 2)
        qh = lambda c0, n: self.q_fm[rb : rb + 64, hp * 2048 + c0 : hp * 2048 + c0 + n]
        kh = lambda c0, n: self.k_fm[rb : rb + 64, hp * 2048 + c0 : hp * 2048 + c0 + n]
        va = self.v_aug.rearrange("p (c g e) -> p c g e", c=NCH, g=8)
        va2 = self.v_aug2.rearrange("p (c g e) -> p c g e", c=NCH, g=8)
        idsl = self.id_sb[rb : rb + 64, rb : rb + 64]

        dv.inc(dv.raw.memset(self.acc1[:], 0.0))
        dv.inc(dv.raw.memset(self.accs[:], 0.0))
        dv.inc(dv.raw.memset(
            self.w_sb.rearrange("p (c r) -> p c r", r=33)[:, :, 0:1], 0.0))
        for g2 in range(2):
            pe.wait(dv)
            last = None
            for m4 in range(8):
                mc = g2 * 8 + m4
                last = self.tr(
                    self.ps_t[:, g2 * 512 + m4 * 64 : g2 * 512 + m4 * 64 + 64],
                    kh(mc * 128, 128), idsl, m4 == 0, m4 == 7, kbase=rb)
            pe.inc(last)
            dv.wait(pe)
            dv.inc(dv.raw.tensor_copy(out=self.ktok[:, g2 * 512 : g2 * 512 + 512],
                                      in_=self.ps_t[:, g2 * 512 : g2 * 512 + 512]))

        if SUB_LIMIT < 2:
            return
        pe.inc(self.pe_mm(rb, self.ps_c[0:64, 448:512], kh(0, 64), qh(0, 64),
                          start=True, stop=True))
        ac.wait(pe)
        ac.inc(ac.raw.activation(out=self.Et[0:64, 0:64],
                                 in_=self.ps_c[0:64, 448:512],
                                 func=AF.Exp, scale=SCALE_S))
        dv.inc(dv.raw.memset(self.Et[64:128, 0:64], 0.0))
        for rnd in range(4):
            bank = self.ps_strip[:, (rnd % 2) * 512 : (rnd % 2) * 512 + 512]
            pe.wait(dv)
            if rnd >= 2:
                pe.wait(ac)
            last = None
            first_s8 = 1 if rnd == 0 else 0
            for s8 in range(first_s8, 8):
                r = rnd * 8 + s8
                last = self.pe_mm(rb, bank[:, s8 * 64 : s8 * 64 + 64],
                                  kh(64 * (r - 1), 128), qh(64 * r, 64),
                                  start=(s8 == first_s8), stop=(s8 == 7))
            pe.inc(last)
            ac.wait(pe)
            ac.inc(ac.raw.activation(
                out=self.Et[:, rnd * 512 + first_s8 * 64 : rnd * 512 + 512],
                in_=bank[:, first_s8 * 64 : 512], func=AF.Exp, scale=SCALE_S))
        dv.wait(ac)
        dv.inc(dv.raw.tensor_tensor(out=self.Et[0:64, 0:64], in0=self.Et[0:64, 0:64],
                                    in1=self.terr0_sb[0:64, :], op=ALU.mult))
        dv.inc(dv.raw.tensor_tensor(
            out=self.Et[:, 64:2048].rearrange("p (s e) -> p s e", e=SEG),
            in0=self.Et[:, 64:2048].rearrange("p (s e) -> p s e", e=SEG),
            in1=self.terr_sb[:].unsqueeze(1).broadcast_to([128, 31, SEG]),
            op=ALU.mult))
        pe.wait(dv)

        if SUB_LIMIT < 3:
            return
        pcs_started = [False, False, False, False]
        for c in range(NCH):
            s4 = c % 4
            w = 128 * c + 128
            pe.wait(ac)
            pe.wait(dv)
            last = None
            for t in range((w + 511) // 512):
                n = min(512, w - 512 * t)
                last = self.pe_mm(rb, self.ps_strip[:, 512 * t : 512 * t + n],
                                  qh(c * 128, 128), kh(512 * t, n),
                                  start=True, stop=True)
            pe.inc(last)
            if c % 4 == 0 and c >= 4:
                pe.wait(dv)
            slot = self.ps_c[:, s4 * 65 : s4 * 65 + 65]
            r0, r1 = 2 * c, 2 * c + 1
            if r0 == 0:
                self.pe_mm(0, slot[0:64, :], self.Et[0:64, 0:64],
                           va[0:64, 0, i, :], start=True, stop=True)
            else:
                self.pe_mm(0, slot[0:64, :], self.Et[:, r0 * 64 : r0 * 64 + 64],
                           va2[:, c - 1, i, :], start=True, stop=True)
            mm1 = self.pe_mm(0, slot[64:128, :], self.Et[:, r1 * 64 : r1 * 64 + 64],
                             va[:, c, i, :], start=True, stop=True,
                             tile_position=(0, 64))
            pe.inc(mm1)

            ac.wait(pe)
            e0 = s4 * 2048
            if c >= 1:
                ac.inc(ac.raw.activation(
                    out=self.E4[:, e0 : e0 + w - 192],
                    in_=self.ps_strip[:, 0 : w - 192],
                    func=AF.Exp, scale=SCALE_S, accum_out=self.acc1[:, c : c + 1]))
                ac.inc(ac.raw.activation(
                    out=self.E4[:, e0 + w - 192 : e0 + w - 128],
                    in_=self.ps_strip[:, w - 192 : w - 128],
                    func=AF.Exp, scale=SCALE_S, accum_out=self.accs[:, c : c + 1]))
            ac.inc(ac.raw.activation(
                out=self.E4[:, e0 + w - 128 : e0 + w],
                in_=self.ps_strip[:, w - 128 : w], func=AF.Exp, scale=SCALE_S))
            dv.wait(ac)
            dv.inc(dv.raw.tensor_tensor(
                out=self.E4[:, e0 + w - 128 : e0 + w],
                in0=self.E4[:, e0 + w - 128 : e0 + w],
                in1=self.tri_sb[:], op=ALU.mult))

            if c % 4 == 3:
                g = c // 4
                cs = slice(4 * g, 4 * g + 4)
                dv.wait(pe)
                dv.inc(dv.raw.scalar_tensor_tensor(
                    out=self.rs4[:, cs], in0=self.accs[:, cs],
                    scalar=self.half_sb[:], in1=self.acc1[:, cs],
                    op0=ALU.mult, op1=ALU.add))
                otr = self.ps_c[:, 0:260].rearrange("p (c e) -> p c e", e=65)[:, :, 64:65]
                dv.inc(dv.raw.tensor_tensor(
                    out=self.rs4[:, cs].unsqueeze(2),
                    in0=self.rs4[:, cs].unsqueeze(2), in1=otr, op=ALU.add))
                dv.inc(dv.raw.reciprocal(out=self.rcp[:, cs], in_=self.rs4[:, cs]))
                pcs_banks = [self.ps_a, self.ps_b, self.ps_t.bitcast(F32),
                             self.ps_strip[:, 0:512]]
                Cv = self.C_sb.rearrange("p (r two) -> p r two", two=2)
                lastmm = None
                for cc in range(4 * g + 3, 4 * g - 1, -1):
                    if cc != 4 * g + 3:
                        dv.wait(pe)
                    dv.inc(dv.raw.memset(self.C_sb[:], 0.0))
                    dv.inc(dv.raw.scalar_tensor_tensor(
                        out=Cv[:, cc : cc + 1, 0:1],
                        in0=self.rcp[:, cc : cc + 1].unsqueeze(2),
                        scalar=self.eide[:, h : h + 1],
                        in1=self.inv_bc[:, h * 32 + 2 * cc : h * 32 + 2 * cc + 1]
                            .unsqueeze(2),
                        op0=ALU.mult, op1=ALU.mult))
                    dv.inc(dv.raw.scalar_tensor_tensor(
                        out=Cv[:, cc : cc + 1, 1:2],
                        in0=self.rcp[:, cc : cc + 1].unsqueeze(2),
                        scalar=self.eido[:, h : h + 1],
                        in1=self.inv_bc[:, h * 32 + 2 * cc + 1 : h * 32 + 2 * cc + 2]
                            .unsqueeze(2),
                        op0=ALU.mult, op1=ALU.mult))
                    pe.wait(dv)
                    wcc = 128 * cc + 128
                    ntt = (wcc + 511) // 512
                    for t in range(ntt):
                        n = min(512, wcc - 512 * t)
                        bank = pcs_banks[t]
                        st = not pcs_started[t]
                        pcs_started[t] = True
                        stop = (cc == 4 * g) and (g == 3)
                        mm = self.pe_mm(
                            0, bank[64 * (t % 2) : 64 * (t % 2) + 32, 0:n],
                            self.C_sb[:],
                            self.E4[:, (cc % 4) * 2048 + 512 * t :
                                    (cc % 4) * 2048 + 512 * t + n],
                            start=st, stop=stop, tile_position=(0, 64 * (t % 2)))
                        lastmm = mm
                    pe.inc(lastmm)
                dv.wait(pe)
                dv.inc(dv.raw.tensor_copy(
                    out=self.ot_sb[:, 4 * g * 65 : 4 * g * 65 + 260],
                    in_=self.ps_c[:, 0:260]))
        if SUB_LIMIT < 4:
            return
        self.post_head(b, half, i, h, hp, rb, qh, va)

    def post_head(self, b, half, i, h, hp, rb, qh, va):
        sy, dv, ac, pe, gp = self.sy, self.dv, self.ac, self.pe, self.gp
        dv.wait(pe)
        pcs_banks = [self.ps_a, self.ps_b, self.ps_t.bitcast(F32),
                     self.ps_strip[:, 0:512]]
        for t in range(4):
            rbt = 64 * (t % 2)
            dv.inc(dv.raw.tensor_copy(
                out=self.pcs_sb[rbt : rbt + 32, (t // 2) * 512 : (t // 2) * 512 + 512],
                in_=pcs_banks[t][rbt : rbt + 32, 0:512]))
        self.tap(f"pcsA_{b}_{h}", self.pcs_sb[0:32, :])
        self.tap(f"pcsB_{b}_{h}", self.pcs_sb[64:96, :])
        if SUB_LIMIT == 40:
            return
        pe.wait(dv)
        last = None
        pst_dst = self.ps_t.bitcast(F32)
        last = None
        for mc in range(16):
            t = mc // 4
            mcl = mc % 4
            base = 64 * (t % 2)
            blk = self.pcs_sb[base : base + 32,
                              (t // 2) * 512 + 128 * mcl : (t // 2) * 512 + 128 * mcl + 128]
            last = self.pe_mm(base, pst_dst[:, mc * 32 : mc * 32 + 32], blk,
                              self.id_sb[base : base + 32, base : base + 32],
                              start=True, stop=True)
        pe.inc(last)
        if SUB_LIMIT == 41:
            return
        dv.wait(pe)
        for c in range(16):
            dv.inc(dv.raw.tensor_tensor_scan(
                out=self.w_sb.rearrange("p (c r) -> p c r", r=33)[:, c, 1:33],
                data0=self.d0_bc[:, h * 32 : h * 32 + 32],
                data1=pst_dst[:, c * 32 : c * 32 + 32],
                initial=0.0, op0=ALU.mult, op1=ALU.add))
        self.tap(f"w_{b}_{h}", self.w_sb[:])
        if SUB_LIMIT < 5:
            return
        pe.wait(ac)
        wv = self.w_sb.rearrange("p (c r) -> p c r", r=33)
        bufs = (self.vw, self.vw2)
        kv_mm_n = []
        kk = 0
        for j in range(4):
            cmax = min(NCH - 1, 4 * j + 3)
            for C in range(cmax + 1):
                buf = bufs[kk % 2]
                eng = dv if kk % 2 == 0 else gp
                if kk >= 2:
                    eng.wait(pe, kv_mm_n[kk - 2])
                eng.wait(dv)
                eng.inc(eng.raw.tensor_tensor(
                    out=buf.rearrange("p (r e) -> p r e", e=64),
                    in0=wv[:, C, 8 * j : 8 * j + 8].unsqueeze(2)
                        .broadcast_to([128, 8, 64]),
                    in1=va[:, C, i, 0:64].unsqueeze(1).broadcast_to([128, 8, 64]),
                    op=ALU.mult))
                pe.wait(eng)
                mm = self.pe_mm(
                    0, self.ps_strip[rb : rb + 64, j * 512 : j * 512 + 512],
                    self.ktok[:, C * 64 : C * 64 + 64], buf[:],
                    start=(C == 0), stop=(C == cmax), tile_position=(0, rb))
                pe.inc(mm)
                kv_mm_n.append(pe.n)
                kk += 1
        dv.wait(pe)
        for j in range(4):
            dv.inc(dv.raw.tensor_copy(
                out=self.kv_sb[rb : rb + 64, j * 512 : j * 512 + 512],
                in_=self.ps_strip[rb : rb + 64, j * 512 : j * 512 + 512]))
        self.tap(f"kv_{b}_{h}", self.kv_sb[rb : rb + 64, :])
        if SUB_LIMIT < 6:
            return
        pe.wait(dv)
        for r in range(R):
            cch = r // 2
            bank = self.ps_a if cch < 8 else self.ps_b
            mm = self.pe_mm(
                rb, bank[64 * (r % 2) : 64 * (r % 2) + 64,
                         (cch % 8) * 64 : (cch % 8) * 64 + 64],
                qh(r * 64, 64), self.kv_sb[rb : rb + 64, r * 64 : r * 64 + 64],
                start=True, stop=True,
                tile_position=(rb, 64 * (r % 2)))
            if r >= R - 2:
                pe.inc(mm)
        dv.wait(pe)
        otv = self.ot_sb.rearrange("p (c e) -> p c e", e=65)
        dv.inc(dv.raw.reciprocal(out=self.rst[:], in_=otv[:, :, 64]))
        dv.inc(dv.raw.tensor_tensor(out=self.Ffac[:], in0=self.rst[:],
                                    in1=self.tw_col[:].broadcast_to([128, NCH]),
                                    op=ALU.mult))
        for grp in range(2):
            bank = self.ps_a if grp == 0 else self.ps_b
            dv.inc(dv.raw.tensor_tensor(
                out=self.t1_sb.rearrange("p (c e) -> p c e", e=64),
                in0=otv[:, 8 * grp : 8 * grp + 8, 0:64],
                in1=self.Ffac[:, 8 * grp : 8 * grp + 8].unsqueeze(2)
                    .broadcast_to([128, 8, 64]),
                op=ALU.mult))
            dv.inc(dv.raw.scalar_tensor_tensor(
                out=self.y_tok[:, grp * 512 : grp * 512 + 512],
                in0=bank[:], scalar=self.f2_col[:], in1=self.t1_sb[:],
                op0=ALU.mult, op1=ALU.add))
        self.tap(f"ytok_{b}_{h}", self.y_tok[:])
        if SUB_LIMIT < 7:
            return
        jc = h // 2
        pstf = self.ps_t.bitcast(F32)
        for rnd in range(4):
            pe.wait(dv)
            last = None
            for s in range(4):
                slot = rnd * 4 + s
                last = self.pe_mm(
                    0, pstf[rb : rb + 64, s * 128 : s * 128 + 128],
                    self.y_tok[:, slot * 64 : slot * 64 + 64],
                    self.id_sb[:], start=(s == 0), stop=(s == 3),
                    tile_position=(0, rb))
            pe.inc(last)
            dv.wait(pe)
            dv.inc(dv.raw.tensor_copy(
                out=self.y_fm[rb : rb + 64,
                              jc * 2048 + rnd * 512 : jc * 2048 + rnd * 512 + 512],
                in_=pstf[rb : rb + 64, 0:512]))

    def ffn_batch(self, b):
        sy, dv, ac, pe = self.sy, self.dv, self.ac, self.pe
        dv.inc(dv.raw.tensor_scalar_mul(out=self.xn[:, 0:8192],
                                        in0=self.xn[:, 0:8192], scalar1=INV127))
        dv.inc(dv.raw.tensor_scalar_mul(out=self.xn[:, 8192:16384],
                                        in0=self.xn[:, 8192:16384], scalar1=INV127))
        pe.wait(dv)
        banks8 = [self.ps_strip[:, i * 512 : i * 512 + 512] for i in range(4)] + \
                 [self.ps_a, self.ps_b, self.ps_c, self.ps_t.bitcast(F32)]
        for lt in range(2):
            for qtr in range(4):
                ac.wait(pe)
                for ds in range(8):
                    self.dmat("ac", self.wgT_buf[:, ds * 768 : ds * 768 + 768],
                              self.scr_wg[qtr * 768 : qtr * 768 + 768,
                                          ds * 128 : ds * 128 + 128])
                    self.dmat("ac", self.wuT_buf[:, ds * 768 : ds * 768 + 768],
                              self.scr_wu[qtr * 768 : qtr * 768 + 768,
                                          ds * 128 : ds * 128 + 128])
                pe.wait(self.dqs)
                for hc6 in range(6):
                    hc = qtr * 6 + hc6
                    pe.wait(ac)
                    pe.wait(dv)
                    lastmm = None
                    for dc in range(8):
                        for s in range(2):
                            lastmm = self.pe_mm(
                                0, self.ps_strip[:, s * 512 : s * 512 + 512],
                                self.wgT_buf[:, dc * 768 + hc6 * 128 :
                                             dc * 768 + hc6 * 128 + 128],
                                self.xn[:, dc * 2048 + lt * 1024 + s * 512 :
                                        dc * 2048 + lt * 1024 + s * 512 + 512],
                                start=(dc == 0), stop=(dc == 7))
                            lastmm = self.pe_mm(
                                0, self.ps_strip[:, 1024 + s * 512 : 1024 + s * 512 + 512],
                                self.wuT_buf[:, dc * 768 + hc6 * 128 :
                                             dc * 768 + hc6 * 128 + 128],
                                self.xn[:, dc * 2048 + lt * 1024 + s * 512 :
                                        dc * 2048 + lt * 1024 + s * 512 + 512],
                                start=(dc == 0), stop=(dc == 7))
                    pe.inc(lastmm)
                    ac.wait(pe)
                    ac.inc(ac.raw.activation(out=self.sgu[:, 0:1024],
                                             in_=self.ps_strip[:, 0:1024],
                                             func=AF.Silu))
                    dv.wait(pe)
                    dv.inc(dv.raw.tensor_copy(out=self.sgu[:, 1024:2048],
                                              in_=self.ps_strip[:, 1024:2048]))
                    dv.wait(ac)
                    dv.inc(dv.raw.tensor_tensor(
                        out=self.h_sb[:, hc * 1024 : hc * 1024 + 1024],
                        in0=self.sgu[:, 0:1024], in1=self.sgu[:, 1024:2048],
                        op=ALU.mult))
            for ot in range(2):
                dn_mm_n = []
                for hc in range(FF // 128):
                    bufsel = (hc % 2) * 1024
                    if hc >= 2:
                        ac.wait(pe, dn_mm_n[hc - 2])
                    if hc >= 1:
                        ac.wait(self.dqs, self.dqs.n)
                    n = self.dmat("ac", self.wdT_buf[:, bufsel : bufsel + 512],
                                  self.scr_wd[ot * 512 : ot * 512 + 512,
                                              hc * 128 : hc * 128 + 128])
                    pe.wait(self.dqs, n)
                    if hc == 0:
                        pe.wait(ac)
                        pe.wait(dv)
                    lastmm = None
                    for lc in range(8):
                        lastmm = self.pe_mm(
                            0, banks8[lc][:],
                            self.h_sb[:, hc * 1024 + lc * 128 : hc * 1024 + lc * 128 + 128],
                            self.wdT_buf[:, bufsel : bufsel + 512],
                            start=(hc == 0), stop=(hc == FF // 128 - 1))
                    pe.inc(lastmm)
                    dn_mm_n.append(pe.n)
                ac.wait(pe)
                lastact = None
                for lc in range(8):
                    lastact = ac.raw.activation(
                        out=self.g_tok[:, (lt * 8 + lc) * 1024 + ot * 512 :
                                       (lt * 8 + lc) * 1024 + ot * 512 + 512],
                        in_=banks8[lc][:], func=AF.Silu, scale=INV127)
                    ac.inc(lastact)
        self.tap(f"g_{b}", self.g_tok[:])

    def wc_combine(self, b):
        sy, dv, ac, pe = self.sy, self.dv, self.ac, self.pe
        for ot in range(2):
            for ds in range(8):
                self.dmat("ac", self.wcT_buf[:, ot * 4096 + ds * 512 :
                                             ot * 4096 + ds * 512 + 512],
                          self.scr_wc[ot * 512 : ot * 512 + 512,
                                      ds * 128 : ds * 128 + 128])
        pe.wait(self.dqs)
        xin = self.x_in2.rearrange("p (i m) -> p i m", i=2)
        xout = self.x_out.rearrange("p (i m) -> p i m", i=2)
        gp = self.gp
        out_dma_n = []
        for lc in range(NCH):
            buf = lc % 2
            q = "sy" if buf == 0 else "gp"
            qeng = sy if buf == 0 else gp
            ctr = self.dq if buf == 0 else self.dqg
            if lc >= 2:
                qeng.wait(dv)
                dv.wait(self.dq, out_dma_n[lc - 2])
            n = self.dma(q, xin[:, buf, :], self.x[b, lc * 128 : lc * 128 + 128, :],
                         fence=True)
            for ot in range(2):
                bank = self.ps_a if ot == 0 else self.ps_b
                pe.wait(dv)
                lastmm = None
                for jcc in range(8):
                    lastmm = self.pe_mm(
                        0, bank[:],
                        self.y_fm[:, jcc * 2048 + lc * 128 : jcc * 2048 + lc * 128 + 128],
                        self.wcT_buf[:, ot * 4096 + jcc * 512 : ot * 4096 + jcc * 512 + 512],
                        start=(jcc == 0), stop=(jcc == 7))
                pe.inc(lastmm)
                dv.wait(pe)
                dv.inc(dv.raw.tensor_tensor(
                    out=self.t1_sb[:], in0=bank[:],
                    in1=self.g_tok[:, lc * 1024 + ot * 512 : lc * 1024 + ot * 512 + 512],
                    op=ALU.mult))
                dv.wait(ctr, n)
                dv.inc(dv.raw.tensor_tensor(
                    out=xout[:, buf, ot * 512 : ot * 512 + 512],
                    in0=self.t1_sb[:],
                    in1=xin[:, buf, ot * 512 : ot * 512 + 512], op=ALU.add))
            sy.wait(dv)
            out_dma_n.append(
                self.dma("sy", self.out[b, lc * 128 : lc * 128 + 128, :],
                         xout[:, buf, :], fence=True))


def _consts():
    i = np.arange(128)
    j = np.arange(128)
    js = np.arange(SEG)
    c_tri = (j[None, :] <= i[:, None]).astype(bf)
    c_terr = np.zeros((128, SEG), dtype=np.float32)
    c_terr[0:64, :] = 1.0
    c_terr[64:128, :] = (i[64:128, None] - 64 <= js[None, :]).astype(np.float32)
    c_terr0 = np.zeros((128, SEG), dtype=np.float32)
    c_terr0[0:64, :] = (i[0:64, None] <= js[None, :]).astype(np.float32)
    c_half = np.zeros((128, 1), dtype=np.float32)
    c_half[64:128] = 1.0
    return dict(c_id=np.eye(128, dtype=bf), c_idf=np.eye(16, dtype=np.float32),
                c_tri=c_tri, c_terr=c_terr.astype(bf), c_terr0=c_terr0.astype(bf),
                c_ones=np.ones((1, 128), dtype=bf), c_half=c_half)


def kernel(L0_attn_hidden_states, L0_attn_Wq, L0_attn_Wk, L0_attn_Wv, L0_attn_Wc,
           L0_attn_terrace_mix, L0_attn_index_decays, L0_attn_mega_decays,
           L0_ffn_w_gate, L0_ffn_w_up, L0_ffn_w_down):
    x = np.ascontiguousarray(L0_attn_hidden_states)
    k = K(taps=DEBUG_TAPS)
    nc = k.build()
    base = dict(Wq=L0_attn_Wq, Wk=L0_attn_Wk, Wv=L0_attn_Wv, Wc=L0_attn_Wc,
                wg=L0_ffn_w_gate, wu=L0_ffn_w_up, wd=L0_ffn_w_down,
                tm=L0_attn_terrace_mix, idec=L0_attn_index_decays,
                mdec=L0_attn_mega_decays, **_consts())
    in_maps = [dict(base, x=x[2 * c : 2 * c + 2]) for c in range(8)]
    res = run_bass_kernel_spmd(nc, in_maps, core_ids=list(range(8)))
    if DEBUG_TAPS:
        kernel.tap_results = [
            {n: rr.get("tap_" + n) for n in DEBUG_TAPS} for rr in
            [r for r in res.results]]
    return np.concatenate([r["out"] for r in res.results], axis=0)


# version 04
# speedup vs baseline: 1.2004x; 1.2004x over previous
import numpy as np
import ml_dtypes
import concourse.bass as bass
import concourse.mybir as mybir
from concourse.bass_utils import run_bass_kernel_spmd

F32 = mybir.dt.float32
BF16 = mybir.dt.bfloat16
I32 = mybir.dt.int32
U8 = mybir.dt.uint8
AF = mybir.ActivationFunctionType
ALU = mybir.AluOpType
AXX = mybir.AxisListType.X
bf = ml_dtypes.bfloat16

B_CORE, L, D, H, DH, SEG, R, NCH, FF = 2, 2048, 1024, 16, 64, 64, 32, 16, 3072
EPS_LN = 1e-5
EPS_W = 1e-5
INV127 = 1.0 / 127.0
SCALE_S = 0.125

DEBUG_TAPS = ()
PHASE_LIMIT = 10 ** 9
SUB_LIMIT = 10 ** 9
NPCT = 16


class _RecOp:

    __slots__ = ("name", "args", "kw", "incs")

    def __init__(self, name, args, kw):
        self.name = name
        self.args = args
        self.kw = kw
        self.incs = []


_NO_SELF_SYNC = {"wait_ge", "dma_start", "dma_start_transpose"}


class _Recorder:
    def __init__(self, eng):
        self._eng = eng

    def __getattr__(self, name):
        eng = self._eng

        def call(*a, **kw):
            if eng.self_sync and name not in _NO_SELF_SYNC:
                if eng.n > 0 and eng.waited.get(eng.sem.num, -1) < eng.n:
                    eng.q.append(_RecOp("wait_ge", (eng.sem, eng.n), {}))
                    eng.waited[eng.sem.num] = eng.n
                op = _RecOp(name, a, kw)
                eng.q.append(op)
                op.incs.append((eng.sem, eng.step))
                eng.n += eng.step
            else:
                op = _RecOp(name, a, kw)
                eng.q.append(op)
            return op

        return call


class Eng:

    def __init__(self, nc, name, dma_step=1, self_sync=False):
        self.sem = nc.alloc_semaphore("sem_" + name)
        self.n = 0
        self.step = dma_step
        self.waited = {}
        self.q = []
        self.self_sync = self_sync
        self.raw = _Recorder(self)

    def wait(self, other, val=None):
        if other is self:
            return
        val = other.n if val is None else val
        if val <= 0 or self.waited.get(other.sem.num, -1) >= val:
            return
        self.raw.wait_ge(other.sem, val)
        self.waited[other.sem.num] = val

    def inc(self, op):
        if (self.sem, self.step) in op.incs:
            return self.n
        op.incs.append((self.sem, self.step))
        self.n += self.step
        return self.n

    def replay(self, raw):
        for op in self.q:
            inst = getattr(raw, op.name)(*op.args, **op.kw)
            for sem, v in op.incs:
                inst.then_inc(sem, v)


class K:
    def __init__(self, taps=()):
        self.taps = set(taps)
        self.nc = bass.Bass("TRN2", target_bir_lowering=False)
        self.tap_tensors = {}
        self._declare_io()
        self._alloc_sbuf()
        self._alloc_psum()

    def _declare_io(self):
        nc = self.nc
        t = lambda n, s, d, k: nc.dram_tensor(n, s, d, kind=k).ap()
        self.x = t("x", [B_CORE, L, D], F32, "ExternalInput")
        self.Wq = t("Wq", [D, D], F32, "ExternalInput")
        self.Wk = t("Wk", [D, D], F32, "ExternalInput")
        self.Wv = t("Wv", [D, D], F32, "ExternalInput")
        self.Wc = t("Wc", [D, D], F32, "ExternalInput")
        self.wg = t("wg", [FF, D], I32, "ExternalInput")
        self.wu = t("wu", [FF, D], I32, "ExternalInput")
        self.wd = t("wd", [D, FF], I32, "ExternalInput")
        self.tm = t("tm", [SEG], F32, "ExternalInput")
        self.idec = t("idec", [H, SEG], F32, "ExternalInput")
        self.mdec = t("mdec", [H], F32, "ExternalInput")
        self.c_id = t("c_id", [128, 128], BF16, "ExternalInput")
        self.c_idf = t("c_idf", [16, 16], F32, "ExternalInput")
        self.c_tri = t("c_tri", [128, 128], BF16, "ExternalInput")
        self.c_terr = t("c_terr", [128, SEG], BF16, "ExternalInput")
        self.c_terr0 = t("c_terr0", [128, SEG], BF16, "ExternalInput")
        self.c_ones = t("c_ones", [1, 128], BF16, "ExternalInput")
        self.c_half = t("c_half", [128, 1], F32, "ExternalInput")
        self.out = t("out", [B_CORE, L, D], F32, "ExternalOutput")
        self.scr_wq = t("scr_wq", [D, D], BF16, "Internal")
        self.scr_wk = t("scr_wk", [D, D], BF16, "Internal")
        self.scr_wv = t("scr_wv", [D, D], BF16, "Internal")
        self.scr_wc = t("scr_wc", [D, D], BF16, "Internal")
        self.scr_wg = t("scr_wg", [FF, D], BF16, "Internal")
        self.scr_wu = t("scr_wu", [FF, D], BF16, "Internal")
        self.scr_wd = t("scr_wd", [D, FF], BF16, "Internal")
        self.scr_row = t("scr_row", [2, 512], F32, "Internal")

    def _alloc_sbuf(self):
        nc = self.nc
        A = lambda n, s, d: nc.alloc_sbuf_tensor(n, s, d).ap()
        self.id_sb = A("id_sb", [128, 128], BF16)
        self.tri_sb = A("tri_sb", [128, 128], BF16)
        self.terr_sb = A("terr_sb", [128, SEG], BF16)
        self.terr0_sb = A("terr0_sb", [128, SEG], BF16)
        self.ones_sb = A("ones_sb", [1, 128], BF16)
        self.half_sb = A("half_sb", [128, 1], F32)
        self.idf_sb = A("idf_sb", [16, 16], F32)
        self.tw_col = A("tw_col", [128, 1], F32)
        self.f2_col = A("f2_col", [128, 1], F32)
        self.eide = A("eide", [128, H], F32)
        self.eido = A("eido", [128, H], F32)
        self.d0_bc = A("d0_bc", [128, 512], BF16)
        self.inv_bc = A("inv_bc", [128, 512], BF16)
        self.idec_sb = A("idec_sb", [H, SEG], F32)
        self.eid_hs = A("eid_hs", [H, SEG], F32)
        self.sh_col = A("sh_col", [H, 1], F32)
        self.mega_col = A("mega_col", [H, 1], F32)
        self.norm_hs = A("norm_hs", [H, 33], F32)
        self.d0_hs = A("d0_hs", [H, 32], F32)
        self.inv_hs = A("inv_hs", [H, 32], F32)
        self.tmp_hs = A("tmp_hs", [H, 33], F32)
        self.rowd0f = A("rowd0f", [1, 512], F32)
        self.rowinvf = A("rowinvf", [1, 512], F32)
        self.rowd0 = A("rowd0", [1, 512], BF16)
        self.rowinv = A("rowinv", [1, 512], BF16)
        self.C_sb = A("C_sb", [128, 32], BF16)
        self.acc1 = A("acc1", [128, NCH], F32)
        self.accs = A("accs", [128, NCH], F32)
        self.rs4 = A("rs4", [128, NCH], F32)
        self.rcp = A("rcp", [128, NCH], F32)
        self.rst = A("rst", [128, NCH], F32)
        self.Ffac = A("Ffac", [128, NCH], F32)
        self.ln_cols = A("ln_cols", [128, 8], F32)
        self.t1_sb = A("t1_sb", [128, 512], F32)
        self.xn = A("xn", [128, 8 * L], BF16)
        self.y_fm = A("y_fm", [128, 8 * L], BF16)
        self.g_tok = A("g_tok", [128, NCH * D], BF16)
        self.pool_q = A("pool_q", [128, 66048], U8)
        self.pool_a = A("pool_a", [128, 33856], U8)

        qb = self.pool_q.bitcast(BF16)
        qf = self.pool_q.bitcast(F32)
        ab = self.pool_a.bitcast(BF16)
        af = self.pool_a.bitcast(F32)
        self.stage = qb[:, 0:24576]
        self.q_fm = qb[:, 0:8192]
        self.k_fm = qb[:, 8192:16384]
        self.v_aug = qb[:, 16384:24704]
        self.v_aug2 = qb[:, 24704:33024]
        self.h_sb = qb[:, 0:24576]
        self.sgu = qb[:, 24576:26624]
        self.wdT_buf = qb[:, 26624:28672]
        self.wcT_buf = qb[:, 0:8192]
        self.x_out = qf[:, 4096:6144]
        self.x_in2 = qf[:, 6144:8192]
        self.x_in = af[:, 0:2048]
        self.ln_sq = af[:, 2048:3072]
        self.xt_ln = ab[:, 4096:5120]
        self.wT = [ab[:, 0:4096], ab[:, 4096:8192], ab[:, 8192:12288]]
        self.E4 = ab[:, 0:8192]
        self.Et = ab[:, 8192:10240]
        self.ktok = ab[:, 10240:11264]
        self.w_sb = ab[:, 11264:11792]
        self.vw = ab[:, 11792:12304]
        self.vw2 = ab[:, 12304:12816]
        self.pcs_sb = ab[:, 12816:13840]
        self.kv_sb = ab[:, 13840:15888]
        self.ot_sb = ab[:, 15888:16928]
        self.y_tok = ab[:, 0:1024]
        self.wgT_buf = ab[:, 0:6144]
        self.wuT_buf = ab[:, 6144:12288]

    def _alloc_psum(self):
        nc = self.nc
        P = lambda n, s, d: nc.alloc_psum_tensor(n, s, d).ap()
        self.ps_strip = P("ps_strip", [128, 2048], F32)
        self.ps_a = P("ps_a", [128, 512], F32)
        self.ps_b = P("ps_b", [128, 512], F32)
        self.ps_c = P("ps_c", [128, 512], F32)
        self.ps_t = P("ps_t", [128, 1024], BF16)

    def build(self):
        nc = self.nc
        self._kbase = -1
        self._last_mm = None
        self.pe = Eng(nc, "pe")
        self.dv = Eng(nc, "dv", self_sync=True)
        self.ac = Eng(nc, "ac", self_sync=True)
        self.gp = Eng(nc, "gp", self_sync=True)
        self.sy = Eng(nc, "sy")
        self.dq = Eng(nc, "dq", dma_step=16)
        self.dqs = Eng(nc, "dqs", dma_step=16)
        self.dqg = Eng(nc, "dqg", dma_step=16)
        self.emit()
        with nc.Block() as blk:
            @blk.tensor
            def _(t):
                self.pe.replay(t)

            @blk.vector
            def _(v):
                self.dv.replay(v)

            @blk.scalar
            def _(s):
                self.ac.replay(s)

            @blk.gpsimd
            def _(g):
                self.gp.replay(g)

            @blk.sync
            def _(sy):
                self.sy.replay(sy)
        return nc

    def pe_mm(self, kbase, out, lhsT, rhs, **kw):
        if kbase != self._kbase:
            if self._last_mm is not None:
                self.pe.inc(self._last_mm)
            if self.pe.n > 0:
                self.pe.raw.wait_ge(self.pe.sem, self.pe.n)
            self._kbase = kbase
        op = self.pe.raw.matmul(out, lhsT, rhs, **kw)
        self._last_mm = op
        return op

    def tr(self, out, in_, ident, first, last, tp=None, kbase=0):
        kw = dict(start=first, stop=last, is_transpose=True)
        if tp is not None:
            kw["tile_position"] = tp
        return self.pe_mm(kbase, out, in_, ident, **kw)

    def dma(self, q, out, in_, fence=False, **kw):
        eng, ctr = {"sy": (self.sy, self.dq), "ac": (self.ac, self.dqs),
                    "gp": (self.gp, self.dqg)}[q]
        if fence:
            eng.wait(ctr, ctr.n)
        ctr.inc(eng.raw.dma_start(out=out, in_=in_, **kw))
        return ctr.n

    def dmat(self, q, out, in_):
        eng, ctr = {"sy": (self.sy, self.dq), "ac": (self.ac, self.dqs)}[q]
        ctr.inc(eng.raw.dma_start_transpose(out, in_))
        return ctr.n

    def barrier(self):
        engs = (self.pe, self.dv, self.ac, self.gp)
        for e in engs:
            for o in engs:
                if o is not e:
                    e.wait(o)
            e.wait(self.dq)
            e.wait(self.dqs)
            e.wait(self.dqg)
        for e in engs:
            self.sy.wait(e)

    def tap(self, name, src_ap, dtype=None):
        if name not in self.taps:
            return
        dtype = dtype or src_ap.dtype
        dst = self.nc.dram_tensor("tap_" + name, list(src_ap.shape), dtype,
                                  kind="ExternalOutput").ap()
        self.tap_tensors[name] = dst
        for e in (self.pe, self.dv, self.ac, self.gp):
            self.sy.wait(e)
        n = self.dma("sy", dst, src_ap, fence=True)
        for e in (self.pe, self.dv, self.ac, self.gp):
            e.wait(self.dq, n)

    def emit(self):
        ph = [0]

        def go():
            ph[0] += 1
            return ph[0] <= PHASE_LIMIT

        self.setup()
        for b in range(B_CORE):
            if go():
                self.ln_batch(b)
                self.barrier()
            for half in range(2):
                if go():
                    self.proj_half(b, half)
                    self.barrier()
                for i in range(8):
                    if go():
                        self.attn_head(b, half, i)
                self.barrier()
            if b == 0:
                self.tap(f"yfm_{b}", self.y_fm[:])
            if go():
                self.ffn_batch(b)
                self.barrier()
            if go():
                self.wc_combine(b)
                self.barrier()
        self.sy.wait(self.dq)
        self.ac.wait(self.dqs)
        self.gp.wait(self.dqg)

    def setup(self):
        sy, dv, ac, pe, gp = self.sy, self.dv, self.ac, self.pe, self.gp
        self.dma("sy", self.id_sb[:], self.c_id)
        self.dma("sy", self.tri_sb[:], self.c_tri)
        self.dma("sy", self.terr_sb[:], self.c_terr)
        self.dma("sy", self.terr0_sb[:], self.c_terr0)
        self.dma("sy", self.ones_sb[:], self.c_ones)
        self.dma("sy", self.half_sb[:], self.c_half)
        self.dma("sy", self.idf_sb[:], self.c_idf)
        self.dma("sy", self.idec_sb[:], self.idec)
        self.dma("sy", self.mega_col[:], self.mdec.unsqueeze(1))
        n_consts = self.dma("sy", self.tw_col[0:SEG, :], self.tm.unsqueeze(1))

        for src, scr, rows in ((self.Wq, self.scr_wq, D), (self.Wk, self.scr_wk, D),
                               (self.Wv, self.scr_wv, D), (self.Wc, self.scr_wc, D),
                               (self.wg, self.scr_wg, FF), (self.wu, self.scr_wu, FF),
                               (self.wd, self.scr_wd, D)):
            cols = src.shape[1]
            nch = rows // 128
            stg = self.stage[:, 0 : nch * cols]
            gp.wait(self.dq)
            n = self.dma("gp", stg.rearrange("p (c m) -> p c m", c=nch),
                         src.rearrange("(c p) m -> p c m", p=128))
            sy.wait(self.dqg, n)
            self.dma("sy", scr.rearrange("(c p) m -> p c m", p=128),
                     stg.rearrange("p (c m) -> p c m", c=nch))
        self.wb_n = self.dq.n

        dv.wait(self.dq, n_consts)
        dv.inc(dv.raw.tensor_copy(out=self.tw_col[64:128, :], in_=self.tw_col[0:SEG, :]))
        ac.wait(dv)
        ac.inc(ac.raw.activation(out=self.tw_col[:], in_=self.tw_col[:], func=AF.Sigmoid))
        dv.wait(ac)
        dv.inc(dv.raw.tensor_scalar(out=self.f2_col[:], in0=self.tw_col[:],
                                    scalar1=-1.0, scalar2=-1.0,
                                    op0=ALU.mult, op1=ALU.subtract))
        ac.inc(ac.raw.activation(out=self.eid_hs[:], in_=self.idec_sb[:], func=AF.Exp,
                                 accum_out=self.sh_col[:]))
        ac.inc(ac.raw.activation(out=self.mega_col[:], in_=self.mega_col[:],
                                 func=AF.Exp, scale=float(SEG)))
        dv.wait(ac)
        dv.inc(dv.raw.tensor_scalar_min(out=self.mega_col[:], in0=self.mega_col[:],
                                        scalar1=2.0))
        dv.inc(dv.raw.tensor_tensor_scan(
            out=self.tmp_hs[:, 0:32],
            data0=self.mega_col[:].broadcast_to([H, 32]),
            data1=self.sh_col[:].broadcast_to([H, 32]),
            initial=0.0, op0=ALU.mult, op1=ALU.add))
        dv.inc(dv.raw.memset(self.norm_hs[:, 0:1], EPS_W))
        dv.inc(dv.raw.tensor_scalar_add(out=self.norm_hs[:, 1:33],
                                        in0=self.tmp_hs[:, 0:32], scalar1=EPS_W))
        dv.inc(dv.raw.reciprocal(out=self.inv_hs[:], in_=self.norm_hs[:, 1:33]))
        dv.inc(dv.raw.tensor_tensor(out=self.d0_hs[:], in0=self.norm_hs[:, 0:32],
                                    in1=self.inv_hs[:], op=ALU.mult))
        dv.inc(dv.raw.tensor_scalar_mul(out=self.d0_hs[:], in0=self.d0_hs[:],
                                        scalar1=self.mega_col[:]))
        sy.wait(dv)
        self.dma("sy", self.scr_row[0:1, :].rearrange("o (h r) -> (o h) r", h=H),
                 self.d0_hs[:])
        self.dma("sy", self.scr_row[1:2, :].rearrange("o (h r) -> (o h) r", h=H),
                 self.inv_hs[:])
        sy.wait(self.dq)
        self.dma("sy", self.rowd0f[:], self.scr_row[0:1, :])
        n = self.dma("sy", self.rowinvf[:], self.scr_row[1:2, :])
        dv.wait(self.dq, n)
        dv.inc(dv.raw.tensor_copy(out=self.rowd0[:], in_=self.rowd0f[:]))
        dv.inc(dv.raw.tensor_copy(out=self.rowinv[:], in_=self.rowinvf[:]))
        pe.wait(dv)
        pe.inc(self.pe_mm(0, self.ps_a[:], self.ones_sb[:], self.rowd0[:],
                          start=True, stop=True))
        pe.inc(self.pe_mm(0, self.ps_b[:], self.ones_sb[:], self.rowinv[:],
                          start=True, stop=True))
        dv.wait(pe)
        dv.inc(dv.raw.tensor_copy(out=self.d0_bc[:], in_=self.ps_a[:]))
        dv.inc(dv.raw.tensor_copy(out=self.inv_bc[:], in_=self.ps_b[:]))
        dv.inc(dv.raw.memset(self.eide[:], 0.0))
        dv.inc(dv.raw.memset(self.eido[:], 0.0))
        pe.wait(dv)
        psf = self.ps_c
        pe.inc(self.tr(psf[0:64, 0:16], self.eid_hs[:], self.idf_sb[:], True, True))
        dv.wait(pe)
        dv.inc(dv.raw.tensor_copy(out=self.eide[0:64, :], in_=psf[0:64, 0:16]))
        scr16 = self.scr_row.rearrange("a (c d) -> (a c) d", c=32, d=16)[0:64, :]
        sy.wait(dv)
        n = self.dma("sy", scr16, self.eide[0:64, :], fence=True)
        sy.wait(self.dq, n)
        n = self.dma("sy", self.eido[64:128, :], scr16, fence=True)
        dv.wait(self.dq, n)
        dv.inc(dv.raw.memset(self.w_sb[:], 0.0))
        va = self.v_aug.rearrange("p (c g e) -> p c g e", c=NCH, g=8)
        va2 = self.v_aug2.rearrange("p (c g e) -> p c g e", c=NCH, g=8)
        dv.inc(dv.raw.memset(va[:, :, :, 64:65], 1.0))
        dv.inc(dv.raw.memset(va2[:, :, :, 64:65], 1.0))
        self.barrier()
        self.tap("d0_bc", self.d0_bc[:])
        self.tap("inv_bc", self.inv_bc[:])
        self.tap("eide", self.eide[:])
        self.tap("norms", self.norm_hs[:])

    def ln_batch(self, b):
        sy, dv, ac, pe = self.sy, self.dv, self.ac, self.pe
        xin = self.x_in.rearrange("p (i m) -> p i m", i=2)
        lnc = self.ln_cols
        gp = self.gp
        for lc in range(NCH):
            buf = lc % 2
            q = "sy" if buf == 0 else "gp"
            qeng = sy if buf == 0 else gp
            ctr = self.dq if buf == 0 else self.dqg
            if lc >= 2:
                qeng.wait(dv)
                qeng.wait(ac)
            n = self.dma(q, xin[:, buf, :], self.x[b, lc * 128 : lc * 128 + 128, :],
                         fence=True)
            dv.wait(ctr, n)
            ac.wait(ctr, n)
            ac.wait(pe)
            ac.inc(ac.raw.activation(out=self.ln_sq[:], in_=xin[:, buf, :],
                                     func=AF.Square, accum_out=lnc[:, 1:2]))
            dv.inc(dv.raw.reduce_sum(out=lnc[:, 0:1], in_=xin[:, buf, :], axis=AXX))
            dv.wait(ac)
            dv.inc(dv.raw.tensor_scalar_mul(out=lnc[:, 2:3], in0=lnc[:, 0:1],
                                            scalar1=1.0 / D))
            dv.inc(dv.raw.tensor_scalar_mul(out=lnc[:, 3:4], in0=lnc[:, 1:2],
                                            scalar1=1.0 / D))
            dv.inc(dv.raw.tensor_tensor(out=lnc[:, 4:5], in0=lnc[:, 2:3],
                                        in1=lnc[:, 2:3], op=ALU.mult))
            dv.inc(dv.raw.tensor_tensor(out=lnc[:, 5:6], in0=lnc[:, 3:4],
                                        in1=lnc[:, 4:5], op=ALU.subtract))
            dv.inc(dv.raw.tensor_scalar_add(out=lnc[:, 5:6], in0=lnc[:, 5:6],
                                            scalar1=EPS_LN))
            ac.wait(dv)
            ac.inc(ac.raw.activation(out=lnc[:, 6:7], in_=lnc[:, 5:6],
                                     func=AF.Sqrt))
            dv.wait(ac)
            dv.inc(dv.raw.reciprocal(out=lnc[:, 7:8], in_=lnc[:, 6:7]))
            xt = self.xt_ln
            dv.inc(dv.raw.tensor_scalar(out=xt, in0=xin[:, buf, :],
                                        scalar1=lnc[:, 2:3], scalar2=lnc[:, 7:8],
                                        op0=ALU.subtract, op1=ALU.mult))
            pe.wait(dv)
            for g2 in range(2):
                if lc >= 1 or g2 >= 1:
                    pe.wait(dv)
                last = None
                for dc4 in range(4):
                    dc = g2 * 4 + dc4
                    last = self.tr(
                        self.ps_t[:, g2 * 512 + dc4 * 128 : g2 * 512 + dc4 * 128 + 128],
                        xt[:, dc * 128 : dc * 128 + 128], self.id_sb[:],
                        dc4 == 0, dc4 == 3)
                pe.inc(last)
                dv.wait(pe)
                dst = self.xn.rearrange("p (c m) -> p c m", c=8)[
                    :, g2 * 4 : g2 * 4 + 4, lc * 128 : lc * 128 + 128]
                dv.inc(dv.raw.tensor_copy(
                    out=dst,
                    in_=self.ps_t[:, g2 * 512 : g2 * 512 + 512]
                    .rearrange("p (c m) -> p c m", c=4)))
        self.tap(f"xn_{b}", self.xn[:])

    def proj_half(self, b, half):
        sy, dv, ac, pe = self.sy, self.dv, self.ac, self.pe
        self.ac.wait(self.dq, self.wb_n)
        for j, scr in enumerate((self.scr_wq, self.scr_wk, self.scr_wv)):
            for ds in range(8):
                self.dmat("ac", self.wT[j][:, ds * 512 : ds * 512 + 512],
                          scr[half * 512 : half * 512 + 512, ds * 128 : ds * 128 + 128])
        pe.wait(self.dqs)

        banks = [self.ps_strip[:, i * 512 : i * 512 + 512] for i in range(4)]
        for j, dst in ((0, self.q_fm), (1, self.k_fm)):
            for oc in range(4):
                pe.wait(dv)
                for dc in range(8):
                    lastmm = None
                    for lt in range(4):
                        lastmm = self.pe_mm(
                            0, banks[lt],
                            self.wT[j][:, dc * 512 + oc * 128 : dc * 512 + oc * 128 + 128],
                            self.xn[:, dc * 2048 + lt * 512 : dc * 2048 + lt * 512 + 512],
                            start=(dc == 0), stop=(dc == 7))
                    if dc == 7:
                        pe.inc(lastmm)
                dv.wait(pe)
                for lt in range(4):
                    dv.inc(dv.raw.tensor_copy(
                        out=dst[:, oc * 2048 + lt * 512 : oc * 2048 + lt * 512 + 512],
                        in_=banks[lt]))
        va = self.v_aug.rearrange("p (c g e) -> p c g e", c=NCH, g=8)
        va2 = self.v_aug2.rearrange("p (c g e) -> p c g e", c=NCH, g=8)
        dv.inc(dv.raw.memset(va[:, :, :, 64:65], 1.0))
        dv.inc(dv.raw.memset(va2[:, :, :, 64:65], 1.0))
        for lc in range(NCH):
            bank = self.ps_a if lc % 2 == 0 else self.ps_b
            pe.wait(dv)
            lastmm = None
            for dc in range(8):
                lastmm = self.pe_mm(
                    0, bank[:],
                    self.xn[:, dc * 2048 + lc * 128 : dc * 2048 + lc * 128 + 128],
                    self.wT[2][:, dc * 512 : dc * 512 + 512],
                    start=(dc == 0), stop=(dc == 7))
            pe.inc(lastmm)
            dv.wait(pe)
            src = bank.rearrange("p (g e) -> p g e", g=8)
            dv.inc(dv.raw.tensor_copy(out=va[:, lc, :, 0:64], in_=src))
            if lc >= 1:
                dv.inc(dv.raw.tensor_copy(out=va2[64:128, lc - 1, :, 0:64],
                                          in_=src[0:64, :, :]))
            if lc <= NCH - 2:
                dv.inc(dv.raw.tensor_copy(out=va2[0:64, lc, :, 0:64],
                                          in_=src[64:128, :, :]))
        self.tap(f"q_fm_{b}_{half}", self.q_fm[:])
        self.tap(f"k_fm_{b}_{half}", self.k_fm[:])
        self.tap(f"v_aug_{b}_{half}", self.v_aug[:])

    def attn_head(self, b, half, i):
        sy, dv, ac, pe, gp = self.sy, self.dv, self.ac, self.pe, self.gp
        h = half * 8 + i
        hp = i // 2
        rb = 64 * (i % 2)
        qh = lambda c0, n: self.q_fm[rb : rb + 64, hp * 2048 + c0 : hp * 2048 + c0 + n]
        kh = lambda c0, n: self.k_fm[rb : rb + 64, hp * 2048 + c0 : hp * 2048 + c0 + n]
        va = self.v_aug.rearrange("p (c g e) -> p c g e", c=NCH, g=8)
        va2 = self.v_aug2.rearrange("p (c g e) -> p c g e", c=NCH, g=8)
        idsl = self.id_sb[rb : rb + 64, rb : rb + 64]

        dv.inc(dv.raw.memset(self.acc1[:], 0.0))
        dv.inc(dv.raw.memset(self.accs[:], 0.0))
        dv.inc(dv.raw.memset(
            self.w_sb.rearrange("p (c r) -> p c r", r=33)[:, :, 0:1], 0.0))
        for g2 in range(2):
            pe.wait(dv)
            last = None
            for m4 in range(8):
                mc = g2 * 8 + m4
                last = self.tr(
                    self.ps_t[:, g2 * 512 + m4 * 64 : g2 * 512 + m4 * 64 + 64],
                    kh(mc * 128, 128), idsl, m4 == 0, m4 == 7, kbase=rb)
            pe.inc(last)
            dv.wait(pe)
            dv.inc(dv.raw.tensor_copy(out=self.ktok[:, g2 * 512 : g2 * 512 + 512],
                                      in_=self.ps_t[:, g2 * 512 : g2 * 512 + 512]))

        if SUB_LIMIT < 2:
            return
        pe.inc(self.pe_mm(rb, self.ps_c[0:64, 448:512], kh(0, 64), qh(0, 64),
                          start=True, stop=True))
        ac.wait(pe)
        ac.inc(ac.raw.activation(out=self.Et[0:64, 0:64],
                                 in_=self.ps_c[0:64, 448:512],
                                 func=AF.Exp, scale=SCALE_S))
        dv.inc(dv.raw.memset(self.Et[64:128, 0:64], 0.0))
        for rnd in range(4):
            bank = self.ps_strip[:, (rnd % 2) * 512 : (rnd % 2) * 512 + 512]
            pe.wait(dv)
            if rnd >= 2:
                pe.wait(ac)
            last = None
            first_s8 = 1 if rnd == 0 else 0
            for s8 in range(first_s8, 8):
                r = rnd * 8 + s8
                last = self.pe_mm(rb, bank[:, s8 * 64 : s8 * 64 + 64],
                                  kh(64 * (r - 1), 128), qh(64 * r, 64),
                                  start=(s8 == first_s8), stop=(s8 == 7))
            pe.inc(last)
            ac.wait(pe)
            ac.inc(ac.raw.activation(
                out=self.Et[:, rnd * 512 + first_s8 * 64 : rnd * 512 + 512],
                in_=bank[:, first_s8 * 64 : 512], func=AF.Exp, scale=SCALE_S))
        dv.wait(ac)
        dv.inc(dv.raw.tensor_tensor(out=self.Et[0:64, 0:64], in0=self.Et[0:64, 0:64],
                                    in1=self.terr0_sb[0:64, :], op=ALU.mult))
        dv.inc(dv.raw.tensor_tensor(
            out=self.Et[:, 64:2048].rearrange("p (s e) -> p s e", e=SEG),
            in0=self.Et[:, 64:2048].rearrange("p (s e) -> p s e", e=SEG),
            in1=self.terr_sb[:].unsqueeze(1).broadcast_to([128, 31, SEG]),
            op=ALU.mult))
        pe.wait(dv)

        if SUB_LIMIT < 3:
            return
        pcs_started = [False, False, False, False]
        for c in range(NCH):
            s4 = c % 4
            w = 128 * c + 128
            pe.wait(ac)
            pe.wait(dv)
            last = None
            for t in range((w + 511) // 512):
                n = min(512, w - 512 * t)
                last = self.pe_mm(rb, self.ps_strip[:, 512 * t : 512 * t + n],
                                  qh(c * 128, 128), kh(512 * t, n),
                                  start=True, stop=True)
            pe.inc(last)
            if c % 4 == 0 and c >= 4:
                pe.wait(dv)
            slot = self.ps_c[:, s4 * 65 : s4 * 65 + 65]
            r0, r1 = 2 * c, 2 * c + 1
            if r0 == 0:
                self.pe_mm(0, slot[0:64, :], self.Et[0:64, 0:64],
                           va[0:64, 0, i, :], start=True, stop=True)
            else:
                self.pe_mm(0, slot[0:64, :], self.Et[:, r0 * 64 : r0 * 64 + 64],
                           va2[:, c - 1, i, :], start=True, stop=True)
            mm1 = self.pe_mm(0, slot[64:128, :], self.Et[:, r1 * 64 : r1 * 64 + 64],
                             va[:, c, i, :], start=True, stop=True,
                             tile_position=(0, 64))
            pe.inc(mm1)

            ac.wait(pe)
            e0 = s4 * 2048
            if c >= 1:
                ac.inc(ac.raw.activation(
                    out=self.E4[:, e0 : e0 + w - 192],
                    in_=self.ps_strip[:, 0 : w - 192],
                    func=AF.Exp, scale=SCALE_S, accum_out=self.acc1[:, c : c + 1]))
                ac.inc(ac.raw.activation(
                    out=self.E4[:, e0 + w - 192 : e0 + w - 128],
                    in_=self.ps_strip[:, w - 192 : w - 128],
                    func=AF.Exp, scale=SCALE_S, accum_out=self.accs[:, c : c + 1]))
            ac.inc(ac.raw.activation(
                out=self.E4[:, e0 + w - 128 : e0 + w],
                in_=self.ps_strip[:, w - 128 : w], func=AF.Exp, scale=SCALE_S))
            dv.wait(ac)
            dv.inc(dv.raw.tensor_tensor(
                out=self.E4[:, e0 + w - 128 : e0 + w],
                in0=self.E4[:, e0 + w - 128 : e0 + w],
                in1=self.tri_sb[:], op=ALU.mult))

            if c % 4 == 3:
                g = c // 4
                cs = slice(4 * g, 4 * g + 4)
                dv.wait(pe)
                dv.inc(dv.raw.scalar_tensor_tensor(
                    out=self.rs4[:, cs], in0=self.accs[:, cs],
                    scalar=self.half_sb[:], in1=self.acc1[:, cs],
                    op0=ALU.mult, op1=ALU.add))
                otr = self.ps_c[:, 0:260].rearrange("p (c e) -> p c e", e=65)[:, :, 64:65]
                dv.inc(dv.raw.tensor_tensor(
                    out=self.rs4[:, cs].unsqueeze(2),
                    in0=self.rs4[:, cs].unsqueeze(2), in1=otr, op=ALU.add))
                dv.inc(dv.raw.reciprocal(out=self.rcp[:, cs], in_=self.rs4[:, cs]))
                pcs_banks = [self.ps_a, self.ps_b, self.ps_t.bitcast(F32),
                             self.ps_strip[:, 0:512]]
                Cv = self.C_sb.rearrange("p (r two) -> p r two", two=2)
                lastmm = None
                for cc in range(4 * g + 3, 4 * g - 1, -1):
                    if cc != 4 * g + 3:
                        dv.wait(pe)
                    dv.inc(dv.raw.memset(self.C_sb[:], 0.0))
                    dv.inc(dv.raw.scalar_tensor_tensor(
                        out=Cv[:, cc : cc + 1, 0:1],
                        in0=self.rcp[:, cc : cc + 1].unsqueeze(2),
                        scalar=self.eide[:, h : h + 1],
                        in1=self.inv_bc[:, h * 32 + 2 * cc : h * 32 + 2 * cc + 1]
                            .unsqueeze(2),
                        op0=ALU.mult, op1=ALU.mult))
                    dv.inc(dv.raw.scalar_tensor_tensor(
                        out=Cv[:, cc : cc + 1, 1:2],
                        in0=self.rcp[:, cc : cc + 1].unsqueeze(2),
                        scalar=self.eido[:, h : h + 1],
                        in1=self.inv_bc[:, h * 32 + 2 * cc + 1 : h * 32 + 2 * cc + 2]
                            .unsqueeze(2),
                        op0=ALU.mult, op1=ALU.mult))
                    pe.wait(dv)
                    wcc = 128 * cc + 128
                    ntt = (wcc + 511) // 512
                    for t in range(ntt):
                        n = min(512, wcc - 512 * t)
                        bank = pcs_banks[t]
                        st = not pcs_started[t]
                        pcs_started[t] = True
                        stop = (cc == 4 * g) and (g == 3)
                        mm = self.pe_mm(
                            0, bank[64 * (t % 2) : 64 * (t % 2) + 32, 0:n],
                            self.C_sb[:],
                            self.E4[:, (cc % 4) * 2048 + 512 * t :
                                    (cc % 4) * 2048 + 512 * t + n],
                            start=st, stop=stop, tile_position=(0, 64 * (t % 2)))
                        lastmm = mm
                    pe.inc(lastmm)
                dv.wait(pe)
                dv.inc(dv.raw.tensor_copy(
                    out=self.ot_sb[:, 4 * g * 65 : 4 * g * 65 + 260],
                    in_=self.ps_c[:, 0:260]))
        if SUB_LIMIT < 4:
            return
        self.post_head(b, half, i, h, hp, rb, qh, va)

    def post_head(self, b, half, i, h, hp, rb, qh, va):
        sy, dv, ac, pe, gp = self.sy, self.dv, self.ac, self.pe, self.gp
        dv.wait(pe)
        pcs_banks = [self.ps_a, self.ps_b, self.ps_t.bitcast(F32),
                     self.ps_strip[:, 0:512]]
        for t in range(4):
            rbt = 64 * (t % 2)
            dv.inc(dv.raw.tensor_copy(
                out=self.pcs_sb[rbt : rbt + 32, (t // 2) * 512 : (t // 2) * 512 + 512],
                in_=pcs_banks[t][rbt : rbt + 32, 0:512]))
        self.tap(f"pcsA_{b}_{h}", self.pcs_sb[0:32, :])
        self.tap(f"pcsB_{b}_{h}", self.pcs_sb[64:96, :])
        if SUB_LIMIT == 40:
            return
        pe.wait(dv)
        last = None
        pst_dst = self.ps_t.bitcast(F32)
        last = None
        for t in (0, 2, 1, 3):
          for mcl in range(4):
            mc = t * 4 + mcl
            base = 64 * (t % 2)
            blk = self.pcs_sb[base : base + 32,
                              (t // 2) * 512 + 128 * mcl : (t // 2) * 512 + 128 * mcl + 128]
            last = self.pe_mm(base, pst_dst[:, mc * 32 : mc * 32 + 32], blk,
                              self.id_sb[base : base + 32, base : base + 32],
                              start=True, stop=True)
        pe.inc(last)
        if SUB_LIMIT == 41:
            return
        dv.wait(pe)
        for c in range(16):
            dv.inc(dv.raw.tensor_tensor_scan(
                out=self.w_sb.rearrange("p (c r) -> p c r", r=33)[:, c, 1:33],
                data0=self.d0_bc[:, h * 32 : h * 32 + 32],
                data1=pst_dst[:, c * 32 : c * 32 + 32],
                initial=0.0, op0=ALU.mult, op1=ALU.add))
        self.tap(f"w_{b}_{h}", self.w_sb[:])
        if SUB_LIMIT < 5:
            return
        pe.wait(ac)
        wv = self.w_sb.rearrange("p (c r) -> p c r", r=33)
        bufs = (self.vw, self.vw2)
        kv_mm_n = []
        kk = 0
        for j in range(4):
            cmax = min(NCH - 1, 4 * j + 3)
            for C in range(cmax + 1):
                buf = bufs[kk % 2]
                if kk >= 2:
                    dv.wait(pe, kv_mm_n[kk - 2])
                dv.inc(dv.raw.tensor_tensor(
                    out=buf.rearrange("p (r e) -> p r e", e=64),
                    in0=wv[:, C, 8 * j : 8 * j + 8].unsqueeze(2)
                        .broadcast_to([128, 8, 64]),
                    in1=va[:, C, i, 0:64].unsqueeze(1).broadcast_to([128, 8, 64]),
                    op=ALU.mult))
                pe.wait(dv)
                mm = self.pe_mm(
                    0, self.ps_strip[rb : rb + 64, j * 512 : j * 512 + 512],
                    self.ktok[:, C * 64 : C * 64 + 64], buf[:],
                    start=(C == 0), stop=(C == cmax), tile_position=(0, rb))
                pe.inc(mm)
                kv_mm_n.append(pe.n)
                kk += 1
        dv.wait(pe)
        for j in range(4):
            dv.inc(dv.raw.tensor_copy(
                out=self.kv_sb[rb : rb + 64, j * 512 : j * 512 + 512],
                in_=self.ps_strip[rb : rb + 64, j * 512 : j * 512 + 512]))
        self.tap(f"kv_{b}_{h}", self.kv_sb[rb : rb + 64, :])
        if SUB_LIMIT < 6:
            return
        pe.wait(dv)
        for r in range(R):
            cch = r // 2
            bank = self.ps_a if cch < 8 else self.ps_b
            mm = self.pe_mm(
                rb, bank[64 * (r % 2) : 64 * (r % 2) + 64,
                         (cch % 8) * 64 : (cch % 8) * 64 + 64],
                qh(r * 64, 64), self.kv_sb[rb : rb + 64, r * 64 : r * 64 + 64],
                start=True, stop=True,
                tile_position=(rb, 64 * (r % 2)))
            if r >= R - 2:
                pe.inc(mm)
        dv.wait(pe)
        otv = self.ot_sb.rearrange("p (c e) -> p c e", e=65)
        dv.inc(dv.raw.reciprocal(out=self.rst[:], in_=otv[:, :, 64]))
        dv.inc(dv.raw.tensor_tensor(out=self.Ffac[:], in0=self.rst[:],
                                    in1=self.tw_col[:].broadcast_to([128, NCH]),
                                    op=ALU.mult))
        for grp in range(2):
            bank = self.ps_a if grp == 0 else self.ps_b
            dv.inc(dv.raw.tensor_tensor(
                out=self.t1_sb.rearrange("p (c e) -> p c e", e=64),
                in0=otv[:, 8 * grp : 8 * grp + 8, 0:64],
                in1=self.Ffac[:, 8 * grp : 8 * grp + 8].unsqueeze(2)
                    .broadcast_to([128, 8, 64]),
                op=ALU.mult))
            dv.inc(dv.raw.scalar_tensor_tensor(
                out=self.y_tok[:, grp * 512 : grp * 512 + 512],
                in0=bank[:], scalar=self.f2_col[:], in1=self.t1_sb[:],
                op0=ALU.mult, op1=ALU.add))
        self.tap(f"ytok_{b}_{h}", self.y_tok[:])
        if SUB_LIMIT < 7:
            return
        jc = h // 2
        pstf = self.ps_t.bitcast(F32)
        for rnd in range(4):
            pe.wait(dv)
            last = None
            for s in range(4):
                slot = rnd * 4 + s
                last = self.pe_mm(
                    0, pstf[rb : rb + 64, s * 128 : s * 128 + 128],
                    self.y_tok[:, slot * 64 : slot * 64 + 64],
                    self.id_sb[:], start=(s == 0), stop=(s == 3),
                    tile_position=(0, rb))
            pe.inc(last)
            dv.wait(pe)
            dv.inc(dv.raw.tensor_copy(
                out=self.y_fm[rb : rb + 64,
                              jc * 2048 + rnd * 512 : jc * 2048 + rnd * 512 + 512],
                in_=pstf[rb : rb + 64, 0:512]))

    def ffn_batch(self, b):
        sy, dv, ac, pe = self.sy, self.dv, self.ac, self.pe
        dv.inc(dv.raw.tensor_scalar_mul(out=self.xn[:, 0:8192],
                                        in0=self.xn[:, 0:8192], scalar1=INV127))
        dv.inc(dv.raw.tensor_scalar_mul(out=self.xn[:, 8192:16384],
                                        in0=self.xn[:, 8192:16384], scalar1=INV127))
        pe.wait(dv)
        banks8 = [self.ps_strip[:, i * 512 : i * 512 + 512] for i in range(4)] + \
                 [self.ps_a, self.ps_b, self.ps_c, self.ps_t.bitcast(F32)]
        for lt in range(2):
            for qtr in range(4):
                ac.wait(pe)
                for ds in range(8):
                    self.dmat("ac", self.wgT_buf[:, ds * 768 : ds * 768 + 768],
                              self.scr_wg[qtr * 768 : qtr * 768 + 768,
                                          ds * 128 : ds * 128 + 128])
                    self.dmat("ac", self.wuT_buf[:, ds * 768 : ds * 768 + 768],
                              self.scr_wu[qtr * 768 : qtr * 768 + 768,
                                          ds * 128 : ds * 128 + 128])
                pe.wait(self.dqs)
                for hc6 in range(6):
                    hc = qtr * 6 + hc6
                    pe.wait(ac)
                    pe.wait(dv)
                    lastmm = None
                    for dc in range(8):
                        for s in range(2):
                            lastmm = self.pe_mm(
                                0, self.ps_strip[:, s * 512 : s * 512 + 512],
                                self.wgT_buf[:, dc * 768 + hc6 * 128 :
                                             dc * 768 + hc6 * 128 + 128],
                                self.xn[:, dc * 2048 + lt * 1024 + s * 512 :
                                        dc * 2048 + lt * 1024 + s * 512 + 512],
                                start=(dc == 0), stop=(dc == 7))
                            lastmm = self.pe_mm(
                                0, self.ps_strip[:, 1024 + s * 512 : 1024 + s * 512 + 512],
                                self.wuT_buf[:, dc * 768 + hc6 * 128 :
                                             dc * 768 + hc6 * 128 + 128],
                                self.xn[:, dc * 2048 + lt * 1024 + s * 512 :
                                        dc * 2048 + lt * 1024 + s * 512 + 512],
                                start=(dc == 0), stop=(dc == 7))
                    pe.inc(lastmm)
                    ac.wait(pe)
                    ac.inc(ac.raw.activation(out=self.sgu[:, 0:1024],
                                             in_=self.ps_strip[:, 0:1024],
                                             func=AF.Silu))
                    dv.wait(pe)
                    dv.inc(dv.raw.tensor_copy(out=self.sgu[:, 1024:2048],
                                              in_=self.ps_strip[:, 1024:2048]))
                    dv.wait(ac)
                    dv.inc(dv.raw.tensor_tensor(
                        out=self.h_sb[:, hc * 1024 : hc * 1024 + 1024],
                        in0=self.sgu[:, 0:1024], in1=self.sgu[:, 1024:2048],
                        op=ALU.mult))
            for ot in range(2):
                dn_mm_n = []
                for hc in range(FF // 128):
                    bufsel = (hc % 2) * 1024
                    if hc >= 2:
                        ac.wait(pe, dn_mm_n[hc - 2])
                    if hc >= 1:
                        ac.wait(self.dqs, self.dqs.n)
                    n = self.dmat("ac", self.wdT_buf[:, bufsel : bufsel + 512],
                                  self.scr_wd[ot * 512 : ot * 512 + 512,
                                              hc * 128 : hc * 128 + 128])
                    pe.wait(self.dqs, n)
                    if hc == 0:
                        pe.wait(ac)
                        pe.wait(dv)
                    lastmm = None
                    for lc in range(8):
                        lastmm = self.pe_mm(
                            0, banks8[lc][:],
                            self.h_sb[:, hc * 1024 + lc * 128 : hc * 1024 + lc * 128 + 128],
                            self.wdT_buf[:, bufsel : bufsel + 512],
                            start=(hc == 0), stop=(hc == FF // 128 - 1))
                    pe.inc(lastmm)
                    dn_mm_n.append(pe.n)
                ac.wait(pe)
                lastact = None
                for lc in range(8):
                    lastact = ac.raw.activation(
                        out=self.g_tok[:, (lt * 8 + lc) * 1024 + ot * 512 :
                                       (lt * 8 + lc) * 1024 + ot * 512 + 512],
                        in_=banks8[lc][:], func=AF.Silu, scale=INV127)
                    ac.inc(lastact)
        self.tap(f"g_{b}", self.g_tok[:])

    def wc_combine(self, b):
        sy, dv, ac, pe = self.sy, self.dv, self.ac, self.pe
        for ot in range(2):
            for ds in range(8):
                self.dmat("ac", self.wcT_buf[:, ot * 4096 + ds * 512 :
                                             ot * 4096 + ds * 512 + 512],
                          self.scr_wc[ot * 512 : ot * 512 + 512,
                                      ds * 128 : ds * 128 + 128])
        pe.wait(self.dqs)
        xin = self.x_in2.rearrange("p (i m) -> p i m", i=2)
        xout = self.x_out.rearrange("p (i m) -> p i m", i=2)
        gp = self.gp
        out_dma_n = []
        for lc in range(NCH):
            buf = lc % 2
            q = "sy" if buf == 0 else "gp"
            qeng = sy if buf == 0 else gp
            ctr = self.dq if buf == 0 else self.dqg
            if lc >= 2:
                qeng.wait(dv)
                dv.wait(self.dq, out_dma_n[lc - 2])
            n = self.dma(q, xin[:, buf, :], self.x[b, lc * 128 : lc * 128 + 128, :],
                         fence=True)
            for ot in range(2):
                bank = self.ps_a if ot == 0 else self.ps_b
                pe.wait(dv)
                lastmm = None
                for jcc in range(8):
                    lastmm = self.pe_mm(
                        0, bank[:],
                        self.y_fm[:, jcc * 2048 + lc * 128 : jcc * 2048 + lc * 128 + 128],
                        self.wcT_buf[:, ot * 4096 + jcc * 512 : ot * 4096 + jcc * 512 + 512],
                        start=(jcc == 0), stop=(jcc == 7))
                pe.inc(lastmm)
                dv.wait(pe)
                dv.inc(dv.raw.tensor_tensor(
                    out=self.t1_sb[:], in0=bank[:],
                    in1=self.g_tok[:, lc * 1024 + ot * 512 : lc * 1024 + ot * 512 + 512],
                    op=ALU.mult))
                dv.wait(ctr, n)
                dv.inc(dv.raw.tensor_tensor(
                    out=xout[:, buf, ot * 512 : ot * 512 + 512],
                    in0=self.t1_sb[:],
                    in1=xin[:, buf, ot * 512 : ot * 512 + 512], op=ALU.add))
            sy.wait(dv)
            out_dma_n.append(
                self.dma("sy", self.out[b, lc * 128 : lc * 128 + 128, :],
                         xout[:, buf, :], fence=True))


def _consts():
    i = np.arange(128)
    j = np.arange(128)
    js = np.arange(SEG)
    c_tri = (j[None, :] <= i[:, None]).astype(bf)
    c_terr = np.zeros((128, SEG), dtype=np.float32)
    c_terr[0:64, :] = 1.0
    c_terr[64:128, :] = (i[64:128, None] - 64 <= js[None, :]).astype(np.float32)
    c_terr0 = np.zeros((128, SEG), dtype=np.float32)
    c_terr0[0:64, :] = (i[0:64, None] <= js[None, :]).astype(np.float32)
    c_half = np.zeros((128, 1), dtype=np.float32)
    c_half[64:128] = 1.0
    return dict(c_id=np.eye(128, dtype=bf), c_idf=np.eye(16, dtype=np.float32),
                c_tri=c_tri, c_terr=c_terr.astype(bf), c_terr0=c_terr0.astype(bf),
                c_ones=np.ones((1, 128), dtype=bf), c_half=c_half)


def kernel(L0_attn_hidden_states, L0_attn_Wq, L0_attn_Wk, L0_attn_Wv, L0_attn_Wc,
           L0_attn_terrace_mix, L0_attn_index_decays, L0_attn_mega_decays,
           L0_ffn_w_gate, L0_ffn_w_up, L0_ffn_w_down):
    x = np.ascontiguousarray(L0_attn_hidden_states)
    k = K(taps=DEBUG_TAPS)
    nc = k.build()
    base = dict(Wq=L0_attn_Wq, Wk=L0_attn_Wk, Wv=L0_attn_Wv, Wc=L0_attn_Wc,
                wg=L0_ffn_w_gate, wu=L0_ffn_w_up, wd=L0_ffn_w_down,
                tm=L0_attn_terrace_mix, idec=L0_attn_index_decays,
                mdec=L0_attn_mega_decays, **_consts())
    in_maps = [dict(base, x=x[2 * c : 2 * c + 2]) for c in range(8)]
    res = run_bass_kernel_spmd(nc, in_maps, core_ids=list(range(8)))
    if DEBUG_TAPS:
        kernel.tap_results = [
            {n: rr.get("tap_" + n) for n in DEBUG_TAPS} for rr in
            [r for r in res.results]]
    return np.concatenate([r["out"] for r in res.results], axis=0)


# version 05
# speedup vs baseline: 1.2214x; 1.0175x over previous
import numpy as np
import ml_dtypes
import concourse.bass as bass
import concourse.mybir as mybir
from concourse.bass_utils import run_bass_kernel_spmd

F32 = mybir.dt.float32
BF16 = mybir.dt.bfloat16
I32 = mybir.dt.int32
U8 = mybir.dt.uint8
AF = mybir.ActivationFunctionType
ALU = mybir.AluOpType
AXX = mybir.AxisListType.X
bf = ml_dtypes.bfloat16

B_CORE, L, D, H, DH, SEG, R, NCH, FF = 2, 2048, 1024, 16, 64, 64, 32, 16, 3072
EPS_LN = 1e-5
EPS_W = 1e-5
INV127 = 1.0 / 127.0
SCALE_S = 0.125

DEBUG_TAPS = ()
PHASE_LIMIT = 10 ** 9
SUB_LIMIT = 10 ** 9
NPCT = 16


class _RecOp:

    __slots__ = ("name", "args", "kw", "incs")

    def __init__(self, name, args, kw):
        self.name = name
        self.args = args
        self.kw = kw
        self.incs = []


_NO_SELF_SYNC = {"wait_ge", "dma_start", "dma_start_transpose"}


class _Recorder:
    def __init__(self, eng):
        self._eng = eng

    def __getattr__(self, name):
        eng = self._eng

        def call(*a, **kw):
            if eng.self_sync and name not in _NO_SELF_SYNC:
                if eng.n > 0 and eng.waited.get(eng.sem.num, -1) < eng.n:
                    eng.q.append(_RecOp("wait_ge", (eng.sem, eng.n), {}))
                    eng.waited[eng.sem.num] = eng.n
                op = _RecOp(name, a, kw)
                eng.q.append(op)
                op.incs.append((eng.sem, eng.step))
                eng.n += eng.step
            else:
                op = _RecOp(name, a, kw)
                eng.q.append(op)
            return op

        return call


class Eng:

    def __init__(self, nc, name, dma_step=1, self_sync=False):
        self.sem = nc.alloc_semaphore("sem_" + name)
        self.n = 0
        self.step = dma_step
        self.waited = {}
        self.q = []
        self.self_sync = self_sync
        self.raw = _Recorder(self)

    def wait(self, other, val=None):
        if other is self:
            return
        val = other.n if val is None else val
        if val <= 0 or self.waited.get(other.sem.num, -1) >= val:
            return
        self.raw.wait_ge(other.sem, val)
        self.waited[other.sem.num] = val

    def inc(self, op):
        if (self.sem, self.step) in op.incs:
            return self.n
        op.incs.append((self.sem, self.step))
        self.n += self.step
        return self.n

    def replay(self, raw):
        for op in self.q:
            inst = getattr(raw, op.name)(*op.args, **op.kw)
            for sem, v in op.incs:
                inst.then_inc(sem, v)


class K:
    def __init__(self, taps=()):
        self.taps = set(taps)
        self.nc = bass.Bass("TRN2", target_bir_lowering=False)
        self.tap_tensors = {}
        self._declare_io()
        self._alloc_sbuf()
        self._alloc_psum()

    def _declare_io(self):
        nc = self.nc
        t = lambda n, s, d, k: nc.dram_tensor(n, s, d, kind=k).ap()
        self.x = t("x", [B_CORE, L, D], F32, "ExternalInput")
        self.Wq = t("Wq", [D, D], F32, "ExternalInput")
        self.Wk = t("Wk", [D, D], F32, "ExternalInput")
        self.Wv = t("Wv", [D, D], F32, "ExternalInput")
        self.Wc = t("Wc", [D, D], F32, "ExternalInput")
        self.wg = t("wg", [FF, D], I32, "ExternalInput")
        self.wu = t("wu", [FF, D], I32, "ExternalInput")
        self.wd = t("wd", [D, FF], I32, "ExternalInput")
        self.tm = t("tm", [SEG], F32, "ExternalInput")
        self.idec = t("idec", [H, SEG], F32, "ExternalInput")
        self.mdec = t("mdec", [H], F32, "ExternalInput")
        self.c_id = t("c_id", [128, 128], BF16, "ExternalInput")
        self.c_idf = t("c_idf", [16, 16], F32, "ExternalInput")
        self.c_tri = t("c_tri", [128, 128], BF16, "ExternalInput")
        self.c_terr = t("c_terr", [128, SEG], BF16, "ExternalInput")
        self.c_terr0 = t("c_terr0", [128, SEG], BF16, "ExternalInput")
        self.c_ones = t("c_ones", [1, 128], BF16, "ExternalInput")
        self.c_half = t("c_half", [128, 1], F32, "ExternalInput")
        self.out = t("out", [B_CORE, L, D], F32, "ExternalOutput")
        self.scr_wq = t("scr_wq", [D, D], BF16, "Internal")
        self.scr_wk = t("scr_wk", [D, D], BF16, "Internal")
        self.scr_wv = t("scr_wv", [D, D], BF16, "Internal")
        self.scr_wc = t("scr_wc", [D, D], BF16, "Internal")
        self.scr_wg = t("scr_wg", [FF, D], BF16, "Internal")
        self.scr_wu = t("scr_wu", [FF, D], BF16, "Internal")
        self.scr_wd = t("scr_wd", [D, FF], BF16, "Internal")
        self.scr_row = t("scr_row", [2, 512], F32, "Internal")

    def _alloc_sbuf(self):
        nc = self.nc
        A = lambda n, s, d: nc.alloc_sbuf_tensor(n, s, d).ap()
        self.id_sb = A("id_sb", [128, 128], BF16)
        self.tri_sb = A("tri_sb", [128, 128], BF16)
        self.terr_sb = A("terr_sb", [128, SEG], BF16)
        self.terr0_sb = A("terr0_sb", [128, SEG], BF16)
        self.ones_sb = A("ones_sb", [1, 128], BF16)
        self.half_sb = A("half_sb", [128, 1], F32)
        self.idf_sb = A("idf_sb", [16, 16], F32)
        self.tw_col = A("tw_col", [128, 1], F32)
        self.f2_col = A("f2_col", [128, 1], F32)
        self.eide = A("eide", [128, H], F32)
        self.eido = A("eido", [128, H], F32)
        self.d0_bc = A("d0_bc", [128, 512], BF16)
        self.inv_bc = A("inv_bc", [128, 512], BF16)
        self.idec_sb = A("idec_sb", [H, SEG], F32)
        self.eid_hs = A("eid_hs", [H, SEG], F32)
        self.sh_col = A("sh_col", [H, 1], F32)
        self.mega_col = A("mega_col", [H, 1], F32)
        self.norm_hs = A("norm_hs", [H, 33], F32)
        self.d0_hs = A("d0_hs", [H, 32], F32)
        self.inv_hs = A("inv_hs", [H, 32], F32)
        self.tmp_hs = A("tmp_hs", [H, 33], F32)
        self.rowd0f = A("rowd0f", [1, 512], F32)
        self.rowinvf = A("rowinvf", [1, 512], F32)
        self.rowd0 = A("rowd0", [1, 512], BF16)
        self.rowinv = A("rowinv", [1, 512], BF16)
        self.C_sb = A("C_sb", [128, 32], BF16)
        self.acc1 = A("acc1", [128, NCH], F32)
        self.accs = A("accs", [128, NCH], F32)
        self.rs4 = A("rs4", [128, NCH], F32)
        self.rcp = A("rcp", [128, NCH], F32)
        self.rst = A("rst", [128, NCH], F32)
        self.Ffac = A("Ffac", [128, NCH], F32)
        self.ln_cols = A("ln_cols", [128, 8], F32)
        self.t1_sb = A("t1_sb", [128, 512], F32)
        self.xn = A("xn", [128, 8 * L], BF16)
        self.y_fm = A("y_fm", [128, 8 * L], BF16)
        self.g_tok = A("g_tok", [128, NCH * D], BF16)
        self.pool_q = A("pool_q", [128, 66048], U8)
        self.pool_a = A("pool_a", [128, 33856], U8)

        qb = self.pool_q.bitcast(BF16)
        qf = self.pool_q.bitcast(F32)
        ab = self.pool_a.bitcast(BF16)
        af = self.pool_a.bitcast(F32)
        self.stage = qb[:, 0:24576]
        self.q_fm = qb[:, 0:8192]
        self.k_fm = qb[:, 8192:16384]
        self.v_aug = qb[:, 16384:24704]
        self.v_aug2 = qb[:, 24704:33024]
        self.h_sb = qb[:, 0:24576]
        self.sgu = qb[:, 24576:26624]
        self.wdT_buf = qb[:, 26624:28672]
        self.wcT_buf = qb[:, 0:8192]
        self.x_out = qf[:, 4096:6144]
        self.x_in2 = qf[:, 6144:8192]
        self.x_in = af[:, 0:2048]
        self.ln_sq = af[:, 2048:3072]
        self.xt_ln = ab[:, 4096:5120]
        self.wT = [ab[:, 0:4096], ab[:, 4096:8192], ab[:, 8192:12288]]
        self.E4 = ab[:, 0:8192]
        self.Et = ab[:, 8192:10240]
        self.ktok = ab[:, 10240:11264]
        self.w_sb = ab[:, 11264:11792]
        self.vw = ab[:, 11792:12304]
        self.vw2 = ab[:, 12304:12816]
        self.pcs_sb = ab[:, 12816:13840]
        self.kv_sb = ab[:, 13840:15888]
        self.ot_sb = ab[:, 15888:16928]
        self.y_tok = ab[:, 0:1024]
        self.wgT_buf = ab[:, 0:6144]
        self.wuT_buf = ab[:, 6144:12288]

    def _alloc_psum(self):
        nc = self.nc
        P = lambda n, s, d: nc.alloc_psum_tensor(n, s, d).ap()
        self.ps_strip = P("ps_strip", [128, 2048], F32)
        self.ps_a = P("ps_a", [128, 512], F32)
        self.ps_b = P("ps_b", [128, 512], F32)
        self.ps_c = P("ps_c", [128, 512], F32)
        self.ps_t = P("ps_t", [128, 1024], BF16)

    def build(self):
        nc = self.nc
        self._kbase = -1
        self._last_mm = None
        self.pe = Eng(nc, "pe")
        self.dv = Eng(nc, "dv", self_sync=True)
        self.ac = Eng(nc, "ac", self_sync=True)
        self.gp = Eng(nc, "gp", self_sync=True)
        self.sy = Eng(nc, "sy")
        self.dq = Eng(nc, "dq", dma_step=16)
        self.dqs = Eng(nc, "dqs", dma_step=16)
        self.dqg = Eng(nc, "dqg", dma_step=16)
        self.emit()
        with nc.Block() as blk:
            @blk.tensor
            def _(t):
                self.pe.replay(t)

            @blk.vector
            def _(v):
                self.dv.replay(v)

            @blk.scalar
            def _(s):
                self.ac.replay(s)

            @blk.gpsimd
            def _(g):
                self.gp.replay(g)

            @blk.sync
            def _(sy):
                self.sy.replay(sy)
        return nc

    def pe_mm(self, kbase, out, lhsT, rhs, **kw):
        if kbase != self._kbase:
            if self._last_mm is not None:
                self.pe.inc(self._last_mm)
            if self.pe.n > 0:
                self.pe.raw.wait_ge(self.pe.sem, self.pe.n)
            self._kbase = kbase
        op = self.pe.raw.matmul(out, lhsT, rhs, **kw)
        self._last_mm = op
        return op

    def tr(self, out, in_, ident, first, last, tp=None, kbase=0):
        kw = dict(start=first, stop=last, is_transpose=True)
        if tp is not None:
            kw["tile_position"] = tp
        return self.pe_mm(kbase, out, in_, ident, **kw)

    def dma(self, q, out, in_, fence=False, **kw):
        eng, ctr = {"sy": (self.sy, self.dq), "ac": (self.ac, self.dqs),
                    "gp": (self.gp, self.dqg)}[q]
        if fence:
            eng.wait(ctr, ctr.n)
        ctr.inc(eng.raw.dma_start(out=out, in_=in_, **kw))
        return ctr.n

    def dmat(self, q, out, in_):
        eng, ctr = {"sy": (self.sy, self.dq), "ac": (self.ac, self.dqs)}[q]
        ctr.inc(eng.raw.dma_start_transpose(out, in_))
        return ctr.n

    def barrier(self):
        engs = (self.pe, self.dv, self.ac, self.gp)
        for e in engs:
            for o in engs:
                if o is not e:
                    e.wait(o)
            e.wait(self.dq)
            e.wait(self.dqs)
            e.wait(self.dqg)
        for e in engs:
            self.sy.wait(e)

    def tap(self, name, src_ap, dtype=None):
        if name not in self.taps:
            return
        dtype = dtype or src_ap.dtype
        dst = self.nc.dram_tensor("tap_" + name, list(src_ap.shape), dtype,
                                  kind="ExternalOutput").ap()
        self.tap_tensors[name] = dst
        for e in (self.pe, self.dv, self.ac, self.gp):
            self.sy.wait(e)
        n = self.dma("sy", dst, src_ap, fence=True)
        for e in (self.pe, self.dv, self.ac, self.gp):
            e.wait(self.dq, n)

    def emit(self):
        ph = [0]

        def go():
            ph[0] += 1
            return ph[0] <= PHASE_LIMIT

        self.setup()
        for b in range(B_CORE):
            if go():
                self.ln_batch(b)
                self.barrier()
            for half in range(2):
                if go():
                    self.proj_half(b, half)
                    self.barrier()
                for i in range(8):
                    if go():
                        self.attn_head(b, half, i)
                self.barrier()
            if b == 0:
                self.tap(f"yfm_{b}", self.y_fm[:])
            if go():
                self.ffn_batch(b)
                self.barrier()
            if go():
                self.wc_combine(b)
                self.barrier()
        self.sy.wait(self.dq)
        self.ac.wait(self.dqs)
        self.gp.wait(self.dqg)

    def setup(self):
        sy, dv, ac, pe, gp = self.sy, self.dv, self.ac, self.pe, self.gp
        self.dma("sy", self.id_sb[:], self.c_id)
        self.dma("sy", self.tri_sb[:], self.c_tri)
        self.dma("sy", self.terr_sb[:], self.c_terr)
        self.dma("sy", self.terr0_sb[:], self.c_terr0)
        self.dma("sy", self.ones_sb[:], self.c_ones)
        self.dma("sy", self.half_sb[:], self.c_half)
        self.dma("sy", self.idf_sb[:], self.c_idf)
        self.dma("sy", self.idec_sb[:], self.idec)
        self.dma("sy", self.mega_col[:], self.mdec.unsqueeze(1))
        n_consts = self.dma("sy", self.tw_col[0:SEG, :], self.tm.unsqueeze(1))

        for src, scr, rows in ((self.Wq, self.scr_wq, D), (self.Wk, self.scr_wk, D),
                               (self.Wv, self.scr_wv, D), (self.Wc, self.scr_wc, D),
                               (self.wg, self.scr_wg, FF), (self.wu, self.scr_wu, FF),
                               (self.wd, self.scr_wd, D)):
            cols = src.shape[1]
            nch = rows // 128
            stg = self.stage[:, 0 : nch * cols]
            gp.wait(self.dq)
            n = self.dma("gp", stg.rearrange("p (c m) -> p c m", c=nch),
                         src.rearrange("(c p) m -> p c m", p=128))
            sy.wait(self.dqg, n)
            self.dma("sy", scr.rearrange("(c p) m -> p c m", p=128),
                     stg.rearrange("p (c m) -> p c m", c=nch))
        self.wb_n = self.dq.n

        dv.wait(self.dq, n_consts)
        dv.inc(dv.raw.tensor_copy(out=self.tw_col[64:128, :], in_=self.tw_col[0:SEG, :]))
        ac.wait(dv)
        ac.inc(ac.raw.activation(out=self.tw_col[:], in_=self.tw_col[:], func=AF.Sigmoid))
        dv.wait(ac)
        dv.inc(dv.raw.tensor_scalar(out=self.f2_col[:], in0=self.tw_col[:],
                                    scalar1=-1.0, scalar2=-1.0,
                                    op0=ALU.mult, op1=ALU.subtract))
        ac.inc(ac.raw.activation(out=self.eid_hs[:], in_=self.idec_sb[:], func=AF.Exp,
                                 accum_out=self.sh_col[:]))
        ac.inc(ac.raw.activation(out=self.mega_col[:], in_=self.mega_col[:],
                                 func=AF.Exp, scale=float(SEG)))
        dv.wait(ac)
        dv.inc(dv.raw.tensor_scalar_min(out=self.mega_col[:], in0=self.mega_col[:],
                                        scalar1=2.0))
        dv.inc(dv.raw.tensor_tensor_scan(
            out=self.tmp_hs[:, 0:32],
            data0=self.mega_col[:].broadcast_to([H, 32]),
            data1=self.sh_col[:].broadcast_to([H, 32]),
            initial=0.0, op0=ALU.mult, op1=ALU.add))
        dv.inc(dv.raw.memset(self.norm_hs[:, 0:1], EPS_W))
        dv.inc(dv.raw.tensor_scalar_add(out=self.norm_hs[:, 1:33],
                                        in0=self.tmp_hs[:, 0:32], scalar1=EPS_W))
        dv.inc(dv.raw.reciprocal(out=self.inv_hs[:], in_=self.norm_hs[:, 1:33]))
        dv.inc(dv.raw.tensor_tensor(out=self.d0_hs[:], in0=self.norm_hs[:, 0:32],
                                    in1=self.inv_hs[:], op=ALU.mult))
        dv.inc(dv.raw.tensor_scalar_mul(out=self.d0_hs[:], in0=self.d0_hs[:],
                                        scalar1=self.mega_col[:]))
        sy.wait(dv)
        self.dma("sy", self.scr_row[0:1, :].rearrange("o (h r) -> (o h) r", h=H),
                 self.d0_hs[:])
        self.dma("sy", self.scr_row[1:2, :].rearrange("o (h r) -> (o h) r", h=H),
                 self.inv_hs[:])
        sy.wait(self.dq)
        self.dma("sy", self.rowd0f[:], self.scr_row[0:1, :])
        n = self.dma("sy", self.rowinvf[:], self.scr_row[1:2, :])
        dv.wait(self.dq, n)
        dv.inc(dv.raw.tensor_copy(out=self.rowd0[:], in_=self.rowd0f[:]))
        dv.inc(dv.raw.tensor_copy(out=self.rowinv[:], in_=self.rowinvf[:]))
        pe.wait(dv)
        pe.inc(self.pe_mm(0, self.ps_a[:], self.ones_sb[:], self.rowd0[:],
                          start=True, stop=True))
        pe.inc(self.pe_mm(0, self.ps_b[:], self.ones_sb[:], self.rowinv[:],
                          start=True, stop=True))
        dv.wait(pe)
        dv.inc(dv.raw.tensor_copy(out=self.d0_bc[:], in_=self.ps_a[:]))
        dv.inc(dv.raw.tensor_copy(out=self.inv_bc[:], in_=self.ps_b[:]))
        dv.inc(dv.raw.memset(self.eide[:], 0.0))
        dv.inc(dv.raw.memset(self.eido[:], 0.0))
        pe.wait(dv)
        psf = self.ps_c
        pe.inc(self.tr(psf[0:64, 0:16], self.eid_hs[:], self.idf_sb[:], True, True))
        dv.wait(pe)
        dv.inc(dv.raw.tensor_copy(out=self.eide[0:64, :], in_=psf[0:64, 0:16]))
        scr16 = self.scr_row.rearrange("a (c d) -> (a c) d", c=32, d=16)[0:64, :]
        sy.wait(dv)
        n = self.dma("sy", scr16, self.eide[0:64, :], fence=True)
        sy.wait(self.dq, n)
        n = self.dma("sy", self.eido[64:128, :], scr16, fence=True)
        dv.wait(self.dq, n)
        dv.inc(dv.raw.memset(self.w_sb[:], 0.0))
        va = self.v_aug.rearrange("p (c g e) -> p c g e", c=NCH, g=8)
        va2 = self.v_aug2.rearrange("p (c g e) -> p c g e", c=NCH, g=8)
        dv.inc(dv.raw.memset(va[:, :, :, 64:65], 1.0))
        dv.inc(dv.raw.memset(va2[:, :, :, 64:65], 1.0))
        self.barrier()
        self.tap("d0_bc", self.d0_bc[:])
        self.tap("inv_bc", self.inv_bc[:])
        self.tap("eide", self.eide[:])
        self.tap("norms", self.norm_hs[:])

    def ln_batch(self, b):
        sy, dv, ac, pe = self.sy, self.dv, self.ac, self.pe
        xin = self.x_in.rearrange("p (i m) -> p i m", i=2)
        lnc = self.ln_cols
        gp = self.gp
        for lc in range(NCH):
            buf = lc % 2
            q = "sy" if buf == 0 else "gp"
            qeng = sy if buf == 0 else gp
            ctr = self.dq if buf == 0 else self.dqg
            if lc >= 2:
                qeng.wait(dv)
                qeng.wait(ac)
            n = self.dma(q, xin[:, buf, :], self.x[b, lc * 128 : lc * 128 + 128, :],
                         fence=True)
            dv.wait(ctr, n)
            ac.wait(ctr, n)
            ac.wait(pe)
            ac.inc(ac.raw.activation(out=self.ln_sq[:], in_=xin[:, buf, :],
                                     func=AF.Square, accum_out=lnc[:, 1:2]))
            dv.inc(dv.raw.reduce_sum(out=lnc[:, 0:1], in_=xin[:, buf, :], axis=AXX))
            dv.wait(ac)
            dv.inc(dv.raw.tensor_scalar_mul(out=lnc[:, 2:3], in0=lnc[:, 0:1],
                                            scalar1=1.0 / D))
            dv.inc(dv.raw.tensor_scalar_mul(out=lnc[:, 3:4], in0=lnc[:, 1:2],
                                            scalar1=1.0 / D))
            dv.inc(dv.raw.tensor_tensor(out=lnc[:, 4:5], in0=lnc[:, 2:3],
                                        in1=lnc[:, 2:3], op=ALU.mult))
            dv.inc(dv.raw.tensor_tensor(out=lnc[:, 5:6], in0=lnc[:, 3:4],
                                        in1=lnc[:, 4:5], op=ALU.subtract))
            dv.inc(dv.raw.tensor_scalar_add(out=lnc[:, 5:6], in0=lnc[:, 5:6],
                                            scalar1=EPS_LN))
            ac.wait(dv)
            ac.inc(ac.raw.activation(out=lnc[:, 6:7], in_=lnc[:, 5:6],
                                     func=AF.Sqrt))
            dv.wait(ac)
            dv.inc(dv.raw.reciprocal(out=lnc[:, 7:8], in_=lnc[:, 6:7]))
            xt = self.xt_ln
            dv.inc(dv.raw.tensor_scalar(out=xt, in0=xin[:, buf, :],
                                        scalar1=lnc[:, 2:3], scalar2=lnc[:, 7:8],
                                        op0=ALU.subtract, op1=ALU.mult))
            pe.wait(dv)
            for g2 in range(2):
                if lc >= 1 or g2 >= 1:
                    pe.wait(dv)
                last = None
                for dc4 in range(4):
                    dc = g2 * 4 + dc4
                    last = self.tr(
                        self.ps_t[:, g2 * 512 + dc4 * 128 : g2 * 512 + dc4 * 128 + 128],
                        xt[:, dc * 128 : dc * 128 + 128], self.id_sb[:],
                        dc4 == 0, dc4 == 3)
                pe.inc(last)
                dv.wait(pe)
                dst = self.xn.rearrange("p (c m) -> p c m", c=8)[
                    :, g2 * 4 : g2 * 4 + 4, lc * 128 : lc * 128 + 128]
                dv.inc(dv.raw.tensor_copy(
                    out=dst,
                    in_=self.ps_t[:, g2 * 512 : g2 * 512 + 512]
                    .rearrange("p (c m) -> p c m", c=4)))
        self.tap(f"xn_{b}", self.xn[:])

    def proj_half(self, b, half):
        sy, dv, ac, pe = self.sy, self.dv, self.ac, self.pe
        self.ac.wait(self.dq, self.wb_n)
        wload_n = []
        for j, scr in enumerate((self.scr_wq, self.scr_wk, self.scr_wv)):
            if j > 0:
                self.ac.wait(self.dqs, self.dqs.n)
            for ds in range(8):
                self.dmat("ac", self.wT[j][:, ds * 512 : ds * 512 + 512],
                          scr[half * 512 : half * 512 + 512, ds * 128 : ds * 128 + 128])
            wload_n.append(self.dqs.n)

        banks = [self.ps_strip[:, i * 512 : i * 512 + 512] for i in range(4)]
        for j, dst in ((0, self.q_fm), (1, self.k_fm)):
            pe.wait(self.dqs, wload_n[j])
            for oc in range(4):
                pe.wait(dv)
                for dc in range(8):
                    lastmm = None
                    for lt in range(4):
                        lastmm = self.pe_mm(
                            0, banks[lt],
                            self.wT[j][:, dc * 512 + oc * 128 : dc * 512 + oc * 128 + 128],
                            self.xn[:, dc * 2048 + lt * 512 : dc * 2048 + lt * 512 + 512],
                            start=(dc == 0), stop=(dc == 7))
                    if dc == 7:
                        pe.inc(lastmm)
                dv.wait(pe)
                for lt in range(4):
                    dv.inc(dv.raw.tensor_copy(
                        out=dst[:, oc * 2048 + lt * 512 : oc * 2048 + lt * 512 + 512],
                        in_=banks[lt]))
        va = self.v_aug.rearrange("p (c g e) -> p c g e", c=NCH, g=8)
        va2 = self.v_aug2.rearrange("p (c g e) -> p c g e", c=NCH, g=8)
        dv.inc(dv.raw.memset(va[:, :, :, 64:65], 1.0))
        dv.inc(dv.raw.memset(va2[:, :, :, 64:65], 1.0))
        pe.wait(self.dqs, wload_n[2])
        for lc in range(NCH):
            bank = self.ps_a if lc % 2 == 0 else self.ps_b
            pe.wait(dv)
            lastmm = None
            for dc in range(8):
                lastmm = self.pe_mm(
                    0, bank[:],
                    self.xn[:, dc * 2048 + lc * 128 : dc * 2048 + lc * 128 + 128],
                    self.wT[2][:, dc * 512 : dc * 512 + 512],
                    start=(dc == 0), stop=(dc == 7))
            pe.inc(lastmm)
            dv.wait(pe)
            src = bank.rearrange("p (g e) -> p g e", g=8)
            dv.inc(dv.raw.tensor_copy(out=va[:, lc, :, 0:64], in_=src))
            if lc >= 1:
                dv.inc(dv.raw.tensor_copy(out=va2[64:128, lc - 1, :, 0:64],
                                          in_=src[0:64, :, :]))
            if lc <= NCH - 2:
                dv.inc(dv.raw.tensor_copy(out=va2[0:64, lc, :, 0:64],
                                          in_=src[64:128, :, :]))
        self.tap(f"q_fm_{b}_{half}", self.q_fm[:])
        self.tap(f"k_fm_{b}_{half}", self.k_fm[:])
        self.tap(f"v_aug_{b}_{half}", self.v_aug[:])

    def attn_head(self, b, half, i):
        sy, dv, ac, pe, gp = self.sy, self.dv, self.ac, self.pe, self.gp
        h = half * 8 + i
        hp = i // 2
        rb = 64 * (i % 2)
        qh = lambda c0, n: self.q_fm[rb : rb + 64, hp * 2048 + c0 : hp * 2048 + c0 + n]
        kh = lambda c0, n: self.k_fm[rb : rb + 64, hp * 2048 + c0 : hp * 2048 + c0 + n]
        va = self.v_aug.rearrange("p (c g e) -> p c g e", c=NCH, g=8)
        va2 = self.v_aug2.rearrange("p (c g e) -> p c g e", c=NCH, g=8)
        idsl = self.id_sb[rb : rb + 64, rb : rb + 64]

        dv.inc(dv.raw.memset(self.acc1[:], 0.0))
        dv.inc(dv.raw.memset(self.accs[:], 0.0))
        dv.inc(dv.raw.memset(
            self.w_sb.rearrange("p (c r) -> p c r", r=33)[:, :, 0:1], 0.0))
        for g2 in range(2):
            pe.wait(dv)
            last = None
            for m4 in range(8):
                mc = g2 * 8 + m4
                last = self.tr(
                    self.ps_t[:, g2 * 512 + m4 * 64 : g2 * 512 + m4 * 64 + 64],
                    kh(mc * 128, 128), idsl, m4 == 0, m4 == 7, kbase=rb)
            pe.inc(last)
            dv.wait(pe)
            dv.inc(dv.raw.tensor_copy(out=self.ktok[:, g2 * 512 : g2 * 512 + 512],
                                      in_=self.ps_t[:, g2 * 512 : g2 * 512 + 512]))

        if SUB_LIMIT < 2:
            return
        pe.inc(self.pe_mm(rb, self.ps_c[0:64, 448:512], kh(0, 64), qh(0, 64),
                          start=True, stop=True))
        ac.wait(pe)
        ac.inc(ac.raw.activation(out=self.Et[0:64, 0:64],
                                 in_=self.ps_c[0:64, 448:512],
                                 func=AF.Exp, scale=SCALE_S))
        dv.inc(dv.raw.memset(self.Et[64:128, 0:64], 0.0))
        for rnd in range(4):
            bank = self.ps_strip[:, (rnd % 2) * 512 : (rnd % 2) * 512 + 512]
            pe.wait(dv)
            if rnd >= 2:
                pe.wait(ac)
            last = None
            first_s8 = 1 if rnd == 0 else 0
            for s8 in range(first_s8, 8):
                r = rnd * 8 + s8
                last = self.pe_mm(rb, bank[:, s8 * 64 : s8 * 64 + 64],
                                  kh(64 * (r - 1), 128), qh(64 * r, 64),
                                  start=(s8 == first_s8), stop=(s8 == 7))
            pe.inc(last)
            ac.wait(pe)
            ac.inc(ac.raw.activation(
                out=self.Et[:, rnd * 512 + first_s8 * 64 : rnd * 512 + 512],
                in_=bank[:, first_s8 * 64 : 512], func=AF.Exp, scale=SCALE_S))
        dv.wait(ac)
        dv.inc(dv.raw.tensor_tensor(out=self.Et[0:64, 0:64], in0=self.Et[0:64, 0:64],
                                    in1=self.terr0_sb[0:64, :], op=ALU.mult))
        dv.inc(dv.raw.tensor_tensor(
            out=self.Et[:, 64:2048].rearrange("p (s e) -> p s e", e=SEG),
            in0=self.Et[:, 64:2048].rearrange("p (s e) -> p s e", e=SEG),
            in1=self.terr_sb[:].unsqueeze(1).broadcast_to([128, 31, SEG]),
            op=ALU.mult))
        pe.wait(dv)

        if SUB_LIMIT < 3:
            return
        pcs_started = [False, False, False, False]
        for c in range(NCH):
            s4 = c % 4
            w = 128 * c + 128
            pe.wait(ac)
            pe.wait(dv)
            last = None
            for t in range((w + 511) // 512):
                n = min(512, w - 512 * t)
                last = self.pe_mm(rb, self.ps_strip[:, 512 * t : 512 * t + n],
                                  qh(c * 128, 128), kh(512 * t, n),
                                  start=True, stop=True)
            pe.inc(last)
            if c % 4 == 0 and c >= 4:
                pe.wait(dv)
            slot = self.ps_c[:, s4 * 65 : s4 * 65 + 65]
            r0, r1 = 2 * c, 2 * c + 1
            if r0 == 0:
                self.pe_mm(0, slot[0:64, :], self.Et[0:64, 0:64],
                           va[0:64, 0, i, :], start=True, stop=True)
            else:
                self.pe_mm(0, slot[0:64, :], self.Et[:, r0 * 64 : r0 * 64 + 64],
                           va2[:, c - 1, i, :], start=True, stop=True)
            mm1 = self.pe_mm(0, slot[64:128, :], self.Et[:, r1 * 64 : r1 * 64 + 64],
                             va[:, c, i, :], start=True, stop=True,
                             tile_position=(0, 64))
            pe.inc(mm1)

            ac.wait(pe)
            e0 = s4 * 2048
            if c >= 1:
                ac.inc(ac.raw.activation(
                    out=self.E4[:, e0 : e0 + w - 192],
                    in_=self.ps_strip[:, 0 : w - 192],
                    func=AF.Exp, scale=SCALE_S, accum_out=self.acc1[:, c : c + 1]))
                ac.inc(ac.raw.activation(
                    out=self.E4[:, e0 + w - 192 : e0 + w - 128],
                    in_=self.ps_strip[:, w - 192 : w - 128],
                    func=AF.Exp, scale=SCALE_S, accum_out=self.accs[:, c : c + 1]))
            ac.inc(ac.raw.activation(
                out=self.E4[:, e0 + w - 128 : e0 + w],
                in_=self.ps_strip[:, w - 128 : w], func=AF.Exp, scale=SCALE_S))
            dv.wait(ac)
            dv.inc(dv.raw.tensor_tensor(
                out=self.E4[:, e0 + w - 128 : e0 + w],
                in0=self.E4[:, e0 + w - 128 : e0 + w],
                in1=self.tri_sb[:], op=ALU.mult))

            if c % 4 == 3:
                g = c // 4
                cs = slice(4 * g, 4 * g + 4)
                dv.wait(pe)
                dv.inc(dv.raw.scalar_tensor_tensor(
                    out=self.rs4[:, cs], in0=self.accs[:, cs],
                    scalar=self.half_sb[:], in1=self.acc1[:, cs],
                    op0=ALU.mult, op1=ALU.add))
                otr = self.ps_c[:, 0:260].rearrange("p (c e) -> p c e", e=65)[:, :, 64:65]
                dv.inc(dv.raw.tensor_tensor(
                    out=self.rs4[:, cs].unsqueeze(2),
                    in0=self.rs4[:, cs].unsqueeze(2), in1=otr, op=ALU.add))
                dv.inc(dv.raw.reciprocal(out=self.rcp[:, cs], in_=self.rs4[:, cs]))
                pcs_banks = [self.ps_a, self.ps_b, self.ps_t.bitcast(F32),
                             self.ps_strip[:, 0:512]]
                Cv = self.C_sb.rearrange("p (r two) -> p r two", two=2)
                lastmm = None
                for cc in range(4 * g + 3, 4 * g - 1, -1):
                    if cc != 4 * g + 3:
                        dv.wait(pe)
                    dv.inc(dv.raw.memset(self.C_sb[:], 0.0))
                    dv.inc(dv.raw.scalar_tensor_tensor(
                        out=Cv[:, cc : cc + 1, 0:1],
                        in0=self.rcp[:, cc : cc + 1].unsqueeze(2),
                        scalar=self.eide[:, h : h + 1],
                        in1=self.inv_bc[:, h * 32 + 2 * cc : h * 32 + 2 * cc + 1]
                            .unsqueeze(2),
                        op0=ALU.mult, op1=ALU.mult))
                    dv.inc(dv.raw.scalar_tensor_tensor(
                        out=Cv[:, cc : cc + 1, 1:2],
                        in0=self.rcp[:, cc : cc + 1].unsqueeze(2),
                        scalar=self.eido[:, h : h + 1],
                        in1=self.inv_bc[:, h * 32 + 2 * cc + 1 : h * 32 + 2 * cc + 2]
                            .unsqueeze(2),
                        op0=ALU.mult, op1=ALU.mult))
                    pe.wait(dv)
                    wcc = 128 * cc + 128
                    ntt = (wcc + 511) // 512
                    for t in range(ntt):
                        n = min(512, wcc - 512 * t)
                        bank = pcs_banks[t]
                        st = not pcs_started[t]
                        pcs_started[t] = True
                        stop = (cc == 4 * g) and (g == 3)
                        mm = self.pe_mm(
                            0, bank[64 * (t % 2) : 64 * (t % 2) + 32, 0:n],
                            self.C_sb[:],
                            self.E4[:, (cc % 4) * 2048 + 512 * t :
                                    (cc % 4) * 2048 + 512 * t + n],
                            start=st, stop=stop, tile_position=(0, 64 * (t % 2)))
                        lastmm = mm
                    pe.inc(lastmm)
                dv.wait(pe)
                dv.inc(dv.raw.tensor_copy(
                    out=self.ot_sb[:, 4 * g * 65 : 4 * g * 65 + 260],
                    in_=self.ps_c[:, 0:260]))
        if SUB_LIMIT < 4:
            return
        self.post_head(b, half, i, h, hp, rb, qh, va)

    def post_head(self, b, half, i, h, hp, rb, qh, va):
        sy, dv, ac, pe, gp = self.sy, self.dv, self.ac, self.pe, self.gp
        dv.wait(pe)
        pcs_banks = [self.ps_a, self.ps_b, self.ps_t.bitcast(F32),
                     self.ps_strip[:, 0:512]]
        for t in range(4):
            rbt = 64 * (t % 2)
            dv.inc(dv.raw.tensor_copy(
                out=self.pcs_sb[rbt : rbt + 32, (t // 2) * 512 : (t // 2) * 512 + 512],
                in_=pcs_banks[t][rbt : rbt + 32, 0:512]))
        self.tap(f"pcsA_{b}_{h}", self.pcs_sb[0:32, :])
        self.tap(f"pcsB_{b}_{h}", self.pcs_sb[64:96, :])
        if SUB_LIMIT == 40:
            return
        pe.wait(dv)
        last = None
        pst_dst = self.ps_t.bitcast(F32)
        last = None
        for t in (0, 2, 1, 3):
          for mcl in range(4):
            mc = t * 4 + mcl
            base = 64 * (t % 2)
            blk = self.pcs_sb[base : base + 32,
                              (t // 2) * 512 + 128 * mcl : (t // 2) * 512 + 128 * mcl + 128]
            last = self.pe_mm(base, pst_dst[:, mc * 32 : mc * 32 + 32], blk,
                              self.id_sb[base : base + 32, base : base + 32],
                              start=True, stop=True)
        pe.inc(last)
        if SUB_LIMIT == 41:
            return
        dv.wait(pe)
        for c in range(16):
            dv.inc(dv.raw.tensor_tensor_scan(
                out=self.w_sb.rearrange("p (c r) -> p c r", r=33)[:, c, 1:33],
                data0=self.d0_bc[:, h * 32 : h * 32 + 32],
                data1=pst_dst[:, c * 32 : c * 32 + 32],
                initial=0.0, op0=ALU.mult, op1=ALU.add))
        self.tap(f"w_{b}_{h}", self.w_sb[:])
        if SUB_LIMIT < 5:
            return
        pe.wait(ac)
        wv = self.w_sb.rearrange("p (c r) -> p c r", r=33)
        bufs = (self.vw, self.vw2)
        kv_mm_n = []
        kk = 0
        for j in range(4):
            cmax = min(NCH - 1, 4 * j + 3)
            for C in range(cmax + 1):
                buf = bufs[kk % 2]
                if kk >= 2:
                    dv.wait(pe, kv_mm_n[kk - 2])
                dv.inc(dv.raw.tensor_tensor(
                    out=buf.rearrange("p (r e) -> p r e", e=64),
                    in0=wv[:, C, 8 * j : 8 * j + 8].unsqueeze(2)
                        .broadcast_to([128, 8, 64]),
                    in1=va[:, C, i, 0:64].unsqueeze(1).broadcast_to([128, 8, 64]),
                    op=ALU.mult))
                pe.wait(dv)
                mm = self.pe_mm(
                    0, self.ps_strip[rb : rb + 64, j * 512 : j * 512 + 512],
                    self.ktok[:, C * 64 : C * 64 + 64], buf[:],
                    start=(C == 0), stop=(C == cmax), tile_position=(0, rb))
                pe.inc(mm)
                kv_mm_n.append(pe.n)
                kk += 1
        dv.wait(pe)
        for j in range(4):
            dv.inc(dv.raw.tensor_copy(
                out=self.kv_sb[rb : rb + 64, j * 512 : j * 512 + 512],
                in_=self.ps_strip[rb : rb + 64, j * 512 : j * 512 + 512]))
        self.tap(f"kv_{b}_{h}", self.kv_sb[rb : rb + 64, :])
        if SUB_LIMIT < 6:
            return
        pe.wait(dv)
        for r in range(R):
            cch = r // 2
            bank = self.ps_a if cch < 8 else self.ps_b
            mm = self.pe_mm(
                rb, bank[64 * (r % 2) : 64 * (r % 2) + 64,
                         (cch % 8) * 64 : (cch % 8) * 64 + 64],
                qh(r * 64, 64), self.kv_sb[rb : rb + 64, r * 64 : r * 64 + 64],
                start=True, stop=True,
                tile_position=(rb, 64 * (r % 2)))
            if r >= R - 2:
                pe.inc(mm)
        dv.wait(pe)
        otv = self.ot_sb.rearrange("p (c e) -> p c e", e=65)
        dv.inc(dv.raw.reciprocal(out=self.rst[:], in_=otv[:, :, 64]))
        dv.inc(dv.raw.tensor_tensor(out=self.Ffac[:], in0=self.rst[:],
                                    in1=self.tw_col[:].broadcast_to([128, NCH]),
                                    op=ALU.mult))
        for grp in range(2):
            bank = self.ps_a if grp == 0 else self.ps_b
            dv.inc(dv.raw.tensor_tensor(
                out=self.t1_sb.rearrange("p (c e) -> p c e", e=64),
                in0=otv[:, 8 * grp : 8 * grp + 8, 0:64],
                in1=self.Ffac[:, 8 * grp : 8 * grp + 8].unsqueeze(2)
                    .broadcast_to([128, 8, 64]),
                op=ALU.mult))
            dv.inc(dv.raw.scalar_tensor_tensor(
                out=self.y_tok[:, grp * 512 : grp * 512 + 512],
                in0=bank[:], scalar=self.f2_col[:], in1=self.t1_sb[:],
                op0=ALU.mult, op1=ALU.add))
        self.tap(f"ytok_{b}_{h}", self.y_tok[:])
        if SUB_LIMIT < 7:
            return
        jc = h // 2
        pstf = self.ps_t.bitcast(F32)
        for rnd in range(4):
            pe.wait(dv)
            last = None
            for s in range(4):
                slot = rnd * 4 + s
                last = self.pe_mm(
                    0, pstf[rb : rb + 64, s * 128 : s * 128 + 128],
                    self.y_tok[:, slot * 64 : slot * 64 + 64],
                    self.id_sb[:], start=(s == 0), stop=(s == 3),
                    tile_position=(0, rb))
            pe.inc(last)
            dv.wait(pe)
            dv.inc(dv.raw.tensor_copy(
                out=self.y_fm[rb : rb + 64,
                              jc * 2048 + rnd * 512 : jc * 2048 + rnd * 512 + 512],
                in_=pstf[rb : rb + 64, 0:512]))

    def ffn_batch(self, b):
        sy, dv, ac, pe = self.sy, self.dv, self.ac, self.pe
        dv.inc(dv.raw.tensor_scalar_mul(out=self.xn[:, 0:8192],
                                        in0=self.xn[:, 0:8192], scalar1=INV127))
        dv.inc(dv.raw.tensor_scalar_mul(out=self.xn[:, 8192:16384],
                                        in0=self.xn[:, 8192:16384], scalar1=INV127))
        pe.wait(dv)
        banks8 = [self.ps_strip[:, i * 512 : i * 512 + 512] for i in range(4)] + \
                 [self.ps_a, self.ps_b, self.ps_c, self.ps_t.bitcast(F32)]
        for lt in range(2):
            for qtr in range(4):
                ac.wait(pe)
                for ds in range(8):
                    self.dmat("ac", self.wgT_buf[:, ds * 768 : ds * 768 + 768],
                              self.scr_wg[qtr * 768 : qtr * 768 + 768,
                                          ds * 128 : ds * 128 + 128])
                    self.dmat("ac", self.wuT_buf[:, ds * 768 : ds * 768 + 768],
                              self.scr_wu[qtr * 768 : qtr * 768 + 768,
                                          ds * 128 : ds * 128 + 128])
                pe.wait(self.dqs)
                for hc6 in range(6):
                    hc = qtr * 6 + hc6
                    pe.wait(ac)
                    pe.wait(dv)
                    lastmm = None
                    for dc in range(8):
                        for s in range(2):
                            lastmm = self.pe_mm(
                                0, self.ps_strip[:, s * 512 : s * 512 + 512],
                                self.wgT_buf[:, dc * 768 + hc6 * 128 :
                                             dc * 768 + hc6 * 128 + 128],
                                self.xn[:, dc * 2048 + lt * 1024 + s * 512 :
                                        dc * 2048 + lt * 1024 + s * 512 + 512],
                                start=(dc == 0), stop=(dc == 7))
                            lastmm = self.pe_mm(
                                0, self.ps_strip[:, 1024 + s * 512 : 1024 + s * 512 + 512],
                                self.wuT_buf[:, dc * 768 + hc6 * 128 :
                                             dc * 768 + hc6 * 128 + 128],
                                self.xn[:, dc * 2048 + lt * 1024 + s * 512 :
                                        dc * 2048 + lt * 1024 + s * 512 + 512],
                                start=(dc == 0), stop=(dc == 7))
                    pe.inc(lastmm)
                    ac.wait(pe)
                    ac.inc(ac.raw.activation(out=self.sgu[:, 0:1024],
                                             in_=self.ps_strip[:, 0:1024],
                                             func=AF.Silu))
                    dv.wait(pe)
                    dv.inc(dv.raw.tensor_copy(out=self.sgu[:, 1024:2048],
                                              in_=self.ps_strip[:, 1024:2048]))
                    dv.wait(ac)
                    dv.inc(dv.raw.tensor_tensor(
                        out=self.h_sb[:, hc * 1024 : hc * 1024 + 1024],
                        in0=self.sgu[:, 0:1024], in1=self.sgu[:, 1024:2048],
                        op=ALU.mult))
            for ot in range(2):
                dn_mm_n = []
                for hc in range(FF // 128):
                    bufsel = (hc % 2) * 1024
                    if hc >= 2:
                        ac.wait(pe, dn_mm_n[hc - 2])
                    if hc >= 1:
                        ac.wait(self.dqs, self.dqs.n)
                    n = self.dmat("ac", self.wdT_buf[:, bufsel : bufsel + 512],
                                  self.scr_wd[ot * 512 : ot * 512 + 512,
                                              hc * 128 : hc * 128 + 128])
                    pe.wait(self.dqs, n)
                    if hc == 0:
                        pe.wait(ac)
                        pe.wait(dv)
                    lastmm = None
                    for lc in range(8):
                        lastmm = self.pe_mm(
                            0, banks8[lc][:],
                            self.h_sb[:, hc * 1024 + lc * 128 : hc * 1024 + lc * 128 + 128],
                            self.wdT_buf[:, bufsel : bufsel + 512],
                            start=(hc == 0), stop=(hc == FF // 128 - 1))
                    pe.inc(lastmm)
                    dn_mm_n.append(pe.n)
                ac.wait(pe)
                lastact = None
                for lc in range(8):
                    lastact = ac.raw.activation(
                        out=self.g_tok[:, (lt * 8 + lc) * 1024 + ot * 512 :
                                       (lt * 8 + lc) * 1024 + ot * 512 + 512],
                        in_=banks8[lc][:], func=AF.Silu, scale=INV127)
                    ac.inc(lastact)
        self.tap(f"g_{b}", self.g_tok[:])

    def wc_combine(self, b):
        sy, dv, ac, pe = self.sy, self.dv, self.ac, self.pe
        for ot in range(2):
            for ds in range(8):
                self.dmat("ac", self.wcT_buf[:, ot * 4096 + ds * 512 :
                                             ot * 4096 + ds * 512 + 512],
                          self.scr_wc[ot * 512 : ot * 512 + 512,
                                      ds * 128 : ds * 128 + 128])
        pe.wait(self.dqs)
        xin = self.x_in2.rearrange("p (i m) -> p i m", i=2)
        xout = self.x_out.rearrange("p (i m) -> p i m", i=2)
        gp = self.gp
        out_dma_n = []
        for lc in range(NCH):
            buf = lc % 2
            q = "sy" if buf == 0 else "gp"
            qeng = sy if buf == 0 else gp
            ctr = self.dq if buf == 0 else self.dqg
            if lc >= 2:
                qeng.wait(dv)
                dv.wait(self.dq, out_dma_n[lc - 2])
            n = self.dma(q, xin[:, buf, :], self.x[b, lc * 128 : lc * 128 + 128, :],
                         fence=True)
            for ot in range(2):
                bank = self.ps_a if ot == 0 else self.ps_b
                pe.wait(dv)
                lastmm = None
                for jcc in range(8):
                    lastmm = self.pe_mm(
                        0, bank[:],
                        self.y_fm[:, jcc * 2048 + lc * 128 : jcc * 2048 + lc * 128 + 128],
                        self.wcT_buf[:, ot * 4096 + jcc * 512 : ot * 4096 + jcc * 512 + 512],
                        start=(jcc == 0), stop=(jcc == 7))
                pe.inc(lastmm)
                dv.wait(pe)
                dv.inc(dv.raw.tensor_tensor(
                    out=self.t1_sb[:], in0=bank[:],
                    in1=self.g_tok[:, lc * 1024 + ot * 512 : lc * 1024 + ot * 512 + 512],
                    op=ALU.mult))
                dv.wait(ctr, n)
                dv.inc(dv.raw.tensor_tensor(
                    out=xout[:, buf, ot * 512 : ot * 512 + 512],
                    in0=self.t1_sb[:],
                    in1=xin[:, buf, ot * 512 : ot * 512 + 512], op=ALU.add))
            sy.wait(dv)
            out_dma_n.append(
                self.dma("sy", self.out[b, lc * 128 : lc * 128 + 128, :],
                         xout[:, buf, :], fence=True))


def _consts():
    i = np.arange(128)
    j = np.arange(128)
    js = np.arange(SEG)
    c_tri = (j[None, :] <= i[:, None]).astype(bf)
    c_terr = np.zeros((128, SEG), dtype=np.float32)
    c_terr[0:64, :] = 1.0
    c_terr[64:128, :] = (i[64:128, None] - 64 <= js[None, :]).astype(np.float32)
    c_terr0 = np.zeros((128, SEG), dtype=np.float32)
    c_terr0[0:64, :] = (i[0:64, None] <= js[None, :]).astype(np.float32)
    c_half = np.zeros((128, 1), dtype=np.float32)
    c_half[64:128] = 1.0
    return dict(c_id=np.eye(128, dtype=bf), c_idf=np.eye(16, dtype=np.float32),
                c_tri=c_tri, c_terr=c_terr.astype(bf), c_terr0=c_terr0.astype(bf),
                c_ones=np.ones((1, 128), dtype=bf), c_half=c_half)


def kernel(L0_attn_hidden_states, L0_attn_Wq, L0_attn_Wk, L0_attn_Wv, L0_attn_Wc,
           L0_attn_terrace_mix, L0_attn_index_decays, L0_attn_mega_decays,
           L0_ffn_w_gate, L0_ffn_w_up, L0_ffn_w_down):
    x = np.ascontiguousarray(L0_attn_hidden_states)
    k = K(taps=DEBUG_TAPS)
    nc = k.build()
    base = dict(Wq=L0_attn_Wq, Wk=L0_attn_Wk, Wv=L0_attn_Wv, Wc=L0_attn_Wc,
                wg=L0_ffn_w_gate, wu=L0_ffn_w_up, wd=L0_ffn_w_down,
                tm=L0_attn_terrace_mix, idec=L0_attn_index_decays,
                mdec=L0_attn_mega_decays, **_consts())
    in_maps = [dict(base, x=x[2 * c : 2 * c + 2]) for c in range(8)]
    res = run_bass_kernel_spmd(nc, in_maps, core_ids=list(range(8)))
    if DEBUG_TAPS:
        kernel.tap_results = [
            {n: rr.get("tap_" + n) for n in DEBUG_TAPS} for rr in
            [r for r in res.results]]
    return np.concatenate([r["out"] for r in res.results], axis=0)


# version 06 (final)
# speedup vs baseline: 1.2294x; 1.0066x over previous
import numpy as np
import ml_dtypes
import concourse.bass as bass
import concourse.mybir as mybir
from concourse.bass_utils import run_bass_kernel_spmd

F32 = mybir.dt.float32
BF16 = mybir.dt.bfloat16
I32 = mybir.dt.int32
U8 = mybir.dt.uint8
AF = mybir.ActivationFunctionType
ALU = mybir.AluOpType
AXX = mybir.AxisListType.X
bf = ml_dtypes.bfloat16

B_CORE, L, D, H, DH, SEG, R, NCH, FF = 2, 2048, 1024, 16, 64, 64, 32, 16, 3072
EPS_LN = 1e-5
EPS_W = 1e-5
INV127 = 1.0 / 127.0
SCALE_S = 0.125

DEBUG_TAPS = ()
PHASE_LIMIT = 10 ** 9
SUB_LIMIT = 10 ** 9
NPCT = 16


class _RecOp:

    __slots__ = ("name", "args", "kw", "incs")

    def __init__(self, name, args, kw):
        self.name = name
        self.args = args
        self.kw = kw
        self.incs = []


_NO_SELF_SYNC = {"wait_ge", "dma_start", "dma_start_transpose"}


class _Recorder:
    def __init__(self, eng):
        self._eng = eng

    def __getattr__(self, name):
        eng = self._eng

        def call(*a, **kw):
            if eng.self_sync and name not in _NO_SELF_SYNC:
                if eng.n > 0 and eng.waited.get(eng.sem.num, -1) < eng.n:
                    eng.q.append(_RecOp("wait_ge", (eng.sem, eng.n), {}))
                    eng.waited[eng.sem.num] = eng.n
                op = _RecOp(name, a, kw)
                eng.q.append(op)
                op.incs.append((eng.sem, eng.step))
                eng.n += eng.step
            else:
                op = _RecOp(name, a, kw)
                eng.q.append(op)
            return op

        return call


class Eng:

    def __init__(self, nc, name, dma_step=1, self_sync=False):
        self.sem = nc.alloc_semaphore("sem_" + name)
        self.n = 0
        self.step = dma_step
        self.waited = {}
        self.q = []
        self.self_sync = self_sync
        self.raw = _Recorder(self)

    def wait(self, other, val=None):
        if other is self:
            return
        val = other.n if val is None else val
        if val <= 0 or self.waited.get(other.sem.num, -1) >= val:
            return
        self.raw.wait_ge(other.sem, val)
        self.waited[other.sem.num] = val

    def inc(self, op):
        if (self.sem, self.step) in op.incs:
            return self.n
        op.incs.append((self.sem, self.step))
        self.n += self.step
        return self.n

    def replay(self, raw):
        for op in self.q:
            inst = getattr(raw, op.name)(*op.args, **op.kw)
            for sem, v in op.incs:
                inst.then_inc(sem, v)


class K:
    def __init__(self, taps=()):
        self.taps = set(taps)
        self.nc = bass.Bass("TRN2", target_bir_lowering=False)
        self.tap_tensors = {}
        self._declare_io()
        self._alloc_sbuf()
        self._alloc_psum()

    def _declare_io(self):
        nc = self.nc
        t = lambda n, s, d, k: nc.dram_tensor(n, s, d, kind=k).ap()
        self.x = t("x", [B_CORE, L, D], F32, "ExternalInput")
        self.Wq = t("Wq", [D, D], F32, "ExternalInput")
        self.Wk = t("Wk", [D, D], F32, "ExternalInput")
        self.Wv = t("Wv", [D, D], F32, "ExternalInput")
        self.Wc = t("Wc", [D, D], F32, "ExternalInput")
        self.wg = t("wg", [FF, D], I32, "ExternalInput")
        self.wu = t("wu", [FF, D], I32, "ExternalInput")
        self.wd = t("wd", [D, FF], I32, "ExternalInput")
        self.tm = t("tm", [SEG], F32, "ExternalInput")
        self.idec = t("idec", [H, SEG], F32, "ExternalInput")
        self.mdec = t("mdec", [H], F32, "ExternalInput")
        self.c_id = t("c_id", [128, 128], BF16, "ExternalInput")
        self.c_idf = t("c_idf", [16, 16], F32, "ExternalInput")
        self.c_tri = t("c_tri", [128, 128], BF16, "ExternalInput")
        self.c_terr = t("c_terr", [128, SEG], BF16, "ExternalInput")
        self.c_terr0 = t("c_terr0", [128, SEG], BF16, "ExternalInput")
        self.c_ones = t("c_ones", [1, 128], BF16, "ExternalInput")
        self.c_half = t("c_half", [128, 1], F32, "ExternalInput")
        self.out = t("out", [B_CORE, L, D], F32, "ExternalOutput")
        self.scr_wq = t("scr_wq", [D, D], BF16, "Internal")
        self.scr_wk = t("scr_wk", [D, D], BF16, "Internal")
        self.scr_wv = t("scr_wv", [D, D], BF16, "Internal")
        self.scr_wc = t("scr_wc", [D, D], BF16, "Internal")
        self.scr_wg = t("scr_wg", [FF, D], BF16, "Internal")
        self.scr_wu = t("scr_wu", [FF, D], BF16, "Internal")
        self.scr_wd = t("scr_wd", [D, FF], BF16, "Internal")
        self.scr_row = t("scr_row", [2, 512], F32, "Internal")

    def _alloc_sbuf(self):
        nc = self.nc
        A = lambda n, s, d: nc.alloc_sbuf_tensor(n, s, d).ap()
        self.id_sb = A("id_sb", [128, 128], BF16)
        self.tri_sb = A("tri_sb", [128, 128], BF16)
        self.terr_sb = A("terr_sb", [128, SEG], BF16)
        self.terr0_sb = A("terr0_sb", [128, SEG], BF16)
        self.ones_sb = A("ones_sb", [1, 128], BF16)
        self.half_sb = A("half_sb", [128, 1], F32)
        self.idf_sb = A("idf_sb", [16, 16], F32)
        self.tw_col = A("tw_col", [128, 1], F32)
        self.f2_col = A("f2_col", [128, 1], F32)
        self.eide = A("eide", [128, H], F32)
        self.eido = A("eido", [128, H], F32)
        self.d0_bc = A("d0_bc", [128, 512], BF16)
        self.inv_bc = A("inv_bc", [128, 512], BF16)
        self.idec_sb = A("idec_sb", [H, SEG], F32)
        self.eid_hs = A("eid_hs", [H, SEG], F32)
        self.sh_col = A("sh_col", [H, 1], F32)
        self.mega_col = A("mega_col", [H, 1], F32)
        self.norm_hs = A("norm_hs", [H, 33], F32)
        self.d0_hs = A("d0_hs", [H, 32], F32)
        self.inv_hs = A("inv_hs", [H, 32], F32)
        self.tmp_hs = A("tmp_hs", [H, 33], F32)
        self.rowd0f = A("rowd0f", [1, 512], F32)
        self.rowinvf = A("rowinvf", [1, 512], F32)
        self.rowd0 = A("rowd0", [1, 512], BF16)
        self.rowinv = A("rowinv", [1, 512], BF16)
        self.C_sb = A("C_sb", [128, 32], BF16)
        self.acc1 = A("acc1", [128, NCH], F32)
        self.accs = A("accs", [128, NCH], F32)
        self.rs4 = A("rs4", [128, NCH], F32)
        self.rcp = A("rcp", [128, NCH], F32)
        self.rst = A("rst", [128, NCH], F32)
        self.Ffac = A("Ffac", [128, NCH], F32)
        self.ln_cols = A("ln_cols", [128, 8], F32)
        self.t1_sb = A("t1_sb", [128, 512], F32)
        self.xn = A("xn", [128, 8 * L], BF16)
        self.y_fm = A("y_fm", [128, 8 * L], BF16)
        self.g_tok = A("g_tok", [128, NCH * D], BF16)
        self.pool_q = A("pool_q", [128, 66048], U8)
        self.pool_a = A("pool_a", [128, 33856], U8)

        qb = self.pool_q.bitcast(BF16)
        qf = self.pool_q.bitcast(F32)
        ab = self.pool_a.bitcast(BF16)
        af = self.pool_a.bitcast(F32)
        self.stage = qb[:, 0:24576]
        self.q_fm = qb[:, 0:8192]
        self.k_fm = qb[:, 8192:16384]
        self.v_aug = qb[:, 16384:24704]
        self.v_aug2 = qb[:, 24704:33024]
        self.h_sb = qb[:, 0:24576]
        self.sgu = qb[:, 24576:26624]
        self.wdT_buf = qb[:, 26624:28672]
        self.wcT_buf = qb[:, 0:8192]
        self.x_out = qf[:, 4096:6144]
        self.x_in2 = qf[:, 6144:8192]
        self.x_in = af[:, 0:2048]
        self.ln_sq = af[:, 2048:3072]
        self.xt_ln = ab[:, 4096:5120]
        self.wT = [ab[:, 0:4096], ab[:, 4096:8192], ab[:, 8192:12288]]
        self.E4 = ab[:, 0:8192]
        self.Et = ab[:, 8192:10240]
        self.ktok = ab[:, 10240:11264]
        self.w_sb = ab[:, 11264:11792]
        self.vw = ab[:, 11792:12304]
        self.vw2 = ab[:, 12304:12816]
        self.pcs_sb = ab[:, 12816:13840]
        self.kv_sb = ab[:, 13840:15888]
        self.ot_sb = ab[:, 15888:16928]
        self.y_tok = ab[:, 0:1024]
        self.wgT_buf = ab[:, 0:6144]
        self.wuT_buf = ab[:, 6144:12288]

    def _alloc_psum(self):
        nc = self.nc
        P = lambda n, s, d: nc.alloc_psum_tensor(n, s, d).ap()
        self.ps_strip = P("ps_strip", [128, 2048], F32)
        self.ps_a = P("ps_a", [128, 512], F32)
        self.ps_b = P("ps_b", [128, 512], F32)
        self.ps_c = P("ps_c", [128, 512], F32)
        self.ps_t = P("ps_t", [128, 1024], BF16)

    def build(self):
        nc = self.nc
        self._kbase = -1
        self._last_mm = None
        self.pe = Eng(nc, "pe")
        self.dv = Eng(nc, "dv", self_sync=True)
        self.ac = Eng(nc, "ac", self_sync=True)
        self.gp = Eng(nc, "gp", self_sync=True)
        self.sy = Eng(nc, "sy")
        self.dq = Eng(nc, "dq", dma_step=16)
        self.dqs = Eng(nc, "dqs", dma_step=16)
        self.dqg = Eng(nc, "dqg", dma_step=16)
        self.emit()
        with nc.Block() as blk:
            @blk.tensor
            def _(t):
                self.pe.replay(t)

            @blk.vector
            def _(v):
                self.dv.replay(v)

            @blk.scalar
            def _(s):
                self.ac.replay(s)

            @blk.gpsimd
            def _(g):
                self.gp.replay(g)

            @blk.sync
            def _(sy):
                self.sy.replay(sy)
        return nc

    def pe_mm(self, kbase, out, lhsT, rhs, **kw):
        if kbase != self._kbase:
            if self._last_mm is not None:
                self.pe.inc(self._last_mm)
            if self.pe.n > 0:
                self.pe.raw.wait_ge(self.pe.sem, self.pe.n)
            self._kbase = kbase
        op = self.pe.raw.matmul(out, lhsT, rhs, **kw)
        self._last_mm = op
        return op

    def tr(self, out, in_, ident, first, last, tp=None, kbase=0):
        kw = dict(start=first, stop=last, is_transpose=True)
        if tp is not None:
            kw["tile_position"] = tp
        return self.pe_mm(kbase, out, in_, ident, **kw)

    def dma(self, q, out, in_, fence=False, **kw):
        eng, ctr = {"sy": (self.sy, self.dq), "ac": (self.ac, self.dqs),
                    "gp": (self.gp, self.dqg)}[q]
        if fence:
            eng.wait(ctr, ctr.n)
        ctr.inc(eng.raw.dma_start(out=out, in_=in_, **kw))
        return ctr.n

    def dmat(self, q, out, in_):
        eng, ctr = {"sy": (self.sy, self.dq), "ac": (self.ac, self.dqs)}[q]
        ctr.inc(eng.raw.dma_start_transpose(out, in_))
        return ctr.n

    def barrier(self):
        engs = (self.pe, self.dv, self.ac, self.gp)
        for e in engs:
            for o in engs:
                if o is not e:
                    e.wait(o)
            e.wait(self.dq)
            e.wait(self.dqs)
            e.wait(self.dqg)
        for e in engs:
            self.sy.wait(e)

    def tap(self, name, src_ap, dtype=None):
        if name not in self.taps:
            return
        dtype = dtype or src_ap.dtype
        dst = self.nc.dram_tensor("tap_" + name, list(src_ap.shape), dtype,
                                  kind="ExternalOutput").ap()
        self.tap_tensors[name] = dst
        for e in (self.pe, self.dv, self.ac, self.gp):
            self.sy.wait(e)
        n = self.dma("sy", dst, src_ap, fence=True)
        for e in (self.pe, self.dv, self.ac, self.gp):
            e.wait(self.dq, n)

    def emit(self):
        ph = [0]

        def go():
            ph[0] += 1
            return ph[0] <= PHASE_LIMIT

        self.setup()
        for b in range(B_CORE):
            if go():
                self.ln_batch(b)
                self.barrier()
            for half in range(2):
                if go():
                    self.proj_half(b, half)
                    self.barrier()
                for i in range(8):
                    if go():
                        self.attn_head(b, half, i)
                self.barrier()
            if b == 0:
                self.tap(f"yfm_{b}", self.y_fm[:])
            if go():
                self.ffn_batch(b)
                self.barrier()
            if go():
                self.wc_combine(b)
                self.barrier()
        self.sy.wait(self.dq)
        self.ac.wait(self.dqs)
        self.gp.wait(self.dqg)

    def setup(self):
        sy, dv, ac, pe, gp = self.sy, self.dv, self.ac, self.pe, self.gp
        self.dma("sy", self.id_sb[:], self.c_id)
        self.dma("sy", self.tri_sb[:], self.c_tri)
        self.dma("sy", self.terr_sb[:], self.c_terr)
        self.dma("sy", self.terr0_sb[:], self.c_terr0)
        self.dma("sy", self.ones_sb[:], self.c_ones)
        self.dma("sy", self.half_sb[:], self.c_half)
        self.dma("sy", self.idf_sb[:], self.c_idf)
        self.dma("sy", self.idec_sb[:], self.idec)
        self.dma("sy", self.mega_col[:], self.mdec.unsqueeze(1))
        n_consts = self.dma("sy", self.tw_col[0:SEG, :], self.tm.unsqueeze(1))

        for src, scr, rows in ((self.Wq, self.scr_wq, D), (self.Wk, self.scr_wk, D),
                               (self.Wv, self.scr_wv, D), (self.Wc, self.scr_wc, D),
                               (self.wg, self.scr_wg, FF), (self.wu, self.scr_wu, FF),
                               (self.wd, self.scr_wd, D)):
            cols = src.shape[1]
            nch = rows // 128
            stg = self.stage[:, 0 : nch * cols]
            gp.wait(self.dq)
            n = self.dma("gp", stg.rearrange("p (c m) -> p c m", c=nch),
                         src.rearrange("(c p) m -> p c m", p=128))
            sy.wait(self.dqg, n)
            self.dma("sy", scr.rearrange("(c p) m -> p c m", p=128),
                     stg.rearrange("p (c m) -> p c m", c=nch))
        self.wb_n = self.dq.n

        dv.wait(self.dq, n_consts)
        dv.inc(dv.raw.tensor_copy(out=self.tw_col[64:128, :], in_=self.tw_col[0:SEG, :]))
        ac.wait(dv)
        ac.inc(ac.raw.activation(out=self.tw_col[:], in_=self.tw_col[:], func=AF.Sigmoid))
        dv.wait(ac)
        dv.inc(dv.raw.tensor_scalar(out=self.f2_col[:], in0=self.tw_col[:],
                                    scalar1=-1.0, scalar2=-1.0,
                                    op0=ALU.mult, op1=ALU.subtract))
        ac.inc(ac.raw.activation(out=self.eid_hs[:], in_=self.idec_sb[:], func=AF.Exp,
                                 accum_out=self.sh_col[:]))
        ac.inc(ac.raw.activation(out=self.mega_col[:], in_=self.mega_col[:],
                                 func=AF.Exp, scale=float(SEG)))
        dv.wait(ac)
        dv.inc(dv.raw.tensor_scalar_min(out=self.mega_col[:], in0=self.mega_col[:],
                                        scalar1=2.0))
        dv.inc(dv.raw.tensor_tensor_scan(
            out=self.tmp_hs[:, 0:32],
            data0=self.mega_col[:].broadcast_to([H, 32]),
            data1=self.sh_col[:].broadcast_to([H, 32]),
            initial=0.0, op0=ALU.mult, op1=ALU.add))
        dv.inc(dv.raw.memset(self.norm_hs[:, 0:1], EPS_W))
        dv.inc(dv.raw.tensor_scalar_add(out=self.norm_hs[:, 1:33],
                                        in0=self.tmp_hs[:, 0:32], scalar1=EPS_W))
        dv.inc(dv.raw.reciprocal(out=self.inv_hs[:], in_=self.norm_hs[:, 1:33]))
        dv.inc(dv.raw.tensor_tensor(out=self.d0_hs[:], in0=self.norm_hs[:, 0:32],
                                    in1=self.inv_hs[:], op=ALU.mult))
        dv.inc(dv.raw.tensor_scalar_mul(out=self.d0_hs[:], in0=self.d0_hs[:],
                                        scalar1=self.mega_col[:]))
        sy.wait(dv)
        self.dma("sy", self.scr_row[0:1, :].rearrange("o (h r) -> (o h) r", h=H),
                 self.d0_hs[:])
        self.dma("sy", self.scr_row[1:2, :].rearrange("o (h r) -> (o h) r", h=H),
                 self.inv_hs[:])
        sy.wait(self.dq)
        self.dma("sy", self.rowd0f[:], self.scr_row[0:1, :])
        n = self.dma("sy", self.rowinvf[:], self.scr_row[1:2, :])
        dv.wait(self.dq, n)
        dv.inc(dv.raw.tensor_copy(out=self.rowd0[:], in_=self.rowd0f[:]))
        dv.inc(dv.raw.tensor_copy(out=self.rowinv[:], in_=self.rowinvf[:]))
        pe.wait(dv)
        pe.inc(self.pe_mm(0, self.ps_a[:], self.ones_sb[:], self.rowd0[:],
                          start=True, stop=True))
        pe.inc(self.pe_mm(0, self.ps_b[:], self.ones_sb[:], self.rowinv[:],
                          start=True, stop=True))
        dv.wait(pe)
        dv.inc(dv.raw.tensor_copy(out=self.d0_bc[:], in_=self.ps_a[:]))
        dv.inc(dv.raw.tensor_copy(out=self.inv_bc[:], in_=self.ps_b[:]))
        dv.inc(dv.raw.memset(self.eide[:], 0.0))
        dv.inc(dv.raw.memset(self.eido[:], 0.0))
        pe.wait(dv)
        psf = self.ps_c
        pe.inc(self.tr(psf[0:64, 0:16], self.eid_hs[:], self.idf_sb[:], True, True))
        dv.wait(pe)
        dv.inc(dv.raw.tensor_copy(out=self.eide[0:64, :], in_=psf[0:64, 0:16]))
        scr16 = self.scr_row.rearrange("a (c d) -> (a c) d", c=32, d=16)[0:64, :]
        sy.wait(dv)
        n = self.dma("sy", scr16, self.eide[0:64, :], fence=True)
        sy.wait(self.dq, n)
        n = self.dma("sy", self.eido[64:128, :], scr16, fence=True)
        dv.wait(self.dq, n)
        dv.inc(dv.raw.memset(self.w_sb[:], 0.0))
        va = self.v_aug.rearrange("p (c g e) -> p c g e", c=NCH, g=8)
        va2 = self.v_aug2.rearrange("p (c g e) -> p c g e", c=NCH, g=8)
        dv.inc(dv.raw.memset(va[:, :, :, 64:65], 1.0))
        dv.inc(dv.raw.memset(va2[:, :, :, 64:65], 1.0))
        self.barrier()
        self.tap("d0_bc", self.d0_bc[:])
        self.tap("inv_bc", self.inv_bc[:])
        self.tap("eide", self.eide[:])
        self.tap("norms", self.norm_hs[:])

    def ln_batch(self, b):
        sy, dv, ac, pe = self.sy, self.dv, self.ac, self.pe
        xin = self.x_in.rearrange("p (i m) -> p i m", i=2)
        lnc = self.ln_cols
        gp = self.gp
        for lc in range(NCH):
            buf = lc % 2
            q = "sy" if buf == 0 else "gp"
            qeng = sy if buf == 0 else gp
            ctr = self.dq if buf == 0 else self.dqg
            if lc >= 2:
                qeng.wait(dv)
                qeng.wait(ac)
            n = self.dma(q, xin[:, buf, :], self.x[b, lc * 128 : lc * 128 + 128, :],
                         fence=True)
            dv.wait(ctr, n)
            ac.wait(ctr, n)
            ac.wait(pe)
            ac.inc(ac.raw.activation(out=self.ln_sq[:], in_=xin[:, buf, :],
                                     func=AF.Square, accum_out=lnc[:, 1:2]))
            dv.inc(dv.raw.reduce_sum(out=lnc[:, 0:1], in_=xin[:, buf, :], axis=AXX))
            dv.wait(ac)
            dv.inc(dv.raw.tensor_scalar_mul(out=lnc[:, 2:3], in0=lnc[:, 0:1],
                                            scalar1=1.0 / D))
            dv.inc(dv.raw.tensor_scalar_mul(out=lnc[:, 3:4], in0=lnc[:, 1:2],
                                            scalar1=1.0 / D))
            dv.inc(dv.raw.tensor_tensor(out=lnc[:, 4:5], in0=lnc[:, 2:3],
                                        in1=lnc[:, 2:3], op=ALU.mult))
            dv.inc(dv.raw.tensor_tensor(out=lnc[:, 5:6], in0=lnc[:, 3:4],
                                        in1=lnc[:, 4:5], op=ALU.subtract))
            dv.inc(dv.raw.tensor_scalar_add(out=lnc[:, 5:6], in0=lnc[:, 5:6],
                                            scalar1=EPS_LN))
            ac.wait(dv)
            ac.inc(ac.raw.activation(out=lnc[:, 6:7], in_=lnc[:, 5:6],
                                     func=AF.Sqrt))
            dv.wait(ac)
            dv.inc(dv.raw.reciprocal(out=lnc[:, 7:8], in_=lnc[:, 6:7]))
            xt = self.xt_ln
            dv.inc(dv.raw.tensor_scalar(out=xt, in0=xin[:, buf, :],
                                        scalar1=lnc[:, 2:3], scalar2=lnc[:, 7:8],
                                        op0=ALU.subtract, op1=ALU.mult))
            pe.wait(dv)
            for g2 in range(2):
                if lc >= 1 or g2 >= 1:
                    pe.wait(dv)
                last = None
                for dc4 in range(4):
                    dc = g2 * 4 + dc4
                    last = self.tr(
                        self.ps_t[:, g2 * 512 + dc4 * 128 : g2 * 512 + dc4 * 128 + 128],
                        xt[:, dc * 128 : dc * 128 + 128], self.id_sb[:],
                        dc4 == 0, dc4 == 3)
                pe.inc(last)
                dv.wait(pe)
                dst = self.xn.rearrange("p (c m) -> p c m", c=8)[
                    :, g2 * 4 : g2 * 4 + 4, lc * 128 : lc * 128 + 128]
                dv.inc(dv.raw.tensor_copy(
                    out=dst,
                    in_=self.ps_t[:, g2 * 512 : g2 * 512 + 512]
                    .rearrange("p (c m) -> p c m", c=4)))
        self.tap(f"xn_{b}", self.xn[:])

    def proj_half(self, b, half):
        sy, dv, ac, pe = self.sy, self.dv, self.ac, self.pe
        self.ac.wait(self.dq, self.wb_n)
        wload_n = []
        for j, scr in enumerate((self.scr_wq, self.scr_wk, self.scr_wv)):
            if j > 0:
                self.ac.wait(self.dqs, self.dqs.n)
            for ds in range(8):
                self.dmat("ac", self.wT[j][:, ds * 512 : ds * 512 + 512],
                          scr[half * 512 : half * 512 + 512, ds * 128 : ds * 128 + 128])
            wload_n.append(self.dqs.n)

        banks = [self.ps_strip[:, i * 512 : i * 512 + 512] for i in range(4)]
        for j, dst in ((0, self.q_fm), (1, self.k_fm)):
            pe.wait(self.dqs, wload_n[j])
            for oc in range(4):
                pe.wait(dv)
                for dc in range(8):
                    lastmm = None
                    for lt in range(4):
                        lastmm = self.pe_mm(
                            0, banks[lt],
                            self.wT[j][:, dc * 512 + oc * 128 : dc * 512 + oc * 128 + 128],
                            self.xn[:, dc * 2048 + lt * 512 : dc * 2048 + lt * 512 + 512],
                            start=(dc == 0), stop=(dc == 7))
                    if dc == 7:
                        pe.inc(lastmm)
                dv.wait(pe)
                for lt in range(4):
                    dv.inc(dv.raw.tensor_copy(
                        out=dst[:, oc * 2048 + lt * 512 : oc * 2048 + lt * 512 + 512],
                        in_=banks[lt]))
        va = self.v_aug.rearrange("p (c g e) -> p c g e", c=NCH, g=8)
        va2 = self.v_aug2.rearrange("p (c g e) -> p c g e", c=NCH, g=8)
        dv.inc(dv.raw.memset(va[:, :, :, 64:65], 1.0))
        dv.inc(dv.raw.memset(va2[:, :, :, 64:65], 1.0))
        pe.wait(self.dqs, wload_n[2])
        for lc in range(NCH):
            bank = self.ps_a if lc % 2 == 0 else self.ps_b
            pe.wait(dv)
            lastmm = None
            for dc in range(8):
                lastmm = self.pe_mm(
                    0, bank[:],
                    self.xn[:, dc * 2048 + lc * 128 : dc * 2048 + lc * 128 + 128],
                    self.wT[2][:, dc * 512 : dc * 512 + 512],
                    start=(dc == 0), stop=(dc == 7))
            pe.inc(lastmm)
            dv.wait(pe)
            src = bank.rearrange("p (g e) -> p g e", g=8)
            dv.inc(dv.raw.tensor_copy(out=va[:, lc, :, 0:64], in_=src))
            if lc >= 1:
                dv.inc(dv.raw.tensor_copy(out=va2[64:128, lc - 1, :, 0:64],
                                          in_=src[0:64, :, :]))
            if lc <= NCH - 2:
                dv.inc(dv.raw.tensor_copy(out=va2[0:64, lc, :, 0:64],
                                          in_=src[64:128, :, :]))
        self.tap(f"q_fm_{b}_{half}", self.q_fm[:])
        self.tap(f"k_fm_{b}_{half}", self.k_fm[:])
        self.tap(f"v_aug_{b}_{half}", self.v_aug[:])

    def attn_head(self, b, half, i):
        sy, dv, ac, pe, gp = self.sy, self.dv, self.ac, self.pe, self.gp
        h = half * 8 + i
        hp = i // 2
        rb = 64 * (i % 2)
        qh = lambda c0, n: self.q_fm[rb : rb + 64, hp * 2048 + c0 : hp * 2048 + c0 + n]
        kh = lambda c0, n: self.k_fm[rb : rb + 64, hp * 2048 + c0 : hp * 2048 + c0 + n]
        va = self.v_aug.rearrange("p (c g e) -> p c g e", c=NCH, g=8)
        va2 = self.v_aug2.rearrange("p (c g e) -> p c g e", c=NCH, g=8)
        idsl = self.id_sb[rb : rb + 64, rb : rb + 64]

        dv.inc(dv.raw.memset(self.acc1[:], 0.0))
        dv.inc(dv.raw.memset(self.accs[:], 0.0))
        dv.inc(dv.raw.memset(
            self.w_sb.rearrange("p (c r) -> p c r", r=33)[:, :, 0:1], 0.0))
        for g2 in range(2):
            pe.wait(dv)
            last = None
            for m4 in range(8):
                mc = g2 * 8 + m4
                last = self.tr(
                    self.ps_t[:, g2 * 512 + m4 * 64 : g2 * 512 + m4 * 64 + 64],
                    kh(mc * 128, 128), idsl, m4 == 0, m4 == 7, kbase=rb)
            pe.inc(last)
            dv.wait(pe)
            dv.inc(dv.raw.tensor_copy(out=self.ktok[:, g2 * 512 : g2 * 512 + 512],
                                      in_=self.ps_t[:, g2 * 512 : g2 * 512 + 512]))

        if SUB_LIMIT < 2:
            return
        pe.inc(self.pe_mm(rb, self.ps_c[0:64, 448:512], kh(0, 64), qh(0, 64),
                          start=True, stop=True))
        ac.wait(pe)
        ac.inc(ac.raw.activation(out=self.Et[0:64, 0:64],
                                 in_=self.ps_c[0:64, 448:512],
                                 func=AF.Exp, scale=SCALE_S))
        dv.inc(dv.raw.memset(self.Et[64:128, 0:64], 0.0))
        for rnd in range(4):
            bank = self.ps_strip[:, (rnd % 2) * 512 : (rnd % 2) * 512 + 512]
            pe.wait(dv)
            if rnd >= 2:
                pe.wait(ac)
            last = None
            first_s8 = 1 if rnd == 0 else 0
            for s8 in range(first_s8, 8):
                r = rnd * 8 + s8
                last = self.pe_mm(rb, bank[:, s8 * 64 : s8 * 64 + 64],
                                  kh(64 * (r - 1), 128), qh(64 * r, 64),
                                  start=(s8 == first_s8), stop=(s8 == 7))
            pe.inc(last)
            ac.wait(pe)
            ac.inc(ac.raw.activation(
                out=self.Et[:, rnd * 512 + first_s8 * 64 : rnd * 512 + 512],
                in_=bank[:, first_s8 * 64 : 512], func=AF.Exp, scale=SCALE_S))
        dv.wait(ac)
        dv.inc(dv.raw.tensor_tensor(out=self.Et[0:64, 0:64], in0=self.Et[0:64, 0:64],
                                    in1=self.terr0_sb[0:64, :], op=ALU.mult))
        dv.inc(dv.raw.tensor_tensor(
            out=self.Et[:, 64:2048].rearrange("p (s e) -> p s e", e=SEG),
            in0=self.Et[:, 64:2048].rearrange("p (s e) -> p s e", e=SEG),
            in1=self.terr_sb[:].unsqueeze(1).broadcast_to([128, 31, SEG]),
            op=ALU.mult))
        pe.wait(dv)

        if SUB_LIMIT < 3:
            return
        pcs_started = [False, False, False, False]
        for c in range(NCH):
            s4 = c % 4
            w = 128 * c + 128
            if c % 4 == 0 and c >= 4:
                pe.wait(dv)
            slot = self.ps_c[:, s4 * 65 : s4 * 65 + 65]
            r0, r1 = 2 * c, 2 * c + 1
            if r0 == 0:
                self.pe_mm(0, slot[0:64, :], self.Et[0:64, 0:64],
                           va[0:64, 0, i, :], start=True, stop=True)
            else:
                self.pe_mm(0, slot[0:64, :], self.Et[:, r0 * 64 : r0 * 64 + 64],
                           va2[:, c - 1, i, :], start=True, stop=True)
            mm1 = self.pe_mm(0, slot[64:128, :], self.Et[:, r1 * 64 : r1 * 64 + 64],
                             va[:, c, i, :], start=True, stop=True,
                             tile_position=(0, 64))
            pe.inc(mm1)
            pe.wait(ac)
            pe.wait(dv)
            last = None
            for t in range((w + 511) // 512):
                n = min(512, w - 512 * t)
                last = self.pe_mm(rb, self.ps_strip[:, 512 * t : 512 * t + n],
                                  qh(c * 128, 128), kh(512 * t, n),
                                  start=True, stop=True)
            pe.inc(last)

            ac.wait(pe)
            e0 = s4 * 2048
            if c >= 1:
                ac.inc(ac.raw.activation(
                    out=self.E4[:, e0 : e0 + w - 192],
                    in_=self.ps_strip[:, 0 : w - 192],
                    func=AF.Exp, scale=SCALE_S, accum_out=self.acc1[:, c : c + 1]))
                ac.inc(ac.raw.activation(
                    out=self.E4[:, e0 + w - 192 : e0 + w - 128],
                    in_=self.ps_strip[:, w - 192 : w - 128],
                    func=AF.Exp, scale=SCALE_S, accum_out=self.accs[:, c : c + 1]))
            ac.inc(ac.raw.activation(
                out=self.E4[:, e0 + w - 128 : e0 + w],
                in_=self.ps_strip[:, w - 128 : w], func=AF.Exp, scale=SCALE_S))
            dv.wait(ac)
            dv.inc(dv.raw.tensor_tensor(
                out=self.E4[:, e0 + w - 128 : e0 + w],
                in0=self.E4[:, e0 + w - 128 : e0 + w],
                in1=self.tri_sb[:], op=ALU.mult))

            if c % 4 == 3:
                g = c // 4
                cs = slice(4 * g, 4 * g + 4)
                dv.wait(pe)
                dv.inc(dv.raw.scalar_tensor_tensor(
                    out=self.rs4[:, cs], in0=self.accs[:, cs],
                    scalar=self.half_sb[:], in1=self.acc1[:, cs],
                    op0=ALU.mult, op1=ALU.add))
                otr = self.ps_c[:, 0:260].rearrange("p (c e) -> p c e", e=65)[:, :, 64:65]
                dv.inc(dv.raw.tensor_tensor(
                    out=self.rs4[:, cs].unsqueeze(2),
                    in0=self.rs4[:, cs].unsqueeze(2), in1=otr, op=ALU.add))
                dv.inc(dv.raw.reciprocal(out=self.rcp[:, cs], in_=self.rs4[:, cs]))
                pcs_banks = [self.ps_a, self.ps_b, self.ps_t.bitcast(F32),
                             self.ps_strip[:, 0:512]]
                Cv = self.C_sb.rearrange("p (r two) -> p r two", two=2)
                lastmm = None
                for cc in range(4 * g + 3, 4 * g - 1, -1):
                    if cc != 4 * g + 3:
                        dv.wait(pe)
                    dv.inc(dv.raw.memset(self.C_sb[:], 0.0))
                    dv.inc(dv.raw.scalar_tensor_tensor(
                        out=Cv[:, cc : cc + 1, 0:1],
                        in0=self.rcp[:, cc : cc + 1].unsqueeze(2),
                        scalar=self.eide[:, h : h + 1],
                        in1=self.inv_bc[:, h * 32 + 2 * cc : h * 32 + 2 * cc + 1]
                            .unsqueeze(2),
                        op0=ALU.mult, op1=ALU.mult))
                    dv.inc(dv.raw.scalar_tensor_tensor(
                        out=Cv[:, cc : cc + 1, 1:2],
                        in0=self.rcp[:, cc : cc + 1].unsqueeze(2),
                        scalar=self.eido[:, h : h + 1],
                        in1=self.inv_bc[:, h * 32 + 2 * cc + 1 : h * 32 + 2 * cc + 2]
                            .unsqueeze(2),
                        op0=ALU.mult, op1=ALU.mult))
                    pe.wait(dv)
                    wcc = 128 * cc + 128
                    ntt = (wcc + 511) // 512
                    for t in range(ntt):
                        n = min(512, wcc - 512 * t)
                        bank = pcs_banks[t]
                        st = not pcs_started[t]
                        pcs_started[t] = True
                        stop = (cc == 4 * g) and (g == 3)
                        mm = self.pe_mm(
                            0, bank[64 * (t % 2) : 64 * (t % 2) + 32, 0:n],
                            self.C_sb[:],
                            self.E4[:, (cc % 4) * 2048 + 512 * t :
                                    (cc % 4) * 2048 + 512 * t + n],
                            start=st, stop=stop, tile_position=(0, 64 * (t % 2)))
                        lastmm = mm
                    pe.inc(lastmm)
                dv.wait(pe)
                dv.inc(dv.raw.tensor_copy(
                    out=self.ot_sb[:, 4 * g * 65 : 4 * g * 65 + 260],
                    in_=self.ps_c[:, 0:260]))
        if SUB_LIMIT < 4:
            return
        self.post_head(b, half, i, h, hp, rb, qh, va)

    def post_head(self, b, half, i, h, hp, rb, qh, va):
        sy, dv, ac, pe, gp = self.sy, self.dv, self.ac, self.pe, self.gp
        dv.wait(pe)
        pcs_banks = [self.ps_a, self.ps_b, self.ps_t.bitcast(F32),
                     self.ps_strip[:, 0:512]]
        for t in range(4):
            rbt = 64 * (t % 2)
            dv.inc(dv.raw.tensor_copy(
                out=self.pcs_sb[rbt : rbt + 32, (t // 2) * 512 : (t // 2) * 512 + 512],
                in_=pcs_banks[t][rbt : rbt + 32, 0:512]))
        self.tap(f"pcsA_{b}_{h}", self.pcs_sb[0:32, :])
        self.tap(f"pcsB_{b}_{h}", self.pcs_sb[64:96, :])
        if SUB_LIMIT == 40:
            return
        pe.wait(dv)
        last = None
        pst_dst = self.ps_t.bitcast(F32)
        last = None
        for t in (0, 2, 1, 3):
          for mcl in range(4):
            mc = t * 4 + mcl
            base = 64 * (t % 2)
            blk = self.pcs_sb[base : base + 32,
                              (t // 2) * 512 + 128 * mcl : (t // 2) * 512 + 128 * mcl + 128]
            last = self.pe_mm(base, pst_dst[:, mc * 32 : mc * 32 + 32], blk,
                              self.id_sb[base : base + 32, base : base + 32],
                              start=True, stop=True)
        pe.inc(last)
        if SUB_LIMIT == 41:
            return
        dv.wait(pe)
        for c in range(16):
            dv.inc(dv.raw.tensor_tensor_scan(
                out=self.w_sb.rearrange("p (c r) -> p c r", r=33)[:, c, 1:33],
                data0=self.d0_bc[:, h * 32 : h * 32 + 32],
                data1=pst_dst[:, c * 32 : c * 32 + 32],
                initial=0.0, op0=ALU.mult, op1=ALU.add))
        self.tap(f"w_{b}_{h}", self.w_sb[:])
        if SUB_LIMIT < 5:
            return
        pe.wait(ac)
        wv = self.w_sb.rearrange("p (c r) -> p c r", r=33)
        bufs = (self.vw, self.vw2)
        kv_mm_n = []
        kk = 0
        for j in range(4):
            cmax = min(NCH - 1, 4 * j + 3)
            for C in range(cmax + 1):
                buf = bufs[kk % 2]
                if kk >= 2:
                    dv.wait(pe, kv_mm_n[kk - 2])
                dv.inc(dv.raw.tensor_tensor(
                    out=buf.rearrange("p (r e) -> p r e", e=64),
                    in0=wv[:, C, 8 * j : 8 * j + 8].unsqueeze(2)
                        .broadcast_to([128, 8, 64]),
                    in1=va[:, C, i, 0:64].unsqueeze(1).broadcast_to([128, 8, 64]),
                    op=ALU.mult))
                pe.wait(dv)
                mm = self.pe_mm(
                    0, self.ps_strip[rb : rb + 64, j * 512 : j * 512 + 512],
                    self.ktok[:, C * 64 : C * 64 + 64], buf[:],
                    start=(C == 0), stop=(C == cmax), tile_position=(0, rb))
                pe.inc(mm)
                kv_mm_n.append(pe.n)
                kk += 1
        dv.wait(pe)
        for j in range(4):
            dv.inc(dv.raw.tensor_copy(
                out=self.kv_sb[rb : rb + 64, j * 512 : j * 512 + 512],
                in_=self.ps_strip[rb : rb + 64, j * 512 : j * 512 + 512]))
        self.tap(f"kv_{b}_{h}", self.kv_sb[rb : rb + 64, :])
        if SUB_LIMIT < 6:
            return
        pe.wait(dv)
        for r in range(R):
            cch = r // 2
            bank = self.ps_a if cch < 8 else self.ps_b
            mm = self.pe_mm(
                rb, bank[64 * (r % 2) : 64 * (r % 2) + 64,
                         (cch % 8) * 64 : (cch % 8) * 64 + 64],
                qh(r * 64, 64), self.kv_sb[rb : rb + 64, r * 64 : r * 64 + 64],
                start=True, stop=True,
                tile_position=(rb, 64 * (r % 2)))
            if r >= R - 2:
                pe.inc(mm)
        dv.wait(pe)
        otv = self.ot_sb.rearrange("p (c e) -> p c e", e=65)
        dv.inc(dv.raw.reciprocal(out=self.rst[:], in_=otv[:, :, 64]))
        dv.inc(dv.raw.tensor_tensor(out=self.Ffac[:], in0=self.rst[:],
                                    in1=self.tw_col[:].broadcast_to([128, NCH]),
                                    op=ALU.mult))
        for grp in range(2):
            bank = self.ps_a if grp == 0 else self.ps_b
            dv.inc(dv.raw.tensor_tensor(
                out=self.t1_sb.rearrange("p (c e) -> p c e", e=64),
                in0=otv[:, 8 * grp : 8 * grp + 8, 0:64],
                in1=self.Ffac[:, 8 * grp : 8 * grp + 8].unsqueeze(2)
                    .broadcast_to([128, 8, 64]),
                op=ALU.mult))
            dv.inc(dv.raw.scalar_tensor_tensor(
                out=self.y_tok[:, grp * 512 : grp * 512 + 512],
                in0=bank[:], scalar=self.f2_col[:], in1=self.t1_sb[:],
                op0=ALU.mult, op1=ALU.add))
        self.tap(f"ytok_{b}_{h}", self.y_tok[:])
        if SUB_LIMIT < 7:
            return
        jc = h // 2
        pstf = self.ps_t.bitcast(F32)
        for rnd in range(4):
            pe.wait(dv)
            last = None
            for s in range(4):
                slot = rnd * 4 + s
                last = self.pe_mm(
                    0, pstf[rb : rb + 64, s * 128 : s * 128 + 128],
                    self.y_tok[:, slot * 64 : slot * 64 + 64],
                    self.id_sb[:], start=(s == 0), stop=(s == 3),
                    tile_position=(0, rb))
            pe.inc(last)
            dv.wait(pe)
            dv.inc(dv.raw.tensor_copy(
                out=self.y_fm[rb : rb + 64,
                              jc * 2048 + rnd * 512 : jc * 2048 + rnd * 512 + 512],
                in_=pstf[rb : rb + 64, 0:512]))

    def ffn_batch(self, b):
        sy, dv, ac, pe = self.sy, self.dv, self.ac, self.pe
        dv.inc(dv.raw.tensor_scalar_mul(out=self.xn[:, 0:8192],
                                        in0=self.xn[:, 0:8192], scalar1=INV127))
        dv.inc(dv.raw.tensor_scalar_mul(out=self.xn[:, 8192:16384],
                                        in0=self.xn[:, 8192:16384], scalar1=INV127))
        pe.wait(dv)
        banks8 = [self.ps_strip[:, i * 512 : i * 512 + 512] for i in range(4)] + \
                 [self.ps_a, self.ps_b, self.ps_c, self.ps_t.bitcast(F32)]
        for lt in range(2):
            for qtr in range(4):
                ac.wait(pe)
                for ds in range(8):
                    self.dmat("ac", self.wgT_buf[:, ds * 768 : ds * 768 + 768],
                              self.scr_wg[qtr * 768 : qtr * 768 + 768,
                                          ds * 128 : ds * 128 + 128])
                    self.dmat("ac", self.wuT_buf[:, ds * 768 : ds * 768 + 768],
                              self.scr_wu[qtr * 768 : qtr * 768 + 768,
                                          ds * 128 : ds * 128 + 128])
                pe.wait(self.dqs)
                for hc6 in range(6):
                    hc = qtr * 6 + hc6
                    pe.wait(ac)
                    pe.wait(dv)
                    lastmm = None
                    for dc in range(8):
                        for s in range(2):
                            lastmm = self.pe_mm(
                                0, self.ps_strip[:, s * 512 : s * 512 + 512],
                                self.wgT_buf[:, dc * 768 + hc6 * 128 :
                                             dc * 768 + hc6 * 128 + 128],
                                self.xn[:, dc * 2048 + lt * 1024 + s * 512 :
                                        dc * 2048 + lt * 1024 + s * 512 + 512],
                                start=(dc == 0), stop=(dc == 7))
                            lastmm = self.pe_mm(
                                0, self.ps_strip[:, 1024 + s * 512 : 1024 + s * 512 + 512],
                                self.wuT_buf[:, dc * 768 + hc6 * 128 :
                                             dc * 768 + hc6 * 128 + 128],
                                self.xn[:, dc * 2048 + lt * 1024 + s * 512 :
                                        dc * 2048 + lt * 1024 + s * 512 + 512],
                                start=(dc == 0), stop=(dc == 7))
                    pe.inc(lastmm)
                    ac.wait(pe)
                    ac.inc(ac.raw.activation(out=self.sgu[:, 0:1024],
                                             in_=self.ps_strip[:, 0:1024],
                                             func=AF.Silu))
                    dv.wait(pe)
                    dv.inc(dv.raw.tensor_copy(out=self.sgu[:, 1024:2048],
                                              in_=self.ps_strip[:, 1024:2048]))
                    dv.wait(ac)
                    dv.inc(dv.raw.tensor_tensor(
                        out=self.h_sb[:, hc * 1024 : hc * 1024 + 1024],
                        in0=self.sgu[:, 0:1024], in1=self.sgu[:, 1024:2048],
                        op=ALU.mult))
            for ot in range(2):
                dn_mm_n = []
                for hc in range(FF // 128):
                    bufsel = (hc % 2) * 1024
                    if hc >= 2:
                        ac.wait(pe, dn_mm_n[hc - 2])
                    if hc >= 1:
                        ac.wait(self.dqs, self.dqs.n)
                    n = self.dmat("ac", self.wdT_buf[:, bufsel : bufsel + 512],
                                  self.scr_wd[ot * 512 : ot * 512 + 512,
                                              hc * 128 : hc * 128 + 128])
                    pe.wait(self.dqs, n)
                    if hc == 0:
                        pe.wait(ac)
                        pe.wait(dv)
                    lastmm = None
                    for lc in range(8):
                        lastmm = self.pe_mm(
                            0, banks8[lc][:],
                            self.h_sb[:, hc * 1024 + lc * 128 : hc * 1024 + lc * 128 + 128],
                            self.wdT_buf[:, bufsel : bufsel + 512],
                            start=(hc == 0), stop=(hc == FF // 128 - 1))
                    pe.inc(lastmm)
                    dn_mm_n.append(pe.n)
                ac.wait(pe)
                lastact = None
                for lc in range(8):
                    lastact = ac.raw.activation(
                        out=self.g_tok[:, (lt * 8 + lc) * 1024 + ot * 512 :
                                       (lt * 8 + lc) * 1024 + ot * 512 + 512],
                        in_=banks8[lc][:], func=AF.Silu, scale=INV127)
                    ac.inc(lastact)
        self.tap(f"g_{b}", self.g_tok[:])

    def wc_combine(self, b):
        sy, dv, ac, pe = self.sy, self.dv, self.ac, self.pe
        for ot in range(2):
            for ds in range(8):
                self.dmat("ac", self.wcT_buf[:, ot * 4096 + ds * 512 :
                                             ot * 4096 + ds * 512 + 512],
                          self.scr_wc[ot * 512 : ot * 512 + 512,
                                      ds * 128 : ds * 128 + 128])
        pe.wait(self.dqs)
        xin = self.x_in2.rearrange("p (i m) -> p i m", i=2)
        xout = self.x_out.rearrange("p (i m) -> p i m", i=2)
        gp = self.gp
        out_dma_n = []
        for lc in range(NCH):
            buf = lc % 2
            q = "sy" if buf == 0 else "gp"
            qeng = sy if buf == 0 else gp
            ctr = self.dq if buf == 0 else self.dqg
            if lc >= 2:
                qeng.wait(dv)
                dv.wait(self.dq, out_dma_n[lc - 2])
            n = self.dma(q, xin[:, buf, :], self.x[b, lc * 128 : lc * 128 + 128, :],
                         fence=True)
            for ot in range(2):
                bank = self.ps_a if ot == 0 else self.ps_b
                pe.wait(dv)
                lastmm = None
                for jcc in range(8):
                    lastmm = self.pe_mm(
                        0, bank[:],
                        self.y_fm[:, jcc * 2048 + lc * 128 : jcc * 2048 + lc * 128 + 128],
                        self.wcT_buf[:, ot * 4096 + jcc * 512 : ot * 4096 + jcc * 512 + 512],
                        start=(jcc == 0), stop=(jcc == 7))
                pe.inc(lastmm)
                dv.wait(pe)
                dv.inc(dv.raw.tensor_tensor(
                    out=self.t1_sb[:], in0=bank[:],
                    in1=self.g_tok[:, lc * 1024 + ot * 512 : lc * 1024 + ot * 512 + 512],
                    op=ALU.mult))
                dv.wait(ctr, n)
                dv.inc(dv.raw.tensor_tensor(
                    out=xout[:, buf, ot * 512 : ot * 512 + 512],
                    in0=self.t1_sb[:],
                    in1=xin[:, buf, ot * 512 : ot * 512 + 512], op=ALU.add))
            sy.wait(dv)
            out_dma_n.append(
                self.dma("sy", self.out[b, lc * 128 : lc * 128 + 128, :],
                         xout[:, buf, :], fence=True))


def _consts():
    i = np.arange(128)
    j = np.arange(128)
    js = np.arange(SEG)
    c_tri = (j[None, :] <= i[:, None]).astype(bf)
    c_terr = np.zeros((128, SEG), dtype=np.float32)
    c_terr[0:64, :] = 1.0
    c_terr[64:128, :] = (i[64:128, None] - 64 <= js[None, :]).astype(np.float32)
    c_terr0 = np.zeros((128, SEG), dtype=np.float32)
    c_terr0[0:64, :] = (i[0:64, None] <= js[None, :]).astype(np.float32)
    c_half = np.zeros((128, 1), dtype=np.float32)
    c_half[64:128] = 1.0
    return dict(c_id=np.eye(128, dtype=bf), c_idf=np.eye(16, dtype=np.float32),
                c_tri=c_tri, c_terr=c_terr.astype(bf), c_terr0=c_terr0.astype(bf),
                c_ones=np.ones((1, 128), dtype=bf), c_half=c_half)


def kernel(L0_attn_hidden_states, L0_attn_Wq, L0_attn_Wk, L0_attn_Wv, L0_attn_Wc,
           L0_attn_terrace_mix, L0_attn_index_decays, L0_attn_mega_decays,
           L0_ffn_w_gate, L0_ffn_w_up, L0_ffn_w_down):
    x = np.ascontiguousarray(L0_attn_hidden_states)
    k = K(taps=DEBUG_TAPS)
    nc = k.build()
    base = dict(Wq=L0_attn_Wq, Wk=L0_attn_Wk, Wv=L0_attn_Wv, Wc=L0_attn_Wc,
                wg=L0_ffn_w_gate, wu=L0_ffn_w_up, wd=L0_ffn_w_down,
                tm=L0_attn_terrace_mix, idec=L0_attn_index_decays,
                mdec=L0_attn_mega_decays, **_consts())
    in_maps = [dict(base, x=x[2 * c : 2 * c + 2]) for c in range(8)]
    res = run_bass_kernel_spmd(nc, in_maps, core_ids=list(range(8)))
    if DEBUG_TAPS:
        kernel.tap_results = [
            {n: rr.get("tap_" + n) for n in DEBUG_TAPS} for rr in
            [r for r in res.results]]
    return np.concatenate([r["out"] for r in res.results], axis=0)


# version 07
# speedup vs baseline: 1.2563x; 1.0218x over previous
import numpy as np
import ml_dtypes
import concourse.bass as bass
import concourse.mybir as mybir
from concourse.bass_utils import run_bass_kernel_spmd

F32 = mybir.dt.float32
BF16 = mybir.dt.bfloat16
I32 = mybir.dt.int32
U8 = mybir.dt.uint8
AF = mybir.ActivationFunctionType
ALU = mybir.AluOpType
AXX = mybir.AxisListType.X
bf = ml_dtypes.bfloat16

B_CORE, L, D, H, DH, SEG, R, NCH, FF = 2, 2048, 1024, 16, 64, 64, 32, 16, 3072
EPS_LN = 1e-5
EPS_W = 1e-5
INV127 = 1.0 / 127.0
SCALE_S = 0.125

DEBUG_TAPS = ()
PHASE_LIMIT = 10 ** 9
SUB_LIMIT = 10 ** 9
NPCT = 16


class _RecOp:

    __slots__ = ("name", "args", "kw", "incs")

    def __init__(self, name, args, kw):
        self.name = name
        self.args = args
        self.kw = kw
        self.incs = []


_NO_SELF_SYNC = {"wait_ge", "dma_start", "dma_start_transpose"}


class _Recorder:
    def __init__(self, eng):
        self._eng = eng

    def __getattr__(self, name):
        eng = self._eng

        def call(*a, **kw):
            if eng.self_sync and name not in _NO_SELF_SYNC:
                if eng.n > 0 and eng.waited.get(eng.sem.num, -1) < eng.n:
                    eng.q.append(_RecOp("wait_ge", (eng.sem, eng.n), {}))
                    eng.waited[eng.sem.num] = eng.n
                op = _RecOp(name, a, kw)
                eng.q.append(op)
                op.incs.append((eng.sem, eng.step))
                eng.n += eng.step
            else:
                op = _RecOp(name, a, kw)
                eng.q.append(op)
            return op

        return call


class Eng:

    def __init__(self, nc, name, dma_step=1, self_sync=False):
        self.sem = nc.alloc_semaphore("sem_" + name)
        self.n = 0
        self.step = dma_step
        self.waited = {}
        self.q = []
        self.self_sync = self_sync
        self.raw = _Recorder(self)

    def wait(self, other, val=None):
        if other is self:
            return
        val = other.n if val is None else val
        if val <= 0 or self.waited.get(other.sem.num, -1) >= val:
            return
        self.raw.wait_ge(other.sem, val)
        self.waited[other.sem.num] = val

    def inc(self, op):
        if (self.sem, self.step) in op.incs:
            return self.n
        op.incs.append((self.sem, self.step))
        self.n += self.step
        return self.n

    def replay(self, raw):
        for op in self.q:
            inst = getattr(raw, op.name)(*op.args, **op.kw)
            for sem, v in op.incs:
                inst.then_inc(sem, v)


class K:
    def __init__(self, taps=()):
        self.taps = set(taps)
        self.nc = bass.Bass("TRN2", target_bir_lowering=False)
        self.tap_tensors = {}
        self._declare_io()
        self._alloc_sbuf()
        self._alloc_psum()

    def _declare_io(self):
        nc = self.nc
        t = lambda n, s, d, k: nc.dram_tensor(n, s, d, kind=k).ap()
        self.x = t("x", [B_CORE, L, D], F32, "ExternalInput")
        self.Wq = t("Wq", [D, D], F32, "ExternalInput")
        self.Wk = t("Wk", [D, D], F32, "ExternalInput")
        self.Wv = t("Wv", [D, D], F32, "ExternalInput")
        self.Wc = t("Wc", [D, D], F32, "ExternalInput")
        self.wg = t("wg", [FF, D], I32, "ExternalInput")
        self.wu = t("wu", [FF, D], I32, "ExternalInput")
        self.wd = t("wd", [D, FF], I32, "ExternalInput")
        self.tm = t("tm", [SEG], F32, "ExternalInput")
        self.idec = t("idec", [H, SEG], F32, "ExternalInput")
        self.mdec = t("mdec", [H], F32, "ExternalInput")
        self.c_id = t("c_id", [128, 128], BF16, "ExternalInput")
        self.c_idf = t("c_idf", [16, 16], F32, "ExternalInput")
        self.c_tri = t("c_tri", [128, 128], BF16, "ExternalInput")
        self.c_terr = t("c_terr", [128, SEG], BF16, "ExternalInput")
        self.c_terr0 = t("c_terr0", [128, SEG], BF16, "ExternalInput")
        self.c_ones = t("c_ones", [1, 128], BF16, "ExternalInput")
        self.c_half = t("c_half", [128, 1], F32, "ExternalInput")
        self.out = t("out", [B_CORE, L, D], F32, "ExternalOutput")
        self.scr_wq = t("scr_wq", [D, D], BF16, "Internal")
        self.scr_wk = t("scr_wk", [D, D], BF16, "Internal")
        self.scr_wv = t("scr_wv", [D, D], BF16, "Internal")
        self.scr_wc = t("scr_wc", [D, D], BF16, "Internal")
        self.scr_wg = t("scr_wg", [FF, D], BF16, "Internal")
        self.scr_wu = t("scr_wu", [FF, D], BF16, "Internal")
        self.scr_wd = t("scr_wd", [D, FF], BF16, "Internal")
        self.scr_row = t("scr_row", [2, 512], F32, "Internal")

    def _alloc_sbuf(self):
        nc = self.nc
        A = lambda n, s, d: nc.alloc_sbuf_tensor(n, s, d).ap()
        self.id_sb = A("id_sb", [128, 128], BF16)
        self.tri_sb = A("tri_sb", [128, 128], BF16)
        self.terr_sb = A("terr_sb", [128, SEG], BF16)
        self.terr0_sb = A("terr0_sb", [128, SEG], BF16)
        self.ones_sb = A("ones_sb", [1, 128], BF16)
        self.half_sb = A("half_sb", [128, 1], F32)
        self.idf_sb = A("idf_sb", [16, 16], F32)
        self.tw_col = A("tw_col", [128, 1], F32)
        self.f2_col = A("f2_col", [128, 1], F32)
        self.eide = A("eide", [128, H], F32)
        self.eido = A("eido", [128, H], F32)
        self.d0_bc = A("d0_bc", [128, 512], BF16)
        self.inv_bc = A("inv_bc", [128, 512], BF16)
        self.idec_sb = A("idec_sb", [H, SEG], F32)
        self.eid_hs = A("eid_hs", [H, SEG], F32)
        self.sh_col = A("sh_col", [H, 1], F32)
        self.mega_col = A("mega_col", [H, 1], F32)
        self.norm_hs = A("norm_hs", [H, 33], F32)
        self.d0_hs = A("d0_hs", [H, 32], F32)
        self.inv_hs = A("inv_hs", [H, 32], F32)
        self.tmp_hs = A("tmp_hs", [H, 33], F32)
        self.rowd0f = A("rowd0f", [1, 512], F32)
        self.rowinvf = A("rowinvf", [1, 512], F32)
        self.rowd0 = A("rowd0", [1, 512], BF16)
        self.rowinv = A("rowinv", [1, 512], BF16)
        self.C_sb = A("C_sb", [128, 32], BF16)
        self.acc1 = A("acc1", [128, NCH], F32)
        self.accs = A("accs", [128, NCH], F32)
        self.rs4 = A("rs4", [128, NCH], F32)
        self.rcp = A("rcp", [128, NCH], F32)
        self.rst = A("rst", [128, NCH], F32)
        self.Ffac = A("Ffac", [128, NCH], F32)
        self.ln_cols = A("ln_cols", [128, 8], F32)
        self.t1_sb = A("t1_sb", [128, 512], F32)
        self.xn = A("xn", [128, 8 * L], BF16)
        self.y_fm = A("y_fm", [128, 8 * L], BF16)
        self.g_tok = A("g_tok", [128, NCH * D], BF16)
        self.pool_q = A("pool_q", [128, 66048], U8)
        self.pool_a = A("pool_a", [128, 33856], U8)

        qb = self.pool_q.bitcast(BF16)
        qf = self.pool_q.bitcast(F32)
        ab = self.pool_a.bitcast(BF16)
        af = self.pool_a.bitcast(F32)
        self.stage = qb[:, 0:24576]
        self.q_fm = qb[:, 0:8192]
        self.k_fm = qb[:, 8192:16384]
        self.v_aug = qb[:, 16384:24704]
        self.v_aug2 = qb[:, 24704:33024]
        self.h_sb = qb[:, 0:24576]
        self.sgu = qb[:, 24576:26624]
        self.wdT_buf = qb[:, 26624:28672]
        self.wcT_buf = qb[:, 0:8192]
        self.x_out = qf[:, 4096:6144]
        self.x_in2 = qf[:, 6144:8192]
        self.x_in = af[:, 0:2048]
        self.ln_sq = af[:, 2048:3072]
        self.xt_ln = ab[:, 4096:5120]
        self.wT = [ab[:, 0:4096], ab[:, 4096:8192], ab[:, 8192:12288]]
        self.E4 = ab[:, 0:8192]
        self.Et = ab[:, 8192:10240]
        self.ktok = ab[:, 10240:11264]
        self.w_sb = ab[:, 11264:11792]
        self.vw = ab[:, 11792:12304]
        self.vw2 = ab[:, 12304:12816]
        self.pcs_sb = ab[:, 12816:13840]
        self.kv_sb = ab[:, 13840:15888]
        self.ot_sb = ab[:, 15888:16928]
        self.y_tok = ab[:, 0:1024]
        self.wgT_buf = ab[:, 0:6144]
        self.wuT_buf = ab[:, 6144:12288]

    def _alloc_psum(self):
        nc = self.nc
        P = lambda n, s, d: nc.alloc_psum_tensor(n, s, d).ap()
        self.ps_strip = P("ps_strip", [128, 2048], F32)
        self.ps_a = P("ps_a", [128, 512], F32)
        self.ps_b = P("ps_b", [128, 512], F32)
        self.ps_c = P("ps_c", [128, 512], F32)
        self.ps_t = P("ps_t", [128, 1024], BF16)

    def build(self):
        nc = self.nc
        self._kbase = -1
        self._last_mm = None
        self.pe = Eng(nc, "pe")
        self.dv = Eng(nc, "dv", self_sync=True)
        self.ac = Eng(nc, "ac", self_sync=True)
        self.gp = Eng(nc, "gp", self_sync=True)
        self.sy = Eng(nc, "sy")
        self.dq = Eng(nc, "dq", dma_step=16)
        self.dqs = Eng(nc, "dqs", dma_step=16)
        self.dqg = Eng(nc, "dqg", dma_step=16)
        self.emit()
        with nc.Block() as blk:
            @blk.tensor
            def _(t):
                self.pe.replay(t)

            @blk.vector
            def _(v):
                self.dv.replay(v)

            @blk.scalar
            def _(s):
                self.ac.replay(s)

            @blk.gpsimd
            def _(g):
                self.gp.replay(g)

            @blk.sync
            def _(sy):
                self.sy.replay(sy)
        return nc

    def pe_mm(self, kbase, out, lhsT, rhs, **kw):
        if kbase != self._kbase:
            if self._last_mm is not None:
                self.pe.inc(self._last_mm)
            if self.pe.n > 0:
                self.pe.raw.wait_ge(self.pe.sem, self.pe.n)
            self._kbase = kbase
        op = self.pe.raw.matmul(out, lhsT, rhs, **kw)
        self._last_mm = op
        return op

    def tr(self, out, in_, ident, first, last, tp=None, kbase=0):
        kw = dict(start=first, stop=last, is_transpose=True)
        if tp is not None:
            kw["tile_position"] = tp
        return self.pe_mm(kbase, out, in_, ident, **kw)

    def dma(self, q, out, in_, fence=False, **kw):
        eng, ctr = {"sy": (self.sy, self.dq), "ac": (self.ac, self.dqs),
                    "gp": (self.gp, self.dqg)}[q]
        if fence:
            eng.wait(ctr, ctr.n)
        ctr.inc(eng.raw.dma_start(out=out, in_=in_, **kw))
        return ctr.n

    def dmat(self, q, out, in_):
        eng, ctr = {"sy": (self.sy, self.dq), "ac": (self.ac, self.dqs)}[q]
        ctr.inc(eng.raw.dma_start_transpose(out, in_))
        return ctr.n

    def barrier(self):
        engs = (self.pe, self.dv, self.ac, self.gp)
        for e in engs:
            for o in engs:
                if o is not e:
                    e.wait(o)
            e.wait(self.dq)
            e.wait(self.dqs)
            e.wait(self.dqg)
        for e in engs:
            self.sy.wait(e)

    def tap(self, name, src_ap, dtype=None):
        if name not in self.taps:
            return
        dtype = dtype or src_ap.dtype
        dst = self.nc.dram_tensor("tap_" + name, list(src_ap.shape), dtype,
                                  kind="ExternalOutput").ap()
        self.tap_tensors[name] = dst
        for e in (self.pe, self.dv, self.ac, self.gp):
            self.sy.wait(e)
        n = self.dma("sy", dst, src_ap, fence=True)
        for e in (self.pe, self.dv, self.ac, self.gp):
            e.wait(self.dq, n)

    def emit(self):
        ph = [0]

        def go():
            ph[0] += 1
            return ph[0] <= PHASE_LIMIT

        self.setup()
        for b in range(B_CORE):
            if go():
                self.ln_batch(b)
                self.barrier()
            for half in range(2):
                if go():
                    self.proj_half(b, half)
                    self.barrier()
                for i in range(8):
                    if go():
                        self.attn_head(b, half, i)
                self.barrier()
            if b == 0:
                self.tap(f"yfm_{b}", self.y_fm[:])
            if go():
                self.ffn_batch(b)
                self.barrier()
            if go():
                self.wc_combine(b)
                self.barrier()
        self.sy.wait(self.dq)
        self.ac.wait(self.dqs)
        self.gp.wait(self.dqg)

    def setup(self):
        sy, dv, ac, pe, gp = self.sy, self.dv, self.ac, self.pe, self.gp
        self.dma("sy", self.id_sb[:], self.c_id)
        self.dma("sy", self.tri_sb[:], self.c_tri)
        self.dma("sy", self.terr_sb[:], self.c_terr)
        self.dma("sy", self.terr0_sb[:], self.c_terr0)
        self.dma("sy", self.ones_sb[:], self.c_ones)
        self.dma("sy", self.half_sb[:], self.c_half)
        self.dma("sy", self.idf_sb[:], self.c_idf)
        self.dma("sy", self.idec_sb[:], self.idec)
        self.dma("sy", self.mega_col[:], self.mdec.unsqueeze(1))
        n_consts = self.dma("sy", self.tw_col[0:SEG, :], self.tm.unsqueeze(1))

        for src, scr, rows in ((self.Wq, self.scr_wq, D), (self.Wk, self.scr_wk, D),
                               (self.Wv, self.scr_wv, D), (self.Wc, self.scr_wc, D),
                               (self.wg, self.scr_wg, FF), (self.wu, self.scr_wu, FF),
                               (self.wd, self.scr_wd, D)):
            cols = src.shape[1]
            nch = rows // 128
            stg = self.stage[:, 0 : nch * cols]
            gp.wait(self.dq)
            n = self.dma("gp", stg.rearrange("p (c m) -> p c m", c=nch),
                         src.rearrange("(c p) m -> p c m", p=128))
            sy.wait(self.dqg, n)
            self.dma("sy", scr.rearrange("(c p) m -> p c m", p=128),
                     stg.rearrange("p (c m) -> p c m", c=nch))
        self.wb_n = self.dq.n

        dv.wait(self.dq, n_consts)
        dv.inc(dv.raw.tensor_copy(out=self.tw_col[64:128, :], in_=self.tw_col[0:SEG, :]))
        ac.wait(dv)
        ac.inc(ac.raw.activation(out=self.tw_col[:], in_=self.tw_col[:], func=AF.Sigmoid))
        dv.wait(ac)
        dv.inc(dv.raw.tensor_scalar(out=self.f2_col[:], in0=self.tw_col[:],
                                    scalar1=-1.0, scalar2=-1.0,
                                    op0=ALU.mult, op1=ALU.subtract))
        ac.inc(ac.raw.activation(out=self.eid_hs[:], in_=self.idec_sb[:], func=AF.Exp,
                                 accum_out=self.sh_col[:]))
        ac.inc(ac.raw.activation(out=self.mega_col[:], in_=self.mega_col[:],
                                 func=AF.Exp, scale=float(SEG)))
        dv.wait(ac)
        dv.inc(dv.raw.tensor_scalar_min(out=self.mega_col[:], in0=self.mega_col[:],
                                        scalar1=2.0))
        dv.inc(dv.raw.tensor_tensor_scan(
            out=self.tmp_hs[:, 0:32],
            data0=self.mega_col[:].broadcast_to([H, 32]),
            data1=self.sh_col[:].broadcast_to([H, 32]),
            initial=0.0, op0=ALU.mult, op1=ALU.add))
        dv.inc(dv.raw.memset(self.norm_hs[:, 0:1], EPS_W))
        dv.inc(dv.raw.tensor_scalar_add(out=self.norm_hs[:, 1:33],
                                        in0=self.tmp_hs[:, 0:32], scalar1=EPS_W))
        dv.inc(dv.raw.reciprocal(out=self.inv_hs[:], in_=self.norm_hs[:, 1:33]))
        dv.inc(dv.raw.tensor_tensor(out=self.d0_hs[:], in0=self.norm_hs[:, 0:32],
                                    in1=self.inv_hs[:], op=ALU.mult))
        dv.inc(dv.raw.tensor_scalar_mul(out=self.d0_hs[:], in0=self.d0_hs[:],
                                        scalar1=self.mega_col[:]))
        sy.wait(dv)
        self.dma("sy", self.scr_row[0:1, :].rearrange("o (h r) -> (o h) r", h=H),
                 self.d0_hs[:])
        self.dma("sy", self.scr_row[1:2, :].rearrange("o (h r) -> (o h) r", h=H),
                 self.inv_hs[:])
        sy.wait(self.dq)
        self.dma("sy", self.rowd0f[:], self.scr_row[0:1, :])
        n = self.dma("sy", self.rowinvf[:], self.scr_row[1:2, :])
        dv.wait(self.dq, n)
        dv.inc(dv.raw.tensor_copy(out=self.rowd0[:], in_=self.rowd0f[:]))
        dv.inc(dv.raw.tensor_copy(out=self.rowinv[:], in_=self.rowinvf[:]))
        pe.wait(dv)
        pe.inc(self.pe_mm(0, self.ps_a[:], self.ones_sb[:], self.rowd0[:],
                          start=True, stop=True))
        pe.inc(self.pe_mm(0, self.ps_b[:], self.ones_sb[:], self.rowinv[:],
                          start=True, stop=True))
        dv.wait(pe)
        dv.inc(dv.raw.tensor_copy(out=self.d0_bc[:], in_=self.ps_a[:]))
        dv.inc(dv.raw.tensor_copy(out=self.inv_bc[:], in_=self.ps_b[:]))
        dv.inc(dv.raw.memset(self.eide[:], 0.0))
        dv.inc(dv.raw.memset(self.eido[:], 0.0))
        pe.wait(dv)
        psf = self.ps_c
        pe.inc(self.tr(psf[0:64, 0:16], self.eid_hs[:], self.idf_sb[:], True, True))
        dv.wait(pe)
        dv.inc(dv.raw.tensor_copy(out=self.eide[0:64, :], in_=psf[0:64, 0:16]))
        scr16 = self.scr_row.rearrange("a (c d) -> (a c) d", c=32, d=16)[0:64, :]
        sy.wait(dv)
        n = self.dma("sy", scr16, self.eide[0:64, :], fence=True)
        sy.wait(self.dq, n)
        n = self.dma("sy", self.eido[64:128, :], scr16, fence=True)
        dv.wait(self.dq, n)
        dv.inc(dv.raw.memset(self.w_sb[:], 0.0))
        va = self.v_aug.rearrange("p (c g e) -> p c g e", c=NCH, g=8)
        va2 = self.v_aug2.rearrange("p (c g e) -> p c g e", c=NCH, g=8)
        dv.inc(dv.raw.memset(va[:, :, :, 64:65], 1.0))
        dv.inc(dv.raw.memset(va2[:, :, :, 64:65], 1.0))
        self.barrier()
        self.tap("d0_bc", self.d0_bc[:])
        self.tap("inv_bc", self.inv_bc[:])
        self.tap("eide", self.eide[:])
        self.tap("norms", self.norm_hs[:])

    def ln_batch(self, b):
        sy, dv, ac, pe = self.sy, self.dv, self.ac, self.pe
        xin = self.x_in.rearrange("p (i m) -> p i m", i=2)
        lnc = self.ln_cols
        gp = self.gp
        for lc in range(NCH):
            buf = lc % 2
            q = "sy" if buf == 0 else "gp"
            qeng = sy if buf == 0 else gp
            ctr = self.dq if buf == 0 else self.dqg
            if lc >= 2:
                qeng.wait(dv)
                qeng.wait(ac)
            n = self.dma(q, xin[:, buf, :], self.x[b, lc * 128 : lc * 128 + 128, :],
                         fence=True)
            dv.wait(ctr, n)
            ac.wait(ctr, n)
            ac.wait(pe)
            ac.inc(ac.raw.activation(out=self.ln_sq[:], in_=xin[:, buf, :],
                                     func=AF.Square, accum_out=lnc[:, 1:2]))
            dv.inc(dv.raw.reduce_sum(out=lnc[:, 0:1], in_=xin[:, buf, :], axis=AXX))
            dv.wait(ac)
            dv.inc(dv.raw.tensor_scalar_mul(out=lnc[:, 2:3], in0=lnc[:, 0:1],
                                            scalar1=1.0 / D))
            dv.inc(dv.raw.tensor_scalar_mul(out=lnc[:, 3:4], in0=lnc[:, 1:2],
                                            scalar1=1.0 / D))
            dv.inc(dv.raw.tensor_tensor(out=lnc[:, 4:5], in0=lnc[:, 2:3],
                                        in1=lnc[:, 2:3], op=ALU.mult))
            dv.inc(dv.raw.tensor_tensor(out=lnc[:, 5:6], in0=lnc[:, 3:4],
                                        in1=lnc[:, 4:5], op=ALU.subtract))
            dv.inc(dv.raw.tensor_scalar_add(out=lnc[:, 5:6], in0=lnc[:, 5:6],
                                            scalar1=EPS_LN))
            ac.wait(dv)
            ac.inc(ac.raw.activation(out=lnc[:, 6:7], in_=lnc[:, 5:6],
                                     func=AF.Sqrt))
            dv.wait(ac)
            dv.inc(dv.raw.reciprocal(out=lnc[:, 7:8], in_=lnc[:, 6:7]))
            xt = self.xt_ln
            dv.inc(dv.raw.tensor_scalar(out=xt, in0=xin[:, buf, :],
                                        scalar1=lnc[:, 2:3], scalar2=lnc[:, 7:8],
                                        op0=ALU.subtract, op1=ALU.mult))
            pe.wait(dv)
            for g2 in range(2):
                if lc >= 1 or g2 >= 1:
                    pe.wait(dv)
                last = None
                for dc4 in range(4):
                    dc = g2 * 4 + dc4
                    last = self.tr(
                        self.ps_t[:, g2 * 512 + dc4 * 128 : g2 * 512 + dc4 * 128 + 128],
                        xt[:, dc * 128 : dc * 128 + 128], self.id_sb[:],
                        dc4 == 0, dc4 == 3)
                pe.inc(last)
                dv.wait(pe)
                dst = self.xn.rearrange("p (c m) -> p c m", c=8)[
                    :, g2 * 4 : g2 * 4 + 4, lc * 128 : lc * 128 + 128]
                dv.inc(dv.raw.tensor_copy(
                    out=dst,
                    in_=self.ps_t[:, g2 * 512 : g2 * 512 + 512]
                    .rearrange("p (c m) -> p c m", c=4)))
        self.tap(f"xn_{b}", self.xn[:])

    def proj_half(self, b, half):
        sy, dv, ac, pe = self.sy, self.dv, self.ac, self.pe
        self.ac.wait(self.dq, self.wb_n)
        wload_n = []
        for j, scr in enumerate((self.scr_wq, self.scr_wk, self.scr_wv)):
            if j > 0:
                self.ac.wait(self.dqs, self.dqs.n)
            for ds in range(8):
                self.dmat("ac", self.wT[j][:, ds * 512 : ds * 512 + 512],
                          scr[half * 512 : half * 512 + 512, ds * 128 : ds * 128 + 128])
            wload_n.append(self.dqs.n)

        banks = [self.ps_strip[:, i * 512 : i * 512 + 512] for i in range(4)]
        for j, dst in ((0, self.q_fm), (1, self.k_fm)):
            pe.wait(self.dqs, wload_n[j])
            for oc in range(4):
                pe.wait(dv)
                for dc in range(8):
                    lastmm = None
                    for lt in range(4):
                        lastmm = self.pe_mm(
                            0, banks[lt],
                            self.wT[j][:, dc * 512 + oc * 128 : dc * 512 + oc * 128 + 128],
                            self.xn[:, dc * 2048 + lt * 512 : dc * 2048 + lt * 512 + 512],
                            start=(dc == 0), stop=(dc == 7))
                    if dc == 7:
                        pe.inc(lastmm)
                dv.wait(pe)
                for lt in range(4):
                    dv.inc(dv.raw.tensor_copy(
                        out=dst[:, oc * 2048 + lt * 512 : oc * 2048 + lt * 512 + 512],
                        in_=banks[lt]))
        va = self.v_aug.rearrange("p (c g e) -> p c g e", c=NCH, g=8)
        va2 = self.v_aug2.rearrange("p (c g e) -> p c g e", c=NCH, g=8)
        dv.inc(dv.raw.memset(va[:, :, :, 64:65], 1.0))
        dv.inc(dv.raw.memset(va2[:, :, :, 64:65], 1.0))
        pe.wait(self.dqs, wload_n[2])
        for lc in range(NCH):
            bank = self.ps_a if lc % 2 == 0 else self.ps_b
            pe.wait(dv)
            lastmm = None
            for dc in range(8):
                lastmm = self.pe_mm(
                    0, bank[:],
                    self.xn[:, dc * 2048 + lc * 128 : dc * 2048 + lc * 128 + 128],
                    self.wT[2][:, dc * 512 : dc * 512 + 512],
                    start=(dc == 0), stop=(dc == 7))
            pe.inc(lastmm)
            dv.wait(pe)
            src = bank.rearrange("p (g e) -> p g e", g=8)
            dv.inc(dv.raw.tensor_copy(out=va[:, lc, :, 0:64], in_=src))
            if lc >= 1:
                dv.inc(dv.raw.tensor_copy(out=va2[64:128, lc - 1, :, 0:64],
                                          in_=src[0:64, :, :]))
            if lc <= NCH - 2:
                dv.inc(dv.raw.tensor_copy(out=va2[0:64, lc, :, 0:64],
                                          in_=src[64:128, :, :]))
        self.tap(f"q_fm_{b}_{half}", self.q_fm[:])
        self.tap(f"k_fm_{b}_{half}", self.k_fm[:])
        self.tap(f"v_aug_{b}_{half}", self.v_aug[:])

    def attn_head(self, b, half, i):
        sy, dv, ac, pe, gp = self.sy, self.dv, self.ac, self.pe, self.gp
        h = half * 8 + i
        hp = i // 2
        rb = 64 * (i % 2)
        qh = lambda c0, n: self.q_fm[rb : rb + 64, hp * 2048 + c0 : hp * 2048 + c0 + n]
        kh = lambda c0, n: self.k_fm[rb : rb + 64, hp * 2048 + c0 : hp * 2048 + c0 + n]
        va = self.v_aug.rearrange("p (c g e) -> p c g e", c=NCH, g=8)
        va2 = self.v_aug2.rearrange("p (c g e) -> p c g e", c=NCH, g=8)
        idsl = self.id_sb[rb : rb + 64, rb : rb + 64]

        dv.inc(dv.raw.memset(self.acc1[:], 0.0))
        dv.inc(dv.raw.memset(self.accs[:], 0.0))
        dv.inc(dv.raw.memset(
            self.w_sb.rearrange("p (c r) -> p c r", r=33)[:, :, 0:1], 0.0))
        for g2 in range(2):
            pe.wait(dv)
            last = None
            for m4 in range(8):
                mc = g2 * 8 + m4
                last = self.tr(
                    self.ps_t[:, g2 * 512 + m4 * 64 : g2 * 512 + m4 * 64 + 64],
                    kh(mc * 128, 128), idsl, m4 == 0, m4 == 7, kbase=rb)
            pe.inc(last)
            dv.wait(pe)
            dv.inc(dv.raw.tensor_copy(out=self.ktok[:, g2 * 512 : g2 * 512 + 512],
                                      in_=self.ps_t[:, g2 * 512 : g2 * 512 + 512]))

        if SUB_LIMIT < 2:
            return
        pe.inc(self.pe_mm(rb, self.ps_c[0:64, 448:512], kh(0, 64), qh(0, 64),
                          start=True, stop=True))
        ac.wait(pe)
        ac.inc(ac.raw.activation(out=self.Et[0:64, 0:64],
                                 in_=self.ps_c[0:64, 448:512],
                                 func=AF.Exp, scale=SCALE_S))
        dv.inc(dv.raw.memset(self.Et[64:128, 0:64], 0.0))
        for rnd in range(4):
            bank = self.ps_strip[:, (rnd % 2) * 512 : (rnd % 2) * 512 + 512]
            pe.wait(dv)
            if rnd >= 2:
                pe.wait(ac)
            last = None
            first_s8 = 1 if rnd == 0 else 0
            for s8 in range(first_s8, 8):
                r = rnd * 8 + s8
                last = self.pe_mm(rb, bank[:, s8 * 64 : s8 * 64 + 64],
                                  kh(64 * (r - 1), 128), qh(64 * r, 64),
                                  start=(s8 == first_s8), stop=(s8 == 7))
            pe.inc(last)
            ac.wait(pe)
            ac.inc(ac.raw.activation(
                out=self.Et[:, rnd * 512 + first_s8 * 64 : rnd * 512 + 512],
                in_=bank[:, first_s8 * 64 : 512], func=AF.Exp, scale=SCALE_S))
        dv.wait(ac)
        dv.inc(dv.raw.tensor_tensor(out=self.Et[0:64, 0:64], in0=self.Et[0:64, 0:64],
                                    in1=self.terr0_sb[0:64, :], op=ALU.mult))
        dv.inc(dv.raw.tensor_tensor(
            out=self.Et[:, 64:2048].rearrange("p (s e) -> p s e", e=SEG),
            in0=self.Et[:, 64:2048].rearrange("p (s e) -> p s e", e=SEG),
            in1=self.terr_sb[:].unsqueeze(1).broadcast_to([128, 31, SEG]),
            op=ALU.mult))
        pe.wait(dv)

        if SUB_LIMIT < 3:
            return
        pcs_started = [False, False, False, False]
        for c in range(NCH):
            s4 = c % 4
            w = 128 * c + 128
            if c % 4 == 0 and c >= 4:
                pe.wait(dv)
            slot = self.ps_c[:, s4 * 65 : s4 * 65 + 65]
            r0, r1 = 2 * c, 2 * c + 1
            if r0 == 0:
                self.pe_mm(0, slot[0:64, :], self.Et[0:64, 0:64],
                           va[0:64, 0, i, :], start=True, stop=True)
            else:
                self.pe_mm(0, slot[0:64, :], self.Et[:, r0 * 64 : r0 * 64 + 64],
                           va2[:, c - 1, i, :], start=True, stop=True)
            mm1 = self.pe_mm(0, slot[64:128, :], self.Et[:, r1 * 64 : r1 * 64 + 64],
                             va[:, c, i, :], start=True, stop=True,
                             tile_position=(0, 64))
            pe.inc(mm1)
            pe.wait(ac)
            if c == 0:
                pe.wait(dv)
            last = None
            for t in range((w + 511) // 512):
                n = min(512, w - 512 * t)
                last = self.pe_mm(rb, self.ps_strip[:, 512 * t : 512 * t + n],
                                  qh(c * 128, 128), kh(512 * t, n),
                                  start=True, stop=True)
            pe.inc(last)

            ac.wait(pe)
            e0 = s4 * 2048
            if c >= 1:
                ac.inc(ac.raw.activation(
                    out=self.E4[:, e0 : e0 + w - 192],
                    in_=self.ps_strip[:, 0 : w - 192],
                    func=AF.Exp, scale=SCALE_S, accum_out=self.acc1[:, c : c + 1]))
                ac.inc(ac.raw.activation(
                    out=self.E4[:, e0 + w - 192 : e0 + w - 128],
                    in_=self.ps_strip[:, w - 192 : w - 128],
                    func=AF.Exp, scale=SCALE_S, accum_out=self.accs[:, c : c + 1]))
            ac.inc(ac.raw.activation(
                out=self.E4[:, e0 + w - 128 : e0 + w],
                in_=self.ps_strip[:, w - 128 : w], func=AF.Exp, scale=SCALE_S))
            dv.wait(ac)
            dv.inc(dv.raw.tensor_tensor(
                out=self.E4[:, e0 + w - 128 : e0 + w],
                in0=self.E4[:, e0 + w - 128 : e0 + w],
                in1=self.tri_sb[:], op=ALU.mult))

            if c % 4 == 3:
                g = c // 4
                cs = slice(4 * g, 4 * g + 4)
                dv.wait(pe)
                dv.inc(dv.raw.scalar_tensor_tensor(
                    out=self.rs4[:, cs], in0=self.accs[:, cs],
                    scalar=self.half_sb[:], in1=self.acc1[:, cs],
                    op0=ALU.mult, op1=ALU.add))
                otr = self.ps_c[:, 0:260].rearrange("p (c e) -> p c e", e=65)[:, :, 64:65]
                dv.inc(dv.raw.tensor_tensor(
                    out=self.rs4[:, cs].unsqueeze(2),
                    in0=self.rs4[:, cs].unsqueeze(2), in1=otr, op=ALU.add))
                dv.inc(dv.raw.reciprocal(out=self.rcp[:, cs], in_=self.rs4[:, cs]))
                pcs_banks = [self.ps_a, self.ps_b, self.ps_t.bitcast(F32),
                             self.ps_strip[:, 0:512]]
                Cv = self.C_sb.rearrange("p (r two) -> p r two", two=2)
                lastmm = None
                for cc in range(4 * g + 3, 4 * g - 1, -1):
                    if cc != 4 * g + 3:
                        dv.wait(pe)
                    dv.inc(dv.raw.memset(self.C_sb[:], 0.0))
                    dv.inc(dv.raw.scalar_tensor_tensor(
                        out=Cv[:, cc : cc + 1, 0:1],
                        in0=self.rcp[:, cc : cc + 1].unsqueeze(2),
                        scalar=self.eide[:, h : h + 1],
                        in1=self.inv_bc[:, h * 32 + 2 * cc : h * 32 + 2 * cc + 1]
                            .unsqueeze(2),
                        op0=ALU.mult, op1=ALU.mult))
                    dv.inc(dv.raw.scalar_tensor_tensor(
                        out=Cv[:, cc : cc + 1, 1:2],
                        in0=self.rcp[:, cc : cc + 1].unsqueeze(2),
                        scalar=self.eido[:, h : h + 1],
                        in1=self.inv_bc[:, h * 32 + 2 * cc + 1 : h * 32 + 2 * cc + 2]
                            .unsqueeze(2),
                        op0=ALU.mult, op1=ALU.mult))
                    pe.wait(dv)
                    wcc = 128 * cc + 128
                    ntt = (wcc + 511) // 512
                    for t in range(ntt):
                        n = min(512, wcc - 512 * t)
                        bank = pcs_banks[t]
                        st = not pcs_started[t]
                        pcs_started[t] = True
                        stop = (cc == 4 * g) and (g == 3)
                        mm = self.pe_mm(
                            0, bank[64 * (t % 2) : 64 * (t % 2) + 32, 0:n],
                            self.C_sb[:],
                            self.E4[:, (cc % 4) * 2048 + 512 * t :
                                    (cc % 4) * 2048 + 512 * t + n],
                            start=st, stop=stop, tile_position=(0, 64 * (t % 2)))
                        lastmm = mm
                    pe.inc(lastmm)
                dv.wait(pe)
                dv.inc(dv.raw.tensor_copy(
                    out=self.ot_sb[:, 4 * g * 65 : 4 * g * 65 + 260],
                    in_=self.ps_c[:, 0:260]))
        if SUB_LIMIT < 4:
            return
        self.post_head(b, half, i, h, hp, rb, qh, va)

    def post_head(self, b, half, i, h, hp, rb, qh, va):
        sy, dv, ac, pe, gp = self.sy, self.dv, self.ac, self.pe, self.gp
        dv.wait(pe)
        pcs_banks = [self.ps_a, self.ps_b, self.ps_t.bitcast(F32),
                     self.ps_strip[:, 0:512]]
        for t in range(4):
            rbt = 64 * (t % 2)
            dv.inc(dv.raw.tensor_copy(
                out=self.pcs_sb[rbt : rbt + 32, (t // 2) * 512 : (t // 2) * 512 + 512],
                in_=pcs_banks[t][rbt : rbt + 32, 0:512]))
        self.tap(f"pcsA_{b}_{h}", self.pcs_sb[0:32, :])
        self.tap(f"pcsB_{b}_{h}", self.pcs_sb[64:96, :])
        if SUB_LIMIT == 40:
            return
        pe.wait(dv)
        last = None
        pst_dst = self.ps_t.bitcast(F32)
        last = None
        for t in (0, 2, 1, 3):
          for mcl in range(4):
            mc = t * 4 + mcl
            base = 64 * (t % 2)
            blk = self.pcs_sb[base : base + 32,
                              (t // 2) * 512 + 128 * mcl : (t // 2) * 512 + 128 * mcl + 128]
            last = self.pe_mm(base, pst_dst[:, mc * 32 : mc * 32 + 32], blk,
                              self.id_sb[base : base + 32, base : base + 32],
                              start=True, stop=True)
        pe.inc(last)
        if SUB_LIMIT == 41:
            return
        dv.wait(pe)
        for c in range(16):
            dv.inc(dv.raw.tensor_tensor_scan(
                out=self.w_sb.rearrange("p (c r) -> p c r", r=33)[:, c, 1:33],
                data0=self.d0_bc[:, h * 32 : h * 32 + 32],
                data1=pst_dst[:, c * 32 : c * 32 + 32],
                initial=0.0, op0=ALU.mult, op1=ALU.add))
        self.tap(f"w_{b}_{h}", self.w_sb[:])
        if SUB_LIMIT < 5:
            return
        pe.wait(ac)
        wv = self.w_sb.rearrange("p (c r) -> p c r", r=33)
        bufs = (self.vw, self.vw2)
        kv_mm_n = []
        kk = 0
        for j in range(4):
            cmax = min(NCH - 1, 4 * j + 3)
            for C in range(cmax + 1):
                buf = bufs[kk % 2]
                if kk >= 2:
                    dv.wait(pe, kv_mm_n[kk - 2])
                dv.inc(dv.raw.tensor_tensor(
                    out=buf.rearrange("p (r e) -> p r e", e=64),
                    in0=wv[:, C, 8 * j : 8 * j + 8].unsqueeze(2)
                        .broadcast_to([128, 8, 64]),
                    in1=va[:, C, i, 0:64].unsqueeze(1).broadcast_to([128, 8, 64]),
                    op=ALU.mult))
                pe.wait(dv)
                mm = self.pe_mm(
                    0, self.ps_strip[rb : rb + 64, j * 512 : j * 512 + 512],
                    self.ktok[:, C * 64 : C * 64 + 64], buf[:],
                    start=(C == 0), stop=(C == cmax), tile_position=(0, rb))
                pe.inc(mm)
                kv_mm_n.append(pe.n)
                kk += 1
        dv.wait(pe)
        for j in range(4):
            dv.inc(dv.raw.tensor_copy(
                out=self.kv_sb[rb : rb + 64, j * 512 : j * 512 + 512],
                in_=self.ps_strip[rb : rb + 64, j * 512 : j * 512 + 512]))
        self.tap(f"kv_{b}_{h}", self.kv_sb[rb : rb + 64, :])
        if SUB_LIMIT < 6:
            return
        pe.wait(dv)
        for r in range(R):
            cch = r // 2
            bank = self.ps_a if cch < 8 else self.ps_b
            mm = self.pe_mm(
                rb, bank[64 * (r % 2) : 64 * (r % 2) + 64,
                         (cch % 8) * 64 : (cch % 8) * 64 + 64],
                qh(r * 64, 64), self.kv_sb[rb : rb + 64, r * 64 : r * 64 + 64],
                start=True, stop=True,
                tile_position=(rb, 64 * (r % 2)))
            if r >= R - 2:
                pe.inc(mm)
        dv.wait(pe)
        otv = self.ot_sb.rearrange("p (c e) -> p c e", e=65)
        dv.inc(dv.raw.reciprocal(out=self.rst[:], in_=otv[:, :, 64]))
        dv.inc(dv.raw.tensor_tensor(out=self.Ffac[:], in0=self.rst[:],
                                    in1=self.tw_col[:].broadcast_to([128, NCH]),
                                    op=ALU.mult))
        for grp in range(2):
            bank = self.ps_a if grp == 0 else self.ps_b
            dv.inc(dv.raw.tensor_tensor(
                out=self.t1_sb.rearrange("p (c e) -> p c e", e=64),
                in0=otv[:, 8 * grp : 8 * grp + 8, 0:64],
                in1=self.Ffac[:, 8 * grp : 8 * grp + 8].unsqueeze(2)
                    .broadcast_to([128, 8, 64]),
                op=ALU.mult))
            dv.inc(dv.raw.scalar_tensor_tensor(
                out=self.y_tok[:, grp * 512 : grp * 512 + 512],
                in0=bank[:], scalar=self.f2_col[:], in1=self.t1_sb[:],
                op0=ALU.mult, op1=ALU.add))
        self.tap(f"ytok_{b}_{h}", self.y_tok[:])
        if SUB_LIMIT < 7:
            return
        jc = h // 2
        pstf = self.ps_t.bitcast(F32)
        for rnd in range(4):
            pe.wait(dv)
            last = None
            for s in range(4):
                slot = rnd * 4 + s
                last = self.pe_mm(
                    0, pstf[rb : rb + 64, s * 128 : s * 128 + 128],
                    self.y_tok[:, slot * 64 : slot * 64 + 64],
                    self.id_sb[:], start=(s == 0), stop=(s == 3),
                    tile_position=(0, rb))
            pe.inc(last)
            dv.wait(pe)
            dv.inc(dv.raw.tensor_copy(
                out=self.y_fm[rb : rb + 64,
                              jc * 2048 + rnd * 512 : jc * 2048 + rnd * 512 + 512],
                in_=pstf[rb : rb + 64, 0:512]))

    def ffn_batch(self, b):
        sy, dv, ac, pe = self.sy, self.dv, self.ac, self.pe
        dv.inc(dv.raw.tensor_scalar_mul(out=self.xn[:, 0:8192],
                                        in0=self.xn[:, 0:8192], scalar1=INV127))
        dv.inc(dv.raw.tensor_scalar_mul(out=self.xn[:, 8192:16384],
                                        in0=self.xn[:, 8192:16384], scalar1=INV127))
        pe.wait(dv)
        banks8 = [self.ps_strip[:, i * 512 : i * 512 + 512] for i in range(4)] + \
                 [self.ps_a, self.ps_b, self.ps_c, self.ps_t.bitcast(F32)]
        for lt in range(2):
            for qtr in range(4):
                ac.wait(pe)
                for ds in range(8):
                    self.dmat("ac", self.wgT_buf[:, ds * 768 : ds * 768 + 768],
                              self.scr_wg[qtr * 768 : qtr * 768 + 768,
                                          ds * 128 : ds * 128 + 128])
                    self.dmat("ac", self.wuT_buf[:, ds * 768 : ds * 768 + 768],
                              self.scr_wu[qtr * 768 : qtr * 768 + 768,
                                          ds * 128 : ds * 128 + 128])
                pe.wait(self.dqs)
                for hc6 in range(6):
                    hc = qtr * 6 + hc6
                    pe.wait(ac)
                    pe.wait(dv)
                    lastmm = None
                    for dc in range(8):
                        for s in range(2):
                            lastmm = self.pe_mm(
                                0, self.ps_strip[:, s * 512 : s * 512 + 512],
                                self.wgT_buf[:, dc * 768 + hc6 * 128 :
                                             dc * 768 + hc6 * 128 + 128],
                                self.xn[:, dc * 2048 + lt * 1024 + s * 512 :
                                        dc * 2048 + lt * 1024 + s * 512 + 512],
                                start=(dc == 0), stop=(dc == 7))
                            lastmm = self.pe_mm(
                                0, self.ps_strip[:, 1024 + s * 512 : 1024 + s * 512 + 512],
                                self.wuT_buf[:, dc * 768 + hc6 * 128 :
                                             dc * 768 + hc6 * 128 + 128],
                                self.xn[:, dc * 2048 + lt * 1024 + s * 512 :
                                        dc * 2048 + lt * 1024 + s * 512 + 512],
                                start=(dc == 0), stop=(dc == 7))
                    pe.inc(lastmm)
                    ac.wait(pe)
                    ac.inc(ac.raw.activation(out=self.sgu[:, 0:1024],
                                             in_=self.ps_strip[:, 0:1024],
                                             func=AF.Silu))
                    dv.wait(pe)
                    dv.inc(dv.raw.tensor_copy(out=self.sgu[:, 1024:2048],
                                              in_=self.ps_strip[:, 1024:2048]))
                    dv.wait(ac)
                    dv.inc(dv.raw.tensor_tensor(
                        out=self.h_sb[:, hc * 1024 : hc * 1024 + 1024],
                        in0=self.sgu[:, 0:1024], in1=self.sgu[:, 1024:2048],
                        op=ALU.mult))
            for ot in range(2):
                dn_mm_n = []
                for hc in range(FF // 128):
                    bufsel = (hc % 2) * 1024
                    if hc >= 2:
                        ac.wait(pe, dn_mm_n[hc - 2])
                    if hc >= 1:
                        ac.wait(self.dqs, self.dqs.n)
                    n = self.dmat("ac", self.wdT_buf[:, bufsel : bufsel + 512],
                                  self.scr_wd[ot * 512 : ot * 512 + 512,
                                              hc * 128 : hc * 128 + 128])
                    pe.wait(self.dqs, n)
                    if hc == 0:
                        pe.wait(ac)
                        pe.wait(dv)
                    lastmm = None
                    for lc in range(8):
                        lastmm = self.pe_mm(
                            0, banks8[lc][:],
                            self.h_sb[:, hc * 1024 + lc * 128 : hc * 1024 + lc * 128 + 128],
                            self.wdT_buf[:, bufsel : bufsel + 512],
                            start=(hc == 0), stop=(hc == FF // 128 - 1))
                    pe.inc(lastmm)
                    dn_mm_n.append(pe.n)
                ac.wait(pe)
                lastact = None
                for lc in range(8):
                    lastact = ac.raw.activation(
                        out=self.g_tok[:, (lt * 8 + lc) * 1024 + ot * 512 :
                                       (lt * 8 + lc) * 1024 + ot * 512 + 512],
                        in_=banks8[lc][:], func=AF.Silu, scale=INV127)
                    ac.inc(lastact)
        self.tap(f"g_{b}", self.g_tok[:])

    def wc_combine(self, b):
        sy, dv, ac, pe = self.sy, self.dv, self.ac, self.pe
        for ot in range(2):
            for ds in range(8):
                self.dmat("ac", self.wcT_buf[:, ot * 4096 + ds * 512 :
                                             ot * 4096 + ds * 512 + 512],
                          self.scr_wc[ot * 512 : ot * 512 + 512,
                                      ds * 128 : ds * 128 + 128])
        pe.wait(self.dqs)
        xin = self.x_in2.rearrange("p (i m) -> p i m", i=2)
        xout = self.x_out.rearrange("p (i m) -> p i m", i=2)
        gp = self.gp
        out_dma_n = []
        for lc in range(NCH):
            buf = lc % 2
            q = "sy" if buf == 0 else "gp"
            qeng = sy if buf == 0 else gp
            ctr = self.dq if buf == 0 else self.dqg
            if lc >= 2:
                qeng.wait(dv)
                dv.wait(self.dq, out_dma_n[lc - 2])
            n = self.dma(q, xin[:, buf, :], self.x[b, lc * 128 : lc * 128 + 128, :],
                         fence=True)
            for ot in range(2):
                bank = self.ps_a if ot == 0 else self.ps_b
                pe.wait(dv)
                lastmm = None
                for jcc in range(8):
                    lastmm = self.pe_mm(
                        0, bank[:],
                        self.y_fm[:, jcc * 2048 + lc * 128 : jcc * 2048 + lc * 128 + 128],
                        self.wcT_buf[:, ot * 4096 + jcc * 512 : ot * 4096 + jcc * 512 + 512],
                        start=(jcc == 0), stop=(jcc == 7))
                pe.inc(lastmm)
                dv.wait(pe)
                dv.inc(dv.raw.tensor_tensor(
                    out=self.t1_sb[:], in0=bank[:],
                    in1=self.g_tok[:, lc * 1024 + ot * 512 : lc * 1024 + ot * 512 + 512],
                    op=ALU.mult))
                dv.wait(ctr, n)
                dv.inc(dv.raw.tensor_tensor(
                    out=xout[:, buf, ot * 512 : ot * 512 + 512],
                    in0=self.t1_sb[:],
                    in1=xin[:, buf, ot * 512 : ot * 512 + 512], op=ALU.add))
            sy.wait(dv)
            out_dma_n.append(
                self.dma("sy", self.out[b, lc * 128 : lc * 128 + 128, :],
                         xout[:, buf, :], fence=True))


def _consts():
    i = np.arange(128)
    j = np.arange(128)
    js = np.arange(SEG)
    c_tri = (j[None, :] <= i[:, None]).astype(bf)
    c_terr = np.zeros((128, SEG), dtype=np.float32)
    c_terr[0:64, :] = 1.0
    c_terr[64:128, :] = (i[64:128, None] - 64 <= js[None, :]).astype(np.float32)
    c_terr0 = np.zeros((128, SEG), dtype=np.float32)
    c_terr0[0:64, :] = (i[0:64, None] <= js[None, :]).astype(np.float32)
    c_half = np.zeros((128, 1), dtype=np.float32)
    c_half[64:128] = 1.0
    return dict(c_id=np.eye(128, dtype=bf), c_idf=np.eye(16, dtype=np.float32),
                c_tri=c_tri, c_terr=c_terr.astype(bf), c_terr0=c_terr0.astype(bf),
                c_ones=np.ones((1, 128), dtype=bf), c_half=c_half)


def kernel(L0_attn_hidden_states, L0_attn_Wq, L0_attn_Wk, L0_attn_Wv, L0_attn_Wc,
           L0_attn_terrace_mix, L0_attn_index_decays, L0_attn_mega_decays,
           L0_ffn_w_gate, L0_ffn_w_up, L0_ffn_w_down):
    x = np.ascontiguousarray(L0_attn_hidden_states)
    k = K(taps=DEBUG_TAPS)
    nc = k.build()
    base = dict(Wq=L0_attn_Wq, Wk=L0_attn_Wk, Wv=L0_attn_Wv, Wc=L0_attn_Wc,
                wg=L0_ffn_w_gate, wu=L0_ffn_w_up, wd=L0_ffn_w_down,
                tm=L0_attn_terrace_mix, idec=L0_attn_index_decays,
                mdec=L0_attn_mega_decays, **_consts())
    in_maps = [dict(base, x=x[2 * c : 2 * c + 2]) for c in range(8)]
    res = run_bass_kernel_spmd(nc, in_maps, core_ids=list(range(8)))
    if DEBUG_TAPS:
        kernel.tap_results = [
            {n: rr.get("tap_" + n) for n in DEBUG_TAPS} for rr in
            [r for r in res.results]]
    return np.concatenate([r["out"] for r in res.results], axis=0)
